# version 01
# baseline (speedup 1.0000x reference)
import numpy as np
import ml_dtypes
from contextlib import ExitStack

import concourse.bass as bass
import concourse.mybir as mybir

F32 = mybir.dt.float32
BF16 = mybir.dt.bfloat16
U32 = mybir.dt.uint32
ALU = mybir.AluOpType
AF = mybir.ActivationFunctionType
AX = mybir.AxisListType
IOoA = bass.IndirectOffsetOnAxis

B, L, D = 16, 2048, 1024
NCORES = 8
SEQ_PER_CORE = 2
TOK_PER_CORE = SEQ_PER_CORE * L
NTILE = TOK_PER_CORE // 128
NEXP = 8
CAP = 2048
NH = 4
DK = 256
CHUNK = 128
NCHUNK = L // CHUNK
GROUP = 4
NGROUP = NCHUNK // GROUP
EPS_LN = 1e-5
EPS_L2 = 1e-6
EPS_RMS = 1e-5
IDBUF_N = NEXP * CAP + 128


def host_constants(core):
    c = {}
    r = np.arange(128)
    c["c_ident"] = np.eye(128, dtype=np.float32)
    c["c_identb"] = np.eye(128, dtype=ml_dtypes.bfloat16)
    c["c_iota8"] = np.tile(np.arange(8, dtype=np.float32), (128, 1)).copy()
    c["c_u1strict"] = np.triu(np.ones((128, 128)), 1).astype(ml_dtypes.bfloat16)
    c["c_onesc"] = np.ones((128, 1), dtype=ml_dtypes.bfloat16)
    c["c_onesr"] = np.ones((1, 512), dtype=ml_dtypes.bfloat16)
    c["c_mask_sl"] = (r[:, None] > r[None, :]).astype(np.float32)
    c["c_mask_ui"] = (r[:, None] <= r[None, :]).astype(np.float32)
    blk = r // 32
    c["c_mask_bdsl"] = ((blk[:, None] == blk[None, :]) & (r[:, None] > r[None, :])).astype(np.float32)
    c["c_mask_off"] = (blk[:, None] > blk[None, :]).astype(np.float32)
    c["c_corelt"] = (np.arange(8) < core).astype(np.float32).reshape(8, 1)
    c["c_tokid1"] = (core * TOK_PER_CORE + np.arange(TOK_PER_CORE, dtype=np.float32) + 1.0
                     ).reshape(NTILE, 128).T.copy()
    c["c_slotidx"] = (core * CAP + np.arange(CAP, dtype=np.uint32)).reshape(16, 128).T.copy()
    return c


def host_inputs_for_core(core, inp):
    x = np.ascontiguousarray(np.asarray(inp["L0_attn_hidden_states"], dtype=np.float32).reshape(B * L, D))
    d = {}
    d["x_own"] = np.ascontiguousarray(x[core * TOK_PER_CORE:(core + 1) * TOK_PER_CORE])
    d["x_full"] = x
    d["WqT"] = np.ascontiguousarray(np.asarray(inp["L0_attn_Wq"]).T)
    d["WkT"] = np.ascontiguousarray(np.asarray(inp["L0_attn_Wk"]).T)
    d["WvT"] = np.ascontiguousarray(np.asarray(inp["L0_attn_Wv"]).T)
    d["WoT"] = np.ascontiguousarray(np.asarray(inp["L0_attn_Wo"]).T)
    d["WsmT"] = np.ascontiguousarray(np.concatenate(
        [np.asarray(inp["L0_attn_Wb"]).T, np.asarray(inp["L0_attn_Wdec"]).T,
         np.asarray(inp["L0_attn_Wmix"]).T], axis=1))
    d["mixb"] = np.ascontiguousarray(np.concatenate(
        [np.zeros(8, np.float32), np.asarray(inp["L0_attn_mix_bias"], np.float32)]).reshape(12, 1))
    d["mixb_row"] = np.ascontiguousarray(
        np.concatenate([np.zeros(8, np.float32), np.asarray(inp["L0_attn_mix_bias"], np.float32)]).reshape(1, 12))
    for nm, key in (("wq_c", "L0_attn_wq_conv"), ("wk_c", "L0_attn_wk_conv"), ("wv_c", "L0_attn_wv_conv")):
        w = np.asarray(inp[key])
        d[nm] = np.ascontiguousarray(w.reshape(8, 128, 4).transpose(1, 0, 2).reshape(128, 32))
    d["onorm2"] = np.ascontiguousarray(np.asarray(inp["L0_attn_onorm_w"]).reshape(2, 128).T)
    d["gwT"] = np.ascontiguousarray(np.asarray(inp["L0_ffn_gate_weight"]).T)
    d["ln_g"] = np.ascontiguousarray(np.asarray(inp["L0_ffn_ln_gamma"]).reshape(8, 128).T)
    d["ln_b"] = np.ascontiguousarray(np.asarray(inp["L0_ffn_ln_beta"]).reshape(8, 128).T)
    d["wi_e"] = np.ascontiguousarray(np.asarray(inp["L0_ffn_wi"])[core])
    d["wo_e"] = np.ascontiguousarray(np.asarray(inp["L0_ffn_wo"])[core])
    d.update(host_constants(core))
    return d


class Ctr:
    def __init__(self, sem, name):
        self.sem = sem
        self.n = 0
        self.name = name

    def evt(self):
        return (self, self.n)


def _wait(eng, wstate, event):
    ctr, val = event
    if val <= 0:
        return
    if wstate.get(ctr.name, -1) >= val:
        return
    wstate[ctr.name] = val
    eng.wait_ge(ctr.sem, val)



def _ap_name(a):
    try:
        return a.tensor.name
    except Exception:
        return None


class DrainGuard:

    _READS = {
        "tensor_copy": ("in_",), "tensor_tensor": ("in0", "in1"),
        "tensor_scalar": ("in0", "scalar1", "scalar2"),
        "scalar_tensor_tensor": ("in0", "scalar", "in1"),
        "tensor_reduce": ("in_",), "tensor_tensor_scan": ("data0", "data1", "initial"),
        "reciprocal": ("in_",), "bn_stats": ("in_",), "bn_aggr": ("in_",),
        "max": ("in_",), "max_index": ("in_max", "in_values"),
        "activation": ("in_", "bias", "scale"), "copy": ("in_",),
        "add": ("in_",), "mul": ("in_",), "iota": (), "memset": (),
        "affine_select": ("in_",), "partition_broadcast": ("in_ap",),
        "partition_all_reduce": ("in_ap",),
    }
    _WRITES = {
        "tensor_copy": ("out",), "tensor_tensor": ("out",), "tensor_scalar": ("out", "accum_out"),
        "scalar_tensor_tensor": ("out", "accum_out"), "tensor_reduce": ("out",),
        "tensor_tensor_scan": ("out",), "reciprocal": ("out",), "bn_stats": ("out",),
        "bn_aggr": ("out",), "max": ("out",), "max_index": ("out",),
        "activation": ("out", "accum_out"), "copy": ("out",), "add": ("out",),
        "mul": ("out",), "iota": ("out",), "memset": ("ap",), "affine_select": ("out",),
        "partition_broadcast": ("out_ap",), "partition_all_reduce": ("out_ap",),
    }

    def __init__(self, eng):
        self._eng = eng
        self._dirty_w = set()
        self._dirty_r = set()

    def drain(self):
        self._eng.drain()
        self._dirty_w.clear()
        self._dirty_r.clear()

    def __getattr__(self, name):
        eng = object.__getattribute__(self, "_eng")
        attr = getattr(eng, name)
        if name not in self._READS:
            return attr
        rkeys = self._READS[name]
        wkeys = self._WRITES[name]

        def wrapper(*args, **kw):
            import inspect as _i
            if args:
                try:
                    ba = _i.signature(attr).bind(*args, **kw)
                    allkw = dict(ba.arguments)
                except (TypeError, ValueError):
                    allkw = dict(kw)
            else:
                allkw = dict(kw)
            rn = {n for k in rkeys for n in [_ap_name(allkw.get(k))] if n}
            wn = {n for k in wkeys for n in [_ap_name(allkw.get(k))] if n}
            if (rn & self._dirty_w) or (wn & self._dirty_w) or (wn & self._dirty_r):
                self.drain()
            inst = attr(*args, **kw)
            self._dirty_w |= wn
            self._dirty_r |= rn
            return inst
        return wrapper


class MK:
    ENG = {"t": "tensor", "v": "vector", "s": "scalar", "g": "gpsimd", "sy": "sync"}

    def __init__(self, taps=(), stage=99, tsnap=2, sub=15):
        self.taps = set(taps)
        self.sub = sub
        self.stage = stage
        self.tsnap = tsnap
        self.nc = bass.Bass("TRN2", target_bir_lowering=False)
        self.es = ExitStack()
        self._ctrs = {}

    def din(self, name, shape, dt=F32):
        return self.nc.dram_tensor(name, list(shape), dt, kind="ExternalInput").ap()

    def dout(self, name, shape, dt=F32):
        return self.nc.dram_tensor(name, list(shape), dt, kind="ExternalOutput").ap()

    def sb(self, name, shape, dt=F32):
        return self.es.enter_context(self.nc.sbuf_tensor(name, list(shape), dt))

    def psum(self, name, shape, dt=F32):
        return self.es.enter_context(self.nc.psum_tensor(name, list(shape), dt))

    def ctr(self, name):
        if name not in self._ctrs:
            sem = self.es.enter_context(self.nc.semaphore("sm_" + name))
            self._ctrs[name] = Ctr(sem, name)
        return self._ctrs[name]

    def on(self, which, fn):
        wstate = self.wd[which]

        def body(eng):
            if which in ("v", "s", "g"):
                eng = DrainGuard(eng)
                eng.drain()
            fn(eng, lambda ev: _wait(eng, wstate, ev))
        getattr(self.blk, self.ENG[which])(body)

    def dma(self, eng, ctr, out, in_, **kw):
        inst = eng.dma_start(out=out, in_=in_, **kw)
        inst.then_inc(ctr.sem, 16)
        ctr.n += 16
        return ctr.evt()

    def idma(self, eng, ctr, **kw):
        inst = eng.indirect_dma_start(**kw)
        inst.then_inc(ctr.sem, 16)
        ctr.n += 16
        return ctr.evt()

    def step(self, ctr, inst, k=1):
        inst.then_inc(ctr.sem, k)
        ctr.n += k
        return ctr.evt()

    def build(self):
        s = self
        s.x_own = s.din("x_own", [TOK_PER_CORE, D])
        s.x_full = s.din("x_full", [B * L, D])
        s.i_WqT = s.din("WqT", [D, D]); s.i_WkT = s.din("WkT", [D, D])
        s.i_WvT = s.din("WvT", [D, D]); s.i_WoT = s.din("WoT", [D, D])
        s.i_WsmT = s.din("WsmT", [D, 12])
        s.i_gwT = s.din("gwT", [D, 8])
        s.i_wi = s.din("wi_e", [D, 4096]); s.i_wo = s.din("wo_e", [4096, D])
        s.i_mixb = s.din("mixb", [12, 1])
        s.i_mixbr = s.din("mixb_row", [1, 12])
        s.i_convw = {k: s.din(k, [128, 32]) for k in ("wq_c", "wk_c", "wv_c")}
        s.i_onorm2 = s.din("onorm2", [128, 2])
        s.i_lng = s.din("ln_g", [128, 8]); s.i_lnb = s.din("ln_b", [128, 8])
        s.i_ident = s.din("c_ident", [128, 128])
        s.i_iota8 = s.din("c_iota8", [128, 8])
        s.i_msl = s.din("c_mask_sl", [128, 128]); s.i_mui = s.din("c_mask_ui", [128, 128])
        s.i_mbd = s.din("c_mask_bdsl", [128, 128]); s.i_moff = s.din("c_mask_off", [128, 128])
        s.i_corelt = s.din("c_corelt", [8, 1])
        s.i_tokid1 = s.din("c_tokid1", [128, NTILE])
        s.i_identb = s.din("c_identb", [128, 128], BF16)
        s.i_u1 = s.din("c_u1strict", [128, 128], BF16)
        s.i_onesc = s.din("c_onesc", [128, 1], BF16)
        s.i_onesr = s.din("c_onesr", [1, 512], BF16)
        s.i_slotidx = s.din("c_slotidx", [128, 16], U32)
        s.o_out = s.dout("out", [TOK_PER_CORE, D])

        s.cc_cnt_in = s.nc.dram_tensor("cc_cnt_in", [1, 8], F32)
        s.cc_cnt_out = s.nc.dram_tensor("cc_cnt_out", [8, 8], F32)
        s.cc_id_in = s.nc.dram_tensor("cc_id_in", [IDBUF_N], F32)
        s.cc_id_out = s.nc.dram_tensor("cc_id_out", [IDBUF_N], F32)
        s.cc_e_in = s.nc.dram_tensor("cc_e_in", [CAP, D], BF16)
        s.cc_e_out = s.nc.dram_tensor("cc_e_out", [NEXP * CAP, D], BF16)
        s.xh_dram = s.nc.dram_tensor("xh_dram", [NTILE, 8 * 128 * 128], BF16)

        with s.es:
            s._alloc()
            s.blk = s.es.enter_context(s.nc.Block())
            s.wd = {k: {} for k in s.ENG}
            s._emit()
        return s.nc

    def _alloc(self):
        sb = self.sb
        s = self
        s.ident = sb("identS", [128, 128])
        s.identb = sb("identbS", [128, 128], BF16)
        s.iota8 = sb("iota8S", [128, 8])
        s.u1strict = sb("u1S", [128, 128], BF16)
        s.onesc = sb("onescS", [128, 1], BF16)
        s.onesr = sb("onesrS", [1, 512], BF16)
        s.mask_sl = sb("mslS", [128, 128])
        s.mask_ui = sb("muiS", [128, 128])
        s.mask_bdsl = sb("mbdS", [128, 128])
        s.mask_off = sb("moffS", [128, 128])
        s.corelt = sb("coreltS", [8, 1])
        s.tokid1 = sb("tokid1S", [128, NTILE])
        s.slotidx = sb("slotidxS", [128, 16], U32)
        s.onorm2s = sb("onorm2S", [128, 2])
        s.lng = sb("lngS", [128, 8])
        s.lnb = sb("lnbS", [128, 8])
        s.lnbb = sb("lnbbS", [128, 8], BF16)
        s.mixbS = sb("mixbSS", [12, 1])
        s.mixbR = sb("mixbRS", [1, 12])
        s.convw = {k: sb(k + "S", [128, 32]) for k in ("wq_c", "wk_c", "wv_c")}

        s.WsmTs = sb("WsmTsS", [128, 8 * 12], BF16)
        s.gwf = sb("gwfS", [128, 8 * 8])
        s.gwraw = sb("gwrawS", [128, 8 * 8])
        s.gwh = sb("gwhS", [128, 8 * 8], BF16)
        s.gwl = sb("gwlS", [128, 8 * 8], BF16)
        s.bconst = sb("bconstS", [1, 4096], BF16)

        s.r_top1 = sb("r_top1", [128, NTILE])
        s.r_selr = sb("r_selr", [128, NTILE])
        s.r_wgt = sb("r_wgt", [128, NTILE])
        s.r_mu = sb("r_mu", [128, NTILE])
        s.r_var = sb("r_var", [128, NTILE])
        s.r_rs = sb("r_rs", [128, NTILE])
        s.r_rs2 = sb("r_rs2", [128, NTILE])
        s.r_keep = sb("r_keep", [128, NTILE])
        s.r_rank = sb("r_rank", [128, NTILE])
        s.r_gidx = sb("r_gidx", [128, NTILE], U32)
        s.r_sidx = sb("r_sidx", [128, NTILE], U32)

        s.t_w1 = sb("t_w1", [128, NTILE])
        s.t_w2 = sb("t_w2", [128, NTILE])
        s.cntrow = sb("cntrow", [1, 8 * (NTILE + 1)])
        s.cnt8 = sb("cnt8", [8, 8])
        s.prefrow = sb("prefrow", [1, 8])
        s.cbc = sb("cbc", [128, 8])
        s.crow = sb("crow", [1, 8])


        s.t_x = [sb(f"t_x{i}", [128, D]) for i in range(2)]
        s.t_a = sb("t_a", [128, D])
        s.t_bf = [sb(f"t_bf{i}", [128, D], BF16) for i in range(4)]
        s.t_bn = sb("t_bn", [128, 12])
        s.t_ag = sb("t_ag", [128, 2])
        s.t_p8 = [sb(f"t_p8{i}", [8, 128]) for i in range(2)]
        s.t_l8 = [sb(f"t_l8{i}", [128, 8]) for i in range(2)]
        s.t_mx = [sb(f"t_mx{i}", [128, 8]) for i in range(2)]
        s.t_ai = [sb(f"t_ai{i}", [128, 8], U32) for i in range(2)]
        s.t_oh = [sb(f"t_oh{i}", [128, 8], BF16) for i in range(2)]
        s.t_ohf = [sb(f"t_ohf{i}", [128, 8]) for i in range(2)]
        s.t_s = [sb(f"t_s{i}", [128, 8]) for i in range(6)]
        s.t_zero = sb("t_zero", [128, 129])
        s.slotid = sb("slotid", [128, 16])
        s.gxi = sb("gxi", [128, 16], U32)
        s.f_mu = sb("f_mu", [128, 16])
        s.f_var = sb("f_var", [128, 16])
        s.f_sc = sb("f_sc", [128, 16])
        if "snap" in self.taps:
            s.snap_ps = sb("snap_ps", [128, 8])
            s.snap_oh = sb("snap_oh", [128, 8], BF16)
            s.snap_l8 = sb("snap_l8", [128, 8])

        s.pb = [self.psum(f"pb{i}", [128, 512]) for i in range(8)]

        s.es_s1x = ExitStack()
        s.xhb = [s.es_s1x.enter_context(s.nc.sbuf_tensor(f"xhb{i}", [128, 8 * 128], BF16))
                 for i in range(2)]
        s.xlb = [s.es_s1x.enter_context(s.nc.sbuf_tensor(f"xlb{i}", [128, 8 * 128], BF16))
                 for i in range(2)]
        s.es_s1x.close()

        s.es_ffn = ExitStack()

        def sbf(name, shape, dt=F32):
            return s.es_ffn.enter_context(s.nc.sbuf_tensor(name, list(shape), dt))
        s.d_t = [sbf(f"d_t{i}", [128, 8 * 256], BF16) for i in range(2)]
        s.h_t = [sbf("h_t0", [128, 32 * 256], BF16)]
        s.e_outb = [sbf(f"e_outb{i}", [128, D], BF16) for i in range(2)]
        s.f_x = [sbf(f"f_x{i}", [128, D]) for i in range(2)]
        s.wis = sbf("wisS", [128, 8 * 4096], BF16)
        s.wos = sbf("wosS", [128, 32 * D], BF16)
        self._ffn_freed = False

    def _emit(self):
        s = self
        ev = {}
        s.ev = ev
        C = s.ctr
        s.S = {k: C(k) for k in ("t", "v", "s", "g")}

        def ld_sy(eng, w):
            cc = C("q_const")
            for src, dst in [(s.i_ident, s.ident), (s.i_iota8, s.iota8), (s.i_msl, s.mask_sl),
                             (s.i_mui, s.mask_ui), (s.i_mbd, s.mask_bdsl), (s.i_moff, s.mask_off),
                             (s.i_corelt, s.corelt), (s.i_tokid1, s.tokid1), (s.i_identb, s.identb),
                             (s.i_u1, s.u1strict), (s.i_onesc, s.onesc), (s.i_onesr, s.onesr),
                             (s.i_slotidx, s.slotidx),
                             (s.i_onorm2, s.onorm2s), (s.i_lng, s.lng), (s.i_lnb, s.lnb),
                             (s.i_mixb, s.mixbS), (s.i_mixbr, s.mixbR)]:
                s.dma(eng, cc, dst[:, :], src[:, :])
            for k in ("wq_c", "wk_c", "wv_c"):
                s.dma(eng, cc, s.convw[k][:, :], s.i_convw[k][:, :])
            cg = C("q_gwraw")
            for kc in range(8):
                s.dma(eng, cg, s.gwraw[:, 8 * kc:8 * kc + 8], s.i_gwT[128 * kc:128 * (kc + 1), :])
            ev["consts"] = cc.evt()
            ev["gwraw"] = cg.evt()
        self.on("sy", ld_sy)

        def ld_gp(eng, w):
            cw = C("q_wis")
            for kc in range(8):
                s.dma(eng, cw, s.wis[:, kc * 4096:(kc + 1) * 4096], s.i_wi[128 * kc:128 * (kc + 1), :])
            ev["wis"] = cw.evt()
            co = C("q_wos")
            for kc in range(32):
                s.dma(eng, co, s.wos[:, kc * D:(kc + 1) * D], s.i_wo[128 * kc:128 * (kc + 1), :])
            ev["wos"] = co.evt()
            cf = C("q_gwf")
            for kc in range(8):
                s.dma(eng, cf, s.gwf[:, 8 * kc:8 * kc + 8], s.i_gwT[128 * kc:128 * (kc + 1), :])
            ev["gwf_raw"] = cf.evt()
            cm = C("q_wsm")
            for kc in range(8):
                s.dma(eng, cm, s.WsmTs[:, kc * 12:(kc + 1) * 12], s.i_WsmT[128 * kc:128 * (kc + 1), :])
            ev["wsm"] = cm.evt()
        self.on("g", ld_gp)

        def prep_v(eng, w):
            w(ev["consts"]); w(ev["gwf_raw"])
            for kc in range(8):
                eng.tensor_scalar(out=s.gwf[:, 8 * kc:8 * kc + 8], in0=s.gwf[:, 8 * kc:8 * kc + 8],
                                  scalar1=s.lng[:, kc:kc + 1], scalar2=None, op0=ALU.mult)
            eng.tensor_copy(out=s.lnbb[:, :], in_=s.lnb[:, :])
            eng.drain()
            eng.tensor_copy(out=s.gwh[:, :], in_=s.gwf[:, :])
            eng.drain()
            eng.tensor_tensor(out=s.gwl[:, :], in0=s.gwf[:, :], in1=s.gwh[:, :], op=ALU.subtract)
            ev["gwsplit"] = s.step(s.S["v"], eng.memset(s.t_zero[:, :], 0.0))
        self.on("v", prep_v)

        def crow_t(eng, w):
            w(ev["gwraw"]); w(ev["consts"])
            ps = s.pb[7]
            for kc in range(8):
                inst = eng.matmul(ps[0:1, 0:8], s.lnb[:, kc:kc + 1], s.gwraw[:, 8 * kc:8 * kc + 8],
                                  start=(kc == 0), stop=(kc == 7))
            ev["crow_mm"] = s.step(s.S["t"], inst)
        self.on("t", crow_t)

        def crow_v(eng, w):
            w(ev["crow_mm"])
            ev["crow"] = s.step(s.S["v"], eng.tensor_copy(out=s.crow[:, :], in_=s.pb[7][0:1, 0:8]))
        self.on("v", crow_v)

        def cbc_t(eng, w):
            w(ev["crow"])
            inst = eng.matmul(s.pb[7][0:128, 8:16], s.mask_ui[0:1, 0:128], s.crow[0:1, 0:8],
                              start=True, stop=True)
            ev["cbc_mm"] = s.step(s.S["t"], inst)
        self.on("t", cbc_t)

        def cbc_v(eng, w):
            w(ev["cbc_mm"])
            ev["cbc"] = s.step(s.S["v"], eng.tensor_copy(out=s.cbc[:, :], in_=s.pb[7][0:128, 8:16]))
        self.on("v", cbc_v)

        if self.stage >= 1:
            self._emit_s1()
        if self.stage >= 2:
            self._emit_cc1()
        if self.stage >= 3:
            self._emit_ffn()
        if self.stage >= 4:
            self._emit_attn()
        if self.stage >= 5:
            self._emit_combine()
        if not self._ffn_freed:
            self.es_ffn.close()
            self._ffn_freed = True
        self._emit_taps_and_finish()

    def _emit_s1(self):
        s = self
        ev = s.ev
        C = s.ctr
        SV, ST, SS = s.S["v"], s.S["t"], s.S["s"]
        qx = [C("q_x0"), C("q_x1")]
        qtp = [C("q_tp0"), C("q_tp1")]
        qxs = [C("q_xs0"), C("q_xs1")]

        for t in range(NTILE):
            r = t % 2

            def pa_sy(eng, w, t=t, r=r):
                if t >= 2:
                    w(ev[f"statdone_{t-2}"])
                ev[f"xa_{t}"] = s.dma(eng, qx[r], s.t_x[r][:, :], s.x_own[128 * t:128 * (t + 1), :])
            self.on("sy", pa_sy)

            def pa_v(eng, w, t=t, r=r):
                w(ev[f"xa_{t}"])
                eng.bn_stats(out=s.t_bn[:, 0:6], in_=s.t_x[r][:, 0:512])
                eng.bn_stats(out=s.t_bn[:, 6:12], in_=s.t_x[r][:, 512:1024])
                eng.drain()
                eng.bn_aggr(out=s.t_ag[:, :], in_=s.t_bn[:, :])
                eng.drain()
                eng.tensor_copy(out=s.r_mu[:, t:t + 1], in_=s.t_ag[:, 0:1])
                ev[f"statdone_{t}"] = s.step(SV, eng.tensor_copy(out=s.r_var[:, t:t + 1],
                                                                 in_=s.t_ag[:, 1:2]))
            self.on("v", pa_v)

        def pb_v1(eng, w):
            ev["vareps"] = s.step(SV, eng.tensor_scalar(out=s.t_w1[:, :], in0=s.r_var[:, :],
                                                        scalar1=EPS_LN, scalar2=None, op0=ALU.add))
        self.on("v", pb_v1)

        def pb_s1(eng, w):
            w(ev["vareps"])
            ev["sqrt1"] = s.step(SS, eng.activation(out=s.t_w2[:, :], in_=s.t_w1[:, :], func=AF.Sqrt))
        self.on("s", pb_s1)

        def pb_v2(eng, w):
            w(ev["sqrt1"])
            eng.reciprocal(out=s.r_rs[:, :], in_=s.t_w2[:, :])
            eng.drain()
            eng.tensor_tensor(out=s.t_w1[:, :], in0=s.r_rs[:, :], in1=s.r_rs[:, :], op=ALU.mult)
            ev["v2e"] = s.step(SV, eng.tensor_scalar(out=s.t_w1[:, :], in0=s.t_w1[:, :],
                                                     scalar1=-EPS_LN, scalar2=(1.0 + EPS_LN),
                                                     op0=ALU.mult, op1=ALU.add))
        self.on("v", pb_v2)

        def pb_s2(eng, w):
            w(ev["v2e"])
            ev["sqrt2"] = s.step(SS, eng.activation(out=s.t_w2[:, :], in_=s.t_w1[:, :], func=AF.Sqrt))
        self.on("s", pb_s2)

        def pb_v3(eng, w):
            w(ev["sqrt2"])
            eng.reciprocal(out=s.r_rs2[:, :], in_=s.t_w2[:, :])
            eng.drain()
            ev["stats"] = s.step(SV, eng.tensor_copy(out=s.t_s[0][:, 0:1], in_=s.r_rs2[:, 0:1]))
        self.on("v", pb_v3)

        for t in range(NTILE):
            r = t % 2

            def pbt_sy(eng, w, t=t, r=r):
                if t >= 2:
                    w(ev[f"xbf_{t-2}"])
                else:
                    w(ev[f"statdone_{NTILE - 2 + t}"])
                ev[f"xb_{t}"] = s.dma(eng, qx[r], s.t_x[r][:, :], s.x_own[128 * t:128 * (t + 1), :])
            self.on("sy", pbt_sy)

            def pbt_v(eng, w, t=t, r=r):
                w(ev[f"xb_{t}"]); w(ev["stats"])
                if t >= 2:
                    w(ev[f"xpose_{t-2}"])
                eng.tensor_scalar(out=s.t_bf[r][:, :], in0=s.t_x[r][:, :], scalar1=s.r_mu[:, t:t + 1],
                                  scalar2=s.r_rs[:, t:t + 1], op0=ALU.subtract, op1=ALU.mult)
                eng.tensor_scalar(out=s.t_a[:, :], in0=s.t_x[r][:, :], scalar1=s.r_mu[:, t:t + 1],
                                  scalar2=s.r_rs[:, t:t + 1], op0=ALU.subtract, op1=ALU.mult)
                ev[f"xbf_{t}"] = s.step(SV, eng.tensor_tensor(out=s.t_bf[2 + r][:, :], in0=s.t_a[:, :],
                                                              in1=s.t_bf[r][:, :], op=ALU.subtract))
            self.on("v", pbt_v)

            def pbt_sc(eng, w, t=t, r=r):
                w(ev[f"xbf_{t}"])
                if t >= 2:
                    w(ev[f"lgmm_{t-2}"])
                    w(ev[f"xhst_{t-2}"])
                for kc in range(8):
                    s.dma(eng, qtp[r], s.xhb[r][:, 128 * kc:128 * (kc + 1)],
                          s.t_bf[r][:, 128 * kc:128 * (kc + 1)], transpose=True)
                    s.dma(eng, qtp[r], s.xlb[r][:, 128 * kc:128 * (kc + 1)],
                          s.t_bf[2 + r][:, 128 * kc:128 * (kc + 1)], transpose=True)
                ev[f"xpose_{t}"] = qtp[r].evt()
            self.on("s", pbt_sc)

            def pbt_st(eng, w, t=t, r=r):
                w(ev[f"xpose_{t}"])
                ev[f"xhst_{t}"] = s.dma(eng, qxs[r], s.xh_dram.ap()[t, :], s.xhb[r][:, :])
            self.on("sy", pbt_st)

            def pbt_t1(eng, w, t=t, r=r):
                w(ev[f"xpose_{t}"]); w(ev["gwsplit"]); w(ev["cbc"])
                if t >= 2:
                    w(ev[f"pev_{t-2}"])
                psl = s.pb[4 if r == 0 else 7]
                first = True
                for kc in range(8):
                    xs = s.xhb[r][:, 128 * kc:128 * (kc + 1)]
                    xl = s.xlb[r][:, 128 * kc:128 * (kc + 1)]
                    eng.matmul(psl[0:8, 0:128], s.gwh[:, 8 * kc:8 * kc + 8], xs, start=first, stop=False)
                    first = False
                    eng.matmul(psl[0:8, 0:128], s.gwl[:, 8 * kc:8 * kc + 8], xs, start=False, stop=False)
                    inst = eng.matmul(psl[0:8, 0:128], s.gwh[:, 8 * kc:8 * kc + 8], xl, start=False,
                                      stop=(kc == 7))
                ev[f"lgmm_{t}"] = s.step(ST, inst)
            self.on("t", pbt_t1)

            def pbt_v2(eng, w, t=t, r=r):
                w(ev[f"lgmm_{t}"])
                psl = s.pb[4 if r == 0 else 7]
                ev[f"pev_{t}"] = s.step(SV, eng.tensor_copy(out=s.t_p8[r][:, :], in_=psl[0:8, 0:128]))
            self.on("v", pbt_v2)

            def pbt_t2(eng, w, t=t, r=r):
                w(ev[f"pev_{t}"])
                if t >= 2:
                    w(ev[f"rout_{t-2}"])
                inst = eng.matmul(s.pb[5][0:128, 8 * r:8 * r + 8], s.t_p8[r][:, :], s.ident[0:8, 0:8],
                                  start=True, stop=True)
                ev[f"ptm_{t}"] = s.step(ST, inst)
            self.on("t", pbt_t2)

            def pbt_v3(eng, w, t=t, r=r):
                w(ev[f"ptm_{t}"])
                eng.scalar_tensor_tensor(out=s.t_l8[r][:, :], in0=s.pb[5][0:128, 8 * r:8 * r + 8],
                                         scalar=s.r_rs2[:, t:t + 1], in1=s.cbc[:, :],
                                         op0=ALU.mult, op1=ALU.add)
                eng.drain()
                eng.max(out=s.t_mx[r][:, :], in_=s.t_l8[r][:, :])
                eng.drain()
                eng.max_index(out=s.t_ai[r][:, :], in_max=s.t_mx[r][:, :], in_values=s.t_l8[r][:, :])
                eng.drain()
                eng.tensor_copy(out=s.r_top1[:, t:t + 1], in_=s.t_ai[r][:, 0:1])
                eng.tensor_scalar(out=s.t_s[0][:, r:r + 1], in0=s.t_mx[r][:, 0:1], scalar1=-1.0,
                                  scalar2=None, op0=ALU.mult)
                eng.tensor_scalar(out=s.t_oh[r][:, :], in0=s.iota8[:, :],
                                  scalar1=s.r_top1[:, t:t + 1], scalar2=None, op0=ALU.is_equal)
                eng.drain()
                ev[f"rout_{t}"] = s.step(SV, eng.tensor_copy(out=s.t_ohf[r][:, :], in_=s.t_oh[r][:, :]))
            self.on("v", pbt_v3)

            def pbt_s2(eng, w, t=t, r=r):
                w(ev[f"rout_{t}"])
                ev[f"esum_{t}"] = s.step(SS, eng.activation(
                    out=s.t_s[1][:, 0:8], in_=s.t_l8[r][:, :], func=AF.Exp,
                    bias=s.t_s[0][:, r:r + 1], accum_out=s.t_s[2][:, r:r + 1]))
            self.on("s", pbt_s2)

            def pbt_v4(eng, w, t=t, r=r):
                w(ev[f"esum_{t}"])
                ev[f"wgt_{t}"] = s.step(SV, eng.reciprocal(out=s.r_wgt[:, t:t + 1],
                                                           in_=s.t_s[2][:, r:r + 1]))
            self.on("v", pbt_v4)

            def pbt_t3(eng, w, t=t, r=r):
                w(ev[f"rout_{t}"])
                if t >= 2:
                    w(ev[f"selr_{t-2}"])
                eng.matmul(s.pb[5][0:128, 16 + 8 * r:24 + 8 * r], s.u1strict[:, :], s.t_oh[r][:, :],
                           start=True, stop=True)
                inst = eng.matmul(s.pb[6][0:1, 8 * t:8 * t + 8], s.onesc[:, 0:1], s.t_oh[r][:, :],
                                  start=(t == 0), stop=(t == NTILE - 1))
                ev[f"cnt_{t}"] = s.step(ST, inst)
            self.on("t", pbt_t3)

            def pbt_v5(eng, w, t=t, r=r):
                w(ev[f"cnt_{t}"])
                eng.tensor_copy(out=s.t_s[3][:, 0:8], in_=s.pb[5][0:128, 16 + 8 * r:24 + 8 * r])
                eng.drain()
                eng.tensor_tensor(out=s.t_s[3][:, 0:8], in0=s.t_s[3][:, 0:8],
                                  in1=s.t_ohf[r][:, :], op=ALU.mult)
                eng.drain()
                eng.tensor_reduce(out=s.t_s[4][:, 0:1], in_=s.t_s[3][:, 0:8],
                                  axis=AX.X, op=ALU.add)
                eng.drain()
                eng.tensor_copy(out=s.r_selr[:, t:t + 1], in_=s.t_s[4][:, 0:1])
                if t == self.tsnap and "snap" in self.taps:
                    eng.tensor_copy(out=s.snap_ps[:, 0:8], in_=s.pb[5][0:128, 16 + 8 * r:24 + 8 * r])
                    eng.tensor_copy(out=s.snap_oh[:, :], in_=s.t_oh[r][:, :])
                    eng.tensor_copy(out=s.snap_l8[:, :], in_=s.t_l8[r][:, :])
                ev[f"selr_{t}"] = s.step(SV, eng.tensor_copy(out=s.t_s[3][:, 0:1], in_=s.t_s[3][:, 0:1]))
            self.on("v", pbt_v5)

        def cnts_v(eng, w):
            w(ev[f"cnt_{NTILE-1}"])
            eng.memset(s.cntrow[:, 0:8], 0.0)
            eng.tensor_copy(out=s.cntrow[:, 8:8 + 8 * NTILE], in_=s.pb[6][0:1, 0:8 * NTILE])
            for t in range(NTILE):
                eng.drain()
                inst = eng.tensor_tensor(out=s.cntrow[:, 8 * (t + 1):8 * (t + 2)],
                                         in0=s.cntrow[:, 8 * (t + 1):8 * (t + 2)],
                                         in1=s.cntrow[:, 8 * t:8 * (t + 1)], op=ALU.add)
            eng.drain()
            ev["cntincl"] = s.step(SV, inst)
        self.on("v", cnts_v)

    def _emit_cc1(self):
        s = self
        ev = s.ev
        C = s.ctr
        SV, ST = s.S["v"], s.S["t"]
        ccs = C("cc")

        def cc_g(eng, w):
            w(ev["cntincl"])
            c1 = C("q_ccin")
            s.dma(eng, c1, s.cc_cnt_in.ap(), s.cntrow[:, 8 * NTILE:8 * (NTILE + 1)])
            eng.wait_ge(c1.sem, c1.n)
            if self.sub & 1:
                inst = eng.collective_compute("AllGather", ALU.bypass,
                                              replica_groups=[list(range(NCORES))],
                                              ins=[s.cc_cnt_in.ap().opt()],
                                              outs=[s.cc_cnt_out.ap().opt()])
                s.step(ccs, inst)
                eng.wait_ge(ccs.sem, ccs.n)
                c2 = C("q_ccout")
                ev["cnt8"] = s.dma(eng, c2, s.cnt8[:, :], s.cc_cnt_out.ap())
            else:
                c2 = C("q_ccout")
                for cc_ in range(8):
                    ev["cnt8"] = s.dma(eng, c2, s.cnt8[cc_:cc_ + 1, :], s.cc_cnt_in.ap())
        self.on("g", cc_g)

        def pref_t(eng, w):
            w(ev["cnt8"])
            inst = eng.matmul(s.pb[5][0:1, 32:40], s.corelt[:, :], s.cnt8[:, :], start=True, stop=True)
            ev["prefmm"] = s.step(ST, inst)
        self.on("t", pref_t)

        def pref_v(eng, w):
            w(ev["prefmm"])
            ev["prefrow"] = s.step(SV, eng.tensor_copy(out=s.prefrow[:, :], in_=s.pb[5][0:1, 32:40]))
        self.on("v", pref_v)

        for t in range(NTILE if (self.sub & 2) else 0):
            r = t % 2

            def p2_t(eng, w, t=t, r=r):
                w(ev["prefrow"])
                if t >= 1 and not (self.sub & 16):
                    w(ev[f"p2v_{t-1}"])
                ps = s.pb[5]
                sl = slice(40 + 8 * r, 48 + 8 * r)
                eng.matmul(ps[0:128, sl], s.mask_ui[0:1, 0:128], s.cntrow[0:1, 8 * t:8 * (t + 1)],
                           start=True, stop=False)
                inst = eng.matmul(ps[0:128, sl], s.mask_ui[0:1, 0:128], s.prefrow[0:1, 0:8],
                                  start=False, stop=True)
                ev[f"p2mm_{t}"] = s.step(ST, inst)
            self.on("t", p2_t)

            if self.sub & 16:
                continue

            def p2_v(eng, w, t=t, r=r):
                w(ev[f"p2mm_{t}"])
                sl = slice(40 + 8 * r, 48 + 8 * r)
                if self.sub & 64:
                    ev[f"p2v_{t}"] = s.step(SV, eng.tensor_copy(out=s.t_s[5][:, 0:1],
                                                                in_=s.iota8[:, 0:1]))
                    return
                eng.tensor_scalar(out=s.t_oh[r][:, :], in0=s.iota8[:, :], scalar1=s.r_top1[:, t:t + 1],
                                  scalar2=None, op0=ALU.is_equal)
                eng.tensor_copy(out=s.t_ohf[r][:, :], in_=s.t_oh[r][:, :])
                if self.sub & 128:
                    ev[f"p2v_{t}"] = s.step(SV, eng.tensor_copy(out=s.t_s[5][:, 0:1],
                                                                in_=s.iota8[:, 0:1]))
                    return
                eng.tensor_copy(out=s.t_s[3][:, 0:8], in_=s.pb[5][0:128, sl])
                eng.tensor_tensor(out=s.t_s[3][:, 0:8], in0=s.t_s[3][:, 0:8], in1=s.t_ohf[r][:, :],
                                  op=ALU.mult)
                eng.drain()
                eng.tensor_reduce(out=s.t_s[4][:, 0:1], in_=s.t_s[3][:, 0:8], axis=AX.X, op=ALU.add)
                eng.drain()
                if self.sub & 256:
                    ev[f"p2v_{t}"] = s.step(SV, eng.tensor_copy(out=s.t_s[5][:, 0:1],
                                                                in_=s.iota8[:, 0:1]))
                    return
                eng.tensor_tensor(out=s.r_rank[:, t:t + 1], in0=s.r_selr[:, t:t + 1],
                                  in1=s.t_s[4][:, 0:1], op=ALU.add)
                if self.sub & 32:
                    ev[f"p2v_{t}"] = s.step(SV, eng.tensor_copy(out=s.r_sidx[:, t:t + 1],
                                                                in_=s.t_s[5][:, 0:1]))
                    return
                eng.tensor_scalar(out=s.r_keep[:, t:t + 1], in0=s.r_rank[:, t:t + 1],
                                  scalar1=float(CAP) - 0.5, scalar2=None, op0=ALU.is_le)
                eng.tensor_tensor(out=s.r_wgt[:, t:t + 1], in0=s.r_wgt[:, t:t + 1],
                                  in1=s.r_keep[:, t:t + 1], op=ALU.mult)
                eng.tensor_scalar(out=s.t_s[4][:, 0:1], in0=s.r_rank[:, t:t + 1],
                                  scalar1=float(CAP - 1), scalar2=None, op0=ALU.min)
                eng.scalar_tensor_tensor(out=s.t_s[5][:, 0:1], in0=s.r_top1[:, t:t + 1],
                                         scalar=float(CAP), in1=s.t_s[4][:, 0:1],
                                         op0=ALU.mult, op1=ALU.add)
                eng.tensor_copy(out=s.r_gidx[:, t:t + 1], in_=s.t_s[5][:, 0:1])
                eng.scalar_tensor_tensor(out=s.t_s[5][:, 0:1], in0=s.r_top1[:, t:t + 1],
                                         scalar=float(CAP), in1=s.r_rank[:, t:t + 1],
                                         op0=ALU.mult, op1=ALU.add)
                eng.scalar_tensor_tensor(out=s.t_s[5][:, 0:1], in0=s.r_keep[:, t:t + 1],
                                         scalar=-100000.0, in1=s.t_s[5][:, 0:1],
                                         op0=ALU.mult, op1=ALU.add)
                eng.tensor_scalar(out=s.t_s[5][:, 0:1], in0=s.t_s[5][:, 0:1], scalar1=100000.0,
                                  scalar2=None, op0=ALU.add)
                ev[f"p2v_{t}"] = s.step(SV, eng.tensor_copy(out=s.r_sidx[:, t:t + 1],
                                                            in_=s.t_s[5][:, 0:1]))
            self.on("v", p2_v)

        if not (self.sub & 4):
            return

        def idz_sy(eng, w):
            w(ev["gwsplit"])
            cz = C("q_idz")
            idv = s.cc_id_in.ap().rearrange("(p f) -> p f", p=128)
            ev["idz"] = s.dma(eng, cz, idv, s.t_zero[:, 0:IDBUF_N // 128])
        self.on("sy", idz_sy)

        def scat_g(eng, w):
            w(ev["idz"])
            w(ev[f"p2v_{NTILE-1}"])
            csc = C("q_scat")
            for t in range(NTILE):
                s.idma(eng, csc,
                       out=s.cc_id_in.ap().rearrange("(n o) -> n o", o=1),
                       out_offset=IOoA(ap=s.r_sidx[:, t:t + 1], axis=0),
                       in_=s.tokid1[:, t:t + 1], in_offset=None,
                       bounds_check=NEXP * CAP - 1, oob_is_err=False)
            eng.wait_ge(csc.sem, csc.n)
            ccs = s.ctr("cc")
            inst = eng.collective_compute("AllReduce", ALU.add, replica_groups=[list(range(NCORES))],
                                          ins=[s.cc_id_in.ap().opt()], outs=[s.cc_id_out.ap().opt()])
            s.step(ccs, inst)
            eng.wait_ge(ccs.sem, ccs.n)
            if self.sub & 8:
                cgt = C("q_slotg")
                for c in range(16):
                    s.idma(eng, cgt,
                           out=s.slotid[:, c:c + 1], out_offset=None,
                           in_=s.cc_id_out.ap().rearrange("(n o) -> n o", o=1),
                           in_offset=IOoA(ap=s.slotidx[:, c:c + 1], axis=0))
                ev["slotid"] = cgt.evt()
        self.on("g", scat_g)

        if not (self.sub & 8):
            return

        def gxi_v(eng, w):
            w(ev["slotid"])
            eng.tensor_scalar(out=s.f_sc[:, :], in0=s.slotid[:, :], scalar1=-1.0, scalar2=0.0,
                              op0=ALU.add, op1=ALU.max)
            ev["gxi"] = s.step(s.S["v"], eng.tensor_copy(out=s.gxi[:, :], in_=s.f_sc[:, :]))
        self.on("v", gxi_v)

    def _emit_ffn(self):
        s = self
        ev = s.ev
        C = s.ctr
        SV, ST, SS, SG = s.S["v"], s.S["t"], s.S["s"], s.S["g"]

        for ch in range(8):
            def bc_t1(eng, w, ch=ch):
                w(ev["wis"]); w(ev["consts"])
                if ch > 0:
                    w(ev[f"bcev_{ch-1}"])
                for kc in range(8):
                    inst = eng.matmul(s.pb[0][0:1, 0:512], s.lnbb[:, kc:kc + 1],
                                      s.wis[:, kc * 4096 + 512 * ch: kc * 4096 + 512 * (ch + 1)],
                                      start=(kc == 0), stop=(kc == 7))
                ev[f"bcmm_{ch}"] = s.step(ST, inst)
            self.on("t", bc_t1)

            def bc_v1(eng, w, ch=ch):
                w(ev[f"bcmm_{ch}"])
                ev[f"bcev_{ch}"] = s.step(SV, eng.tensor_copy(out=s.bconst[:, 512 * ch:512 * (ch + 1)],
                                                              in_=s.pb[0][0:1, 0:512]))
            self.on("v", bc_v1)

        def wis_v(eng, w):
            w(ev["bcev_7"]); w(ev["consts"])
            for kc in range(8):
                inst = eng.tensor_scalar(out=s.wis[:, kc * 4096:(kc + 1) * 4096],
                                         in0=s.wis[:, kc * 4096:(kc + 1) * 4096],
                                         scalar1=s.lng[:, kc:kc + 1], scalar2=None, op0=ALU.mult)
            ev["wis_scaled"] = s.step(SV, inst)
        self.on("v", wis_v)

        qgx = [C("q_gx0"), C("q_gx1")]
        qdt = [C("q_dt0"), C("q_dt1")]
        qes = [C("q_es0"), C("q_es1")]
        NT2 = 8

        def emit_tile(c):
            rc = c % 2

            def fg_g(eng, w, c=c, rc=rc):
                w(ev["gxi"])
                if c >= 2:
                    w(ev[f"fd_{c-2}"])
                ev[f"fgx_{c}"] = s.idma(eng, qgx[rc],
                                        out=s.f_x[rc][:, :], out_offset=None,
                                        in_=s.x_full[:, :],
                                        in_offset=IOoA(ap=s.gxi[:, c:c + 1], axis=0))
            self.on("g", fg_g)

            def fs_v(eng, w, c=c, rc=rc):
                w(ev[f"fgx_{c}"])
                eng.bn_stats(out=s.t_bn[:, 0:6], in_=s.f_x[rc][:, 0:512])
                eng.bn_stats(out=s.t_bn[:, 6:12], in_=s.f_x[rc][:, 512:1024])
                eng.drain()
                eng.bn_aggr(out=s.t_ag[:, :], in_=s.t_bn[:, :])
                eng.drain()
                eng.tensor_copy(out=s.f_mu[:, c:c + 1], in_=s.t_ag[:, 0:1])
                ev[f"fstat_{c}"] = s.step(SV, eng.tensor_scalar(
                    out=s.t_s[0][:, rc:rc + 1], in0=s.t_ag[:, 1:2],
                    scalar1=EPS_LN, scalar2=None, op0=ALU.add))
            self.on("v", fs_v)

            def fs_s(eng, w, c=c, rc=rc):
                w(ev[f"fstat_{c}"])
                ev[f"fsq_{c}"] = s.step(SS, eng.activation(out=s.t_s[1][:, rc:rc + 1],
                                                           in_=s.t_s[0][:, rc:rc + 1], func=AF.Sqrt))
            self.on("s", fs_s)

            def fs_v2(eng, w, c=c, rc=rc):
                w(ev[f"fsq_{c}"])
                eng.reciprocal(out=s.t_s[2][:, rc:rc + 1], in_=s.t_s[1][:, rc:rc + 1])
                eng.drain()
                eng.tensor_tensor(out=s.t_s[3][:, rc:rc + 1], in0=s.t_s[2][:, rc:rc + 1],
                                  in1=s.t_s[2][:, rc:rc + 1], op=ALU.mult)
                eng.tensor_scalar(out=s.t_s[3][:, rc:rc + 1], in0=s.t_s[3][:, rc:rc + 1],
                                  scalar1=-EPS_LN, scalar2=(1.0 + EPS_LN), op0=ALU.mult, op1=ALU.add)
                ev[f"fv2_{c}"] = s.step(SV, eng.tensor_copy(out=s.t_s[4][:, rc:rc + 1],
                                                            in_=s.t_s[3][:, rc:rc + 1]))
            self.on("v", fs_v2)

            def fs_s2(eng, w, c=c, rc=rc):
                w(ev[f"fv2_{c}"])
                ev[f"fsq2_{c}"] = s.step(SS, eng.activation(out=s.t_s[5][:, rc:rc + 1],
                                                            in_=s.t_s[4][:, rc:rc + 1], func=AF.Sqrt))
            self.on("s", fs_s2)

            def fd_v(eng, w, c=c, rc=rc):
                w(ev[f"fsq2_{c}"])
                if c >= 4:
                    w(ev[f"hmm_{32 * (c // 2 - 2) + 31}"])
                eng.reciprocal(out=s.t_s[4][:, rc:rc + 1], in_=s.t_s[5][:, rc:rc + 1])
                eng.drain()
                eng.tensor_tensor(out=s.f_sc[:, c:c + 1], in0=s.t_s[2][:, rc:rc + 1],
                                  in1=s.t_s[4][:, rc:rc + 1], op=ALU.mult)
                eng.drain()
                ev[f"fd_{c}"] = s.step(SV, eng.tensor_scalar(
                    out=s.t_bf[rc][:, :], in0=s.f_x[rc][:, :],
                    scalar1=s.f_mu[:, c:c + 1], scalar2=s.f_sc[:, c:c + 1],
                    op0=ALU.subtract, op1=ALU.mult))
            self.on("v", fd_v)

            def fd_sc(eng, w, c=c, rc=rc):
                w(ev[f"fd_{c}"])
                if c < 4 and self.stage >= 1:
                    w(ev[f"xhst_{NTILE-2}"]); w(ev[f"xhst_{NTILE-1}"])
                    w(ev[f"lgmm_{NTILE-1}"])
                if c >= 4:
                    w(ev[f"hmm_{32 * (c // 2 - 2) + 31}"])
                st2 = (c // 2) % 2
                for kc in range(8):
                    s.dma(eng, qdt[rc], s.d_t[st2][:, kc * 256 + rc * 128: kc * 256 + rc * 128 + 128],
                          s.t_bf[rc][:, 128 * kc:128 * (kc + 1)], transpose=True)
                ev[f"fdt_{c}"] = qdt[rc].evt()
            self.on("s", fd_sc)

        def emit_step(st):
            p = st % 2

            for ic4 in range(8):
                def fh_t(eng, w, st=st, p=p, ic4=ic4):
                    if ic4 == 0:
                        w(ev[f"fdt_{2*st}"]); w(ev[f"fdt_{2*st+1}"]); w(ev["wis_scaled"])
                        if st == 0:
                            w(ev[f"pev_{NTILE-1}"])
                    for ic in range(4 * ic4, 4 * ic4 + 4):
                        g = 32 * st + ic
                        if g >= 4:
                            w(ev[f"hgel_{g-4}"])
                        psl = s.pb[(4, 7, 1, 2)[ic % 4]]
                        for kc in range(8):
                            eng.matmul(psl[0:128, 0:256],
                                       s.wis[:, kc * 4096 + 128 * ic:kc * 4096 + 128 * (ic + 1)],
                                       s.d_t[p][:, kc * 256:(kc + 1) * 256], start=(kc == 0), stop=False)
                        inst = eng.matmul(psl[0:128, 0:256], s.bconst[0:1, 128 * ic:128 * (ic + 1)],
                                          s.onesr[0:1, 0:256], start=False, stop=True)
                        ev[f"hmm_{g}"] = s.step(ST, inst)
                self.on("t", fh_t)

                def fh_s(eng, w, st=st, p=p, ic4=ic4):
                    for ic in range(4 * ic4, 4 * ic4 + 4):
                        g = 32 * st + ic
                        w(ev[f"hmm_{g}"])
                        psl = s.pb[(4, 7, 1, 2)[ic % 4]]
                        inst = eng.activation(out=s.h_t[0][:, 256 * ic:256 * (ic + 1)],
                                              in_=psl[0:128, 0:256], func=AF.Gelu)
                        ev[f"hgel_{g}"] = s.step(SS, inst)
                self.on("s", fh_s)

            for mt in range(2):
                def fe_t(eng, w, st=st, p=p, mt=mt):
                    if mt == 0:
                        w(ev[f"hgel_{32*st+31}"]); w(ev["wos"])
                    if st == 0 and mt == 0:
                        w(ev[f"p2v_{NTILE-1}"]); w(ev["cntincl"])
                    prev = (st, mt - 1) if mt == 1 else (st - 1, 1)
                    if prev[0] >= 0:
                        w(ev[f"eoev_{prev[0]}_{prev[1]}"])
                    for nn_ in range(2):
                        psl = s.pb[5 if nn_ == 0 else 6]
                        for kc in range(32):
                            inst = eng.matmul(psl[0:128, 0:512],
                                              s.h_t[0][:, 256 * kc + 128 * mt:256 * kc + 128 * mt + 128],
                                              s.wos[:, kc * D + 512 * nn_: kc * D + 512 * (nn_ + 1)],
                                              start=(kc == 0), stop=(kc == 31))
                        ev[f"eomm_{st}_{mt}_{nn_}"] = s.step(ST, inst)
                self.on("t", fe_t)

                def fe_v(eng, w, st=st, p=p, mt=mt):
                    w(ev[f"eomm_{st}_{mt}_0"]); w(ev[f"eomm_{st}_{mt}_1"])
                    if st > 0:
                        w(ev[f"eost_{st-1}_{mt}"])
                    eng.tensor_copy(out=s.e_outb[mt][:, 0:512], in_=s.pb[5][0:128, 0:512])
                    inst = eng.tensor_copy(out=s.e_outb[mt][:, 512:1024], in_=s.pb[6][0:128, 0:512])
                    ev[f"eoev_{st}_{mt}"] = s.step(SV, inst)
                self.on("v", fe_v)

                def fe_g(eng, w, st=st, p=p, mt=mt):
                    w(ev[f"eoev_{st}_{mt}"])
                    ev[f"eost_{st}_{mt}"] = s.dma(
                        eng, qes[mt],
                        s.cc_e_in.ap()[256 * st + 128 * mt:256 * st + 128 * (mt + 1), :],
                        s.e_outb[mt][:, :])
                self.on("g", fe_g)

        for st in range(NT2):
            emit_tile(2 * st)
            emit_tile(2 * st + 1)
            emit_step(st)

        def eag_g(eng, w):
            for st in range(NT2):
                for mt in range(2):
                    w(ev[f"eost_{st}_{mt}"])
            ccs = s.ctr("cc")
            inst = eng.collective_compute("AllGather", ALU.bypass, replica_groups=[list(range(NCORES))],
                                          ins=[s.cc_e_in.ap().opt()], outs=[s.cc_e_out.ap().opt()])
            ev["eall"] = s.step(ccs, inst)
        self.on("g", eag_g)

    def _emit_attn(self):
        s = self
        ev = s.ev
        C = s.ctr
        SV, ST, SS, SG = s.S["v"], s.S["t"], s.S["s"], s.S["g"]

        s.es_ffn.close()
        s._ffn_freed = True
        s.es_att = ExitStack()

        def sba(name, shape, dt=F32):
            return s.es_att.enter_context(s.nc.sbuf_tensor(name, list(shape), dt))
        s.WqTs = sba("WqTsS", [128, 8 * D], BF16)
        s.WkTs = sba("WkTsS", [128, 8 * D], BF16)
        s.WvTs = sba("WvTsS", [128, 8 * D], BF16)
        s.WoTs = sba("WoTsS", [128, 8 * D], BF16)
        s.xh_g = sba("xh_gS", [128, 8 * 512], BF16)
        s.S_f32 = sba("S_f32S", [128, 4 * 512])
        s.sqs = sba("sqsS", [128, 256], BF16)
        s.q_fm = sba("q_fmS", [128, 8 * 512], BF16)
        s.k_fm = sba("k_fmS", [128, 8 * 512], BF16)
        s.k_tm = sba("k_tmS", [128, 4 * 1024], BF16)
        s.v_tm = sba("v_tmS", [128, 4 * 1024], BF16)
        s.v_fm = sba("v_fmS", [128, 8 * 512], BF16)
        s.praw = [sba(f"praw{i}", [128, 516], BF16) for i in range(2)]
        s.cvt = [sba(f"cvt{i}", [128, 512], BF16) for i in range(2)]
        s.tails = sba("tailsS", [128, 3 * 8 * 4])
        s.smrow = sba("smrowS", [12, 512])
        s.smrowb = sba("smrowbS", [12, 512], BF16)
        s.smcol = sba("smcolS", [128, 4 * 12])
        s.Aoff = sba("AoffS", [128, 16 * 128], BF16)
        s.Wt = sba("WtS", [128, 16 * 128], BF16)
        s.Tt = [sba("TtS", [128, 128], BF16), sba("Tt2S", [128, 128], BF16)]
        s.Abd = sba("AbdS", [128, 128])
        s.zerob = sba("zerobS", [1, 256], BF16)
        s.gdec = sba("gdecS", [1, 2048], BF16)
        s.fs_t = sba("fs_tS", [128, 1024])
        s.S_sb = sba("S_sbS", [128, 4 * 512], BF16)
        s.G_sb = sba("G_sbS", [128, 256], BF16)
        s.JT_sb = sba("JT_sbS", [128, 128], BF16)
        s.g0_sb = sba("g0_sbS", [128, 256], BF16)
        s.h1_sb = sba("h1_sbS", [128, 256], BF16)
        s.u2_sb = sba("u2_sbS", [128, 256], BF16)
        s.aT_sb = sba("aT_sbS", [128, 128], BF16)
        s.o_tm = s.t_bf[1]
        s.onf = s.t_bf[2]
        s.ema_t = s.t_bf[3]
        s.gbc = [sba("gbcS", [128, 512], BF16), sba("gbc2S", [128, 512], BF16)]
        s.d1b = [sba("d1bS", [128, 512], BF16), sba("d1b2S", [128, 512], BF16)]
        s.ema_st = sba("ema_stS", [128, 8])
        s.bv_sb = sba("bv_sbS", [128, 256], BF16)
        s.kq2 = s.t_bf[0]
        s.l2r = sba("l2rS", [1, 512])
        s.colsc = sba("colscS", [128, 4 * 16])
        s.gE = [sba(f"gE{i}", [128, D], BF16) for i in range(2)]

        qxg = [C("q_xg0"), C("q_xg1")]
        qkt = [C("q_kt0"), C("q_kt1")]
        qfsi = [C("q_fsi0"), C("q_fsi1")]
        qfso = [C("q_fso0"), C("q_fso1")]
        qtt = [C("q_tt0"), C("q_tt1")]
        qoe = [C("q_oe0"), C("q_oe1")]
        qxr = [C("q_xr0"), C("q_xr1")]
        qos = [C("q_os0"), C("q_os1")]
        qge = [C("q_ge0"), C("q_ge1")]
        qwa = C("q_wat")

        pbev = {4: ev.get(f"pev_{NTILE-1}"), 5: None, 6: None, 7: ev.get(f"pev_{NTILE-1}")}
        s.pbS = {h: None for h in range(4)}

        def wat_g(eng, w):
            w(ev["eomm_7_1_1"])
            w(ev["eost_7_0"]); w(ev["eost_7_1"])
            w(ev[f"hgel_{8*32-1}"])
            for W_in, W_sb in [(s.i_WqT, s.WqTs), (s.i_WkT, s.WkTs), (s.i_WvT, s.WvTs),
                               (s.i_WoT, s.WoTs)]:
                for kc in range(8):
                    s.dma(eng, qwa, W_sb[:, kc * D:(kc + 1) * D], W_in[128 * kc:128 * (kc + 1), :])
            ev["wat"] = qwa.evt()
        self.on("g", wat_g)

        def wo_v(eng, w):
            w(ev["wat"]); w(ev["consts"])
            eng.memset(s.zerob[:, :], 0.0)
            eng.memset(s.t_zero[:, 120:121], EPS_L2)
            eng.memset(s.t_zero[:, 121:122], EPS_RMS)
            eng.memset(s.Tt[0][:, :], 0.0)
            inst = eng.memset(s.Tt[1][:, :], 0.0)
            ev["ttinit"] = s.step(SV, inst)
            for kc in range(8):
                inst = eng.tensor_scalar(out=s.WoTs[:, kc * D:(kc + 1) * D],
                                         in0=s.WoTs[:, kc * D:(kc + 1) * D],
                                         scalar1=s.onorm2s[:, kc % 2:kc % 2 + 1],
                                         scalar2=None, op0=ALU.mult)
            ev["wofold"] = s.step(SV, inst)
        self.on("v", wo_v)

        for sg in range(SEQ_PER_CORE * NGROUP):
            sq, g = divmod(sg, NGROUP)
            self._attn_pphase(sg, sq, g, pbev)
            self._attn_scan(sg, sq, g, pbev)
        s.es_att.close()

    def _attn_pphase(self, sg, sq, g, pbev):
        s = self
        ev = s.ev
        ALUc = ALU
        SV, ST, SS, SG = s.S["v"], s.S["t"], s.S["s"], s.S["g"]
        qxg = [s.ctr("q_xg0"), s.ctr("q_xg1")]
        qkt = [s.ctr(f"q_kt{i}") for i in range(4)]
        qfsi = [s.ctr("q_fsi0"), s.ctr("q_fsi1")]
        qfso = [s.ctr("q_fso0"), s.ctr("q_fso1")]
        qtt = [s.ctr(f"q_tt{i}") for i in range(16)]
        qgd = s.ctr("q_gd")
        p = sg % 2

        def xg_sy(eng, w, sg=sg, p=p):
            if sg >= 1:
                w(ev[f"pjmm_{sg-1}_2_7"])
                w(ev[f"smmm_{sg-1}"])
            else:
                w(ev[f"hgel_{8*32-1}"]); w(ev["eomm_7_1_1"])
                w(ev["eost_7_0"]); w(ev["eost_7_1"])
            for ct in range(GROUP):
                w(ev[f"xhst_{sg * GROUP + ct}"])
            for ct in range(GROUP):
                t = sg * GROUP + ct
                xr = s.xh_dram.ap()[t, :].rearrange("(p c) -> p c", p=128)
                for kc in range(8):
                    s.dma(eng, qxg[p], s.xh_g[:, kc * 512 + 128 * ct: kc * 512 + 128 * (ct + 1)],
                          xr[:, 128 * kc:128 * (kc + 1)])
            ev[f"xg_{sg}"] = qxg[p].evt()
        self.on("sy", xg_sy)

        if g == 0:
            def tz_v(eng, w, sg=sg):
                if sg >= 2:
                    w(ev[f"cvd_{(sg-1)*24 + 23}"])
                else:
                    w(ev[f"hgel_{8*32-1}"])
                inst = eng.memset(s.tails[:, :], 0.0)
                ev[f"tz_{sg}"] = s.step(SV, inst)
            self.on("v", tz_v)

        WS = [s.WqTs, s.WkTs, s.WvTs]
        CW = [s.convw["wq_c"], s.convw["wk_c"], s.convw["wv_c"]]
        DST = [s.q_fm, s.k_fm, s.v_fm]
        for wi_ in range(3):
            for mc in range(8):
                bidx = sg * 24 + wi_ * 8 + mc
                reg = 4 if bidx % 2 == 0 else 7
                rp = bidx % 2

                def pj_t(eng, w, sg=sg, wi_=wi_, mc=mc, reg=reg, p=p):
                    w(ev[f"xg_{sg}"]); w(ev["wat"])
                    if pbev[reg] is not None:
                        w(pbev[reg])
                    for kc in range(8):
                        inst = eng.matmul(s.pb[reg][0:128, 0:512],
                                          WS[wi_][:, kc * D + 128 * mc:kc * D + 128 * (mc + 1)],
                                          s.xh_g[:, kc * 512:(kc + 1) * 512],
                                          start=(kc == 0), stop=(kc == 7))
                    ev[f"pjmm_{sg}_{wi_}_{mc}"] = s.step(ST, inst)
                self.on("t", pj_t)

                def pj_s(eng, w, sg=sg, wi_=wi_, mc=mc, reg=reg, rp=rp, bidx=bidx):
                    w(ev[f"pjmm_{sg}_{wi_}_{mc}"])
                    if bidx >= 2:
                        w(ev[f"cvd_{bidx-2}"])
                    if sg % NGROUP == 0:
                        w(ev[f"tz_{sg}"])
                    eng.copy(out=s.praw[rp][:, 0:3],
                             in_=s.tails[:, (wi_ * 8 + mc) * 3:(wi_ * 8 + mc) * 3 + 3])
                    inst = eng.copy(out=s.praw[rp][:, 3:515], in_=s.pb[reg][0:128, 0:512])
                    ev[f"pjev_{sg}_{wi_}_{mc}"] = s.step(SS, inst)
                self.on("s", pj_s)

                eng_n = "v"

                def pj_c(eng, w, sg=sg, wi_=wi_, mc=mc, rp=rp, bidx=bidx):
                    w(ev[f"pjev_{sg}_{wi_}_{mc}"])
                    cw = CW[wi_]
                    eng.tensor_scalar(out=s.cvt[rp][:, :], in0=s.praw[rp][:, 0:512],
                                      scalar1=cw[:, 4 * mc:4 * mc + 1], scalar2=None, op0=ALUc.mult)
                    for j in range(1, 4):
                        eng.scalar_tensor_tensor(out=s.cvt[rp][:, :], in0=s.praw[rp][:, j:512 + j],
                                                 scalar=cw[:, 4 * mc + j:4 * mc + j + 1],
                                                 in1=s.cvt[rp][:, :], op0=ALUc.mult, op1=ALUc.add)
                    ev[f"cv_{sg}_{wi_}_{mc}"] = s.step(s.S["v"],
                                                       eng.tensor_copy(out=s.cvt[rp][:, 0:1],
                                                                       in_=s.cvt[rp][:, 0:1]))
                self.on(eng_n, pj_c)

                def pj_sil(eng, w, sg=sg, wi_=wi_, mc=mc, rp=rp, bidx=bidx):
                    w(ev[f"cv_{sg}_{wi_}_{mc}"])
                    if sg >= 1:
                        if wi_ == 2:
                            w(ev[f"emat_{sg-1}_{3}"]); w(ev[f"emat_{sg-1}_{2}"])
                        else:
                            w(ev[f"omm_{sg-1}_{15}"])
                    eng.copy(out=s.tails[:, (wi_ * 8 + mc) * 3:(wi_ * 8 + mc) * 3 + 3],
                             in_=s.praw[rp][:, 512:515])
                    inst = eng.activation(out=DST[wi_][:, 512 * mc:512 * (mc + 1)],
                                          in_=s.cvt[rp][:, :], func=AF.Silu)
                    ev[f"cvd_{bidx}"] = s.step(SS, inst)
                self.on("s", pj_sil)
                pbev[reg] = ev[f"pjev_{sg}_{wi_}_{mc}"]

        def sm_t(eng, w, sg=sg, p=p):
            w(ev[f"xg_{sg}"]); w(ev["wsm"]); w(ev["consts"])
            if pbev[6] is not None:
                w(pbev[6])
            for kc in range(8):
                inst = eng.matmul(s.pb[6][0:12, 0:512], s.WsmTs[:, kc * 12:(kc + 1) * 12],
                                  s.xh_g[:, kc * 512:(kc + 1) * 512],
                                  start=(kc == 0), stop=(kc == 7))
            ev[f"smmm_{sg}"] = s.step(ST, inst)
        self.on("t", sm_t)

        def sm_s(eng, w, sg=sg):
            w(ev[f"smmm_{sg}"])
            if sg >= 1:
                w(ev[f"smcmm_{sg-1}_{3}"]); w(ev[f"smb_{sg-1}"])
            inst = eng.activation(out=s.smrow[:, :], in_=s.pb[6][0:12, 0:512], func=AF.Sigmoid,
                                  bias=s.mixbS[0:12, 0:1])
            ev[f"smsig_{sg}"] = s.step(SS, inst)
        self.on("s", sm_s)
        pbev[6] = ev[f"smsig_{sg}"]

        def sm_v(eng, w, sg=sg):
            w(ev[f"smsig_{sg}"])
            if sg >= 1:
                w(ev[f"gdec_{sg-1}"])
            inst = eng.tensor_copy(out=s.smrowb[:, :], in_=s.smrow[:, :])
            ev[f"smb_{sg}"] = s.step(SV, inst)
        self.on("v", sm_v)

        def sm_gd(eng, w, sg=sg, p=p):
            w(ev[f"smb_{sg}"])
            if sg >= 1:
                w(ev[f"emg_{sg-1}_{3}"])
            for h in range(4):
                s.dma(eng, s.ctr("q_gd"), s.gdec[0:1, 512 * h:512 * (h + 1)],
                      s.smrowb[4 + h:5 + h, 0:512])
            ev[f"gdec_{sg}"] = s.ctr("q_gd").evt()
        self.on("sy", sm_gd)

        for ct in range(GROUP):
            cp = ct % 2

            def kt_sc(eng, w, sg=sg, ct=ct):
                w(ev[f"cvd_{sg*24 + 15}"])
                w(ev[f"cvd_{sg*24 + 23}"])
                if sg >= 1:
                    w(ev[f"supd_{sg-1}_{GROUP-1}_3"])
                    w(ev[f"l2e_{sg-1}_{ct}"])
                    w(ev[f"bv_{sg-1}_{4*ct+3}"])
                    w(ev[f"smm_{sg-1}_{4*ct+3}"])
                    if ct >= 2:
                        w(ev[f"ema_{sg-1}_{1}"]); w(ev[f"ema_{sg-1}_{3}"])
                for b in range(8):
                    s.dma(eng, qkt[ct], s.k_tm[:, ct * 1024 + 128 * b: ct * 1024 + 128 * (b + 1)],
                          s.k_fm[:, 512 * b + 128 * ct: 512 * b + 128 * (ct + 1)], transpose=True)
                    s.dma(eng, qkt[ct], s.v_tm[:, ct * 1024 + 128 * b: ct * 1024 + 128 * (b + 1)],
                          s.v_fm[:, 512 * b + 128 * ct: 512 * b + 128 * (ct + 1)], transpose=True)
                ev[f"ktv_{sg}_{ct}"] = qkt[ct].evt()
            self.on("s", kt_sc)

            def smc_t(eng, w, sg=sg, ct=ct):
                w(ev[f"smsig_{sg}"])
                if pbev[6] is not None:
                    w(pbev[6])
                inst = eng.matmul(s.pb[6][0:128, 0:12], s.smrow[:, 128 * ct:128 * (ct + 1)],
                                  s.ident[0:12, 0:12], start=True, stop=True)
                ev[f"smcmm_{sg}_{ct}"] = s.step(ST, inst)
            self.on("t", smc_t)

            def smc_v(eng, w, sg=sg, ct=ct):
                w(ev[f"smcmm_{sg}_{ct}"])
                inst = eng.tensor_copy(out=s.smcol[:, 12 * ct:12 * (ct + 1)], in_=s.pb[6][0:128, 0:12])
                ev[f"smc_{sg}_{ct}"] = s.step(SV, inst)
            self.on("v", smc_v)
            pbev[6] = ev[f"smc_{sg}_{ct}"]

            def q2_s(eng, w, sg=sg, ct=ct):
                w(ev[f"cvd_{sg*24 + 7}"])
                tg_ = sg * GROUP + ct
                if tg_ >= 1:
                    w(ev[f"l2e_{(tg_-1)//GROUP}_{(tg_-1)%GROUP}"])
                inst = eng.activation(out=s.kq2[:, :],
                                      in_=s.q_fm.rearrange("p (b c) -> p b c", b=8)
                                      [:, :, 128 * ct:128 * (ct + 1)],
                                      func=AF.Square)
                ev[f"q2_{sg}_{ct}"] = s.step(SS, inst)
            self.on("s", q2_s)

            def l2_t(eng, w, sg=sg, ct=ct):
                w(ev[f"q2_{sg}_{ct}"])
                if pbev[5] is not None:
                    w(pbev[5])
                for h in range(4):
                    for bb in range(2):
                        inst = eng.matmul(s.pb[5][0:1, 128 * h:128 * (h + 1)], s.onesc[:, 0:1],
                                          s.kq2[:, 128 * (2 * h + bb):128 * (2 * h + bb + 1)],
                                          start=(h == 0 and bb == 0), stop=(h == 3 and bb == 1))
                ev[f"l2mm_{sg}_{ct}"] = s.step(ST, inst)
            self.on("t", l2_t)

            def l2_v(eng, w, sg=sg, ct=ct):
                w(ev[f"l2mm_{sg}_{ct}"])
                inst = eng.tensor_copy(out=s.l2r[0:1, :], in_=s.pb[5][0:1, 0:512])
                ev[f"l2r_{sg}_{ct}"] = s.step(SV, inst)
            self.on("v", l2_v)
            pbev[5] = ev[f"l2r_{sg}_{ct}"]

            def l2c_t(eng, w, sg=sg, ct=ct):
                w(ev[f"l2r_{sg}_{ct}"])
                for h in range(4):
                    inst = eng.matmul(s.pb[5][0:128, h:h + 1], s.l2r[0:1, 128 * h:128 * (h + 1)],
                                      s.ident[0:1, 0:1], start=(h == 0), stop=(h == 3))
                ev[f"l2cmm_{sg}_{ct}"] = s.step(ST, inst)
            self.on("t", l2c_t)

            def kl2_s(eng, w, sg=sg, ct=ct):
                w(ev[f"ktv_{sg}_{ct}"])
                for h in range(4):
                    inst = eng.activation(out=s.sqs[:, :],
                                          in_=s.k_tm[:, ct * 1024 + 256 * h:ct * 1024 + 256 * (h + 1)],
                                          func=AF.Square, accum_out=s.t_s[1][:, h:h + 1])
                ev[f"kl2_{sg}_{ct}"] = s.step(SS, inst)
            self.on("s", kl2_s)

            def col_v(eng, w, sg=sg, ct=ct):
                w(ev[f"l2cmm_{sg}_{ct}"]); w(ev[f"smc_{sg}_{ct}"])
                eng.tensor_scalar(out=s.t_s[0][:, 0:4], in0=s.pb[5][0:128, 0:4],
                                  scalar1=EPS_L2, scalar2=None, op0=ALUc.add)
                eng.drain()
                inst = eng.tensor_copy(out=s.t_s[2][:, 0:4], in_=s.t_s[0][:, 0:4])
                ev[f"l2e_{sg}_{ct}"] = s.step(SV, inst)
            self.on("v", col_v)
            pbev[5] = ev[f"l2e_{sg}_{ct}"]

            def col_s(eng, w, sg=sg, ct=ct):
                w(ev[f"l2e_{sg}_{ct}"]); w(ev[f"kl2_{sg}_{ct}"]); w(ev["ttinit"])
                eng.activation(out=s.t_s[3][:, 0:4], in_=s.t_s[0][:, 0:4], func=AF.Sqrt)
                inst = eng.activation(out=s.t_s[3][:, 4:8], in_=s.t_s[1][:, 0:4], func=AF.Sqrt,
                                      bias=s.t_zero[:, 120:121])
                ev[f"l2s_{sg}_{ct}"] = s.step(SS, inst)
            self.on("s", col_s)

            def col_v2(eng, w, sg=sg, ct=ct):
                w(ev[f"l2s_{sg}_{ct}"])
                co = s.colsc
                o = 16 * ct
                eng.reciprocal(out=co[:, o + 4:o + 8], in_=s.t_s[3][:, 0:4])
                eng.drain()
                eng.reciprocal(out=co[:, o + 0:o + 4], in_=s.t_s[3][:, 4:8])
                eng.drain()
                eng.tensor_tensor(out=co[:, o + 12:o + 16], in0=s.smcol[:, 12 * ct:12 * ct + 4],
                                  in1=co[:, o + 0:o + 4], op=ALUc.mult)
                eng.drain()
                eng.tensor_tensor(out=co[:, o + 8:o + 12], in0=co[:, o + 12:o + 16],
                                  in1=co[:, o + 0:o + 4], op=ALUc.mult)
                eng.drain()
                eng.tensor_scalar(out=co[:, o + 8:o + 12], in0=co[:, o + 8:o + 12],
                                  scalar1=-1.0, scalar2=None, op0=ALUc.mult)
                eng.drain()
                inst = eng.tensor_copy(out=co[:, o:o + 1], in_=co[:, o:o + 1])
                ev[f"colsdone_{sg}_{ct}"] = s.step(SV, inst)
            self.on("v", col_v2)

            for h in range(4):
                chi = ct * 4 + h

                def amm_t(eng, w, sg=sg, ct=ct, h=h, chi=chi):
                    w(ev[f"cvd_{sg*24 + 15}"])
                    if pbev[4] is not None:
                        w(pbev[4])
                    for bb in range(2):
                        sl_ = slice(512 * (2 * h + bb) + 128 * ct, 512 * (2 * h + bb) + 128 * (ct + 1))
                        inst = eng.matmul(s.pb[4][0:128, 0:128], s.k_fm[:, sl_], s.k_fm[:, sl_],
                                          start=(bb == 0), stop=(bb == 1))
                    ev[f"amm_{sg}_{chi}"] = s.step(ST, inst)
                self.on("t", amm_t)

                def aev_v(eng, w, sg=sg, ct=ct, h=h, chi=chi):
                    w(ev[f"amm_{sg}_{chi}"]); w(ev[f"colsdone_{sg}_{ct}"])
                    if chi > 0:
                        w(ev[f"mi_{sg}_{chi-1}"])
                    o = 16 * ct
                    eng.scalar_tensor_tensor(out=s.Aoff[:, 128 * chi:128 * (chi + 1)],
                                             in0=s.pb[4][0:128, 0:128],
                                             scalar=s.colsc[:, o + 8 + h:o + 9 + h],
                                             in1=s.mask_off[:, :], op0=ALUc.mult, op1=ALUc.mult)
                    eng.drain()
                    inst = eng.scalar_tensor_tensor(out=s.Abd[:, :],
                                                    in0=s.pb[4][0:128, 0:128],
                                                    scalar=s.colsc[:, o + 8 + h:o + 9 + h],
                                                    in1=s.mask_bdsl[:, :], op0=ALUc.mult, op1=ALUc.mult)
                    ev[f"aev_{sg}_{chi}"] = s.step(SV, inst)
                self.on("v", aev_v)
                pbev[4] = ev[f"aev_{sg}_{chi}"]

                def mi_sy(eng, w, sg=sg, chi=chi):
                    w(ev[f"aev_{sg}_{chi}"])
                    p2_ = sg % 2
                    if sg >= 1:
                        w(ev[f"mo_{sg-1}_{15}"]); w(ev[f"mo_{sg-1}_{14}"])
                    for b in range(4):
                        s.dma(eng, qfsi[p2_],
                              s.fs_t[4 * chi + b:4 * chi + b + 1, 0:1024],
                              s.Abd[32 * b:32 * (b + 1), 32 * b:32 * b + 32])
                    ev[f"mi_{sg}_{chi}"] = qfsi[p2_].evt()
                self.on("sy", mi_sy)

        def fs_v(eng, w, sg=sg):
            w(ev[f"mi_{sg}_{15}"])
            X = s.fs_t
            for j in range(0, 31):
                eng.drain()
                for i in range(j + 1, 32):
                    eng.scalar_tensor_tensor(out=X[0:64, 32 * i:32 * (i + 1)],
                                             in0=X[0:64, 32 * j:32 * (j + 1)],
                                             scalar=X[0:64, 32 * i + j:32 * i + j + 1],
                                             in1=X[0:64, 32 * i:32 * (i + 1)],
                                             op0=ALUc.mult, op1=ALUc.add)
            eng.drain()
            inst = eng.tensor_copy(out=X[0:64, 0:1], in_=X[0:64, 0:1])
            ev[f"fsv_{sg}"] = s.step(SV, inst)
        self.on("v", fs_v)

        for chi in range(16):
            tp = chi % 2

            def mo_g(eng, w, sg=sg, chi=chi, tp=tp):
                w(ev[f"fsv_{sg}"])
                p2_ = sg % 2
                if chi >= 2:
                    w(ev[f"wt_{sg}_{chi-2}"])
                elif sg >= 1:
                    w(ev[f"wt_{sg-1}_{14 + chi}"])
                else:
                    w(ev["ttinit"])
                for b in range(4):
                    s.dma(eng, qfso[p2_],
                          s.Tt[tp][32 * b:32 * (b + 1), 32 * b:32 * b + 32],
                          s.fs_t[4 * chi + b:4 * chi + b + 1, 0:1024])
                ev[f"mo_{sg}_{chi}"] = qfso[p2_].evt()
            self.on("g", mo_g)

            def tta_g(eng, w, sg=sg, chi=chi, tp=tp):
                w(ev[f"mo_{sg}_{chi}"])
                inst = eng.tensor_tensor(out=s.Tt[tp][:, :], in0=s.Tt[tp][:, :],
                                         in1=s.identb[:, :], op=ALUc.add)
                ev[f"ttpre_{sg}_{chi}"] = s.step(SG, inst)
            self.on("g", tta_g)

            def tt_sc(eng, w, sg=sg, chi=chi, tp=tp):
                w(ev[f"ttpre_{sg}_{chi}"])
                if sg >= 1:
                    w(ev[f"g0_{sg-1}_{chi}"])
                s.dma(eng, qtt[chi], s.Wt[:, 128 * chi:128 * (chi + 1)], s.Tt[tp][:, :],
                      transpose=True)
                ev[f"wt_{sg}_{chi}"] = qtt[chi].evt()
            self.on("s", tt_sc)

    def _attn_scan(self, sg, sq, g, pbev):
        s = self
        ev = s.ev
        ALUc = ALU
        SV, ST, SS, SG = s.S["v"], s.S["t"], s.S["s"], s.S["g"]
        qoe = [s.ctr("q_oe0"), s.ctr("q_oe1")]
        qxr = [s.ctr("q_xr0"), s.ctr("q_xr1")]
        qos = [s.ctr("q_os0"), s.ctr("q_os1")]
        qge = [s.ctr("q_ge0"), s.ctr("q_ge1")]
        qgd = s.ctr("q_gd")

        for h in range(4):
            def ema_t(eng, w, sg=sg, h=h):
                w(ev[f"gdec_{sg}"])
                if pbev[6] is not None:
                    w(pbev[6])
                inst = eng.matmul(s.pb[6][0:128, 0:512], s.onesr[0:1, 0:128],
                                  s.gdec[0:1, 512 * h:512 * (h + 1)], start=True, stop=True)
                ev[f"emg_{sg}_{h}"] = s.step(ST, inst)
            self.on("t", ema_t)

            eng_n = "v"
            SE = SV
            pe = 0

            def ema_c(eng, w, sg=sg, h=h, pe=pe):
                w(ev[f"emg_{sg}_{h}"])
                inst = eng.tensor_copy(out=s.gbc[pe][:, :], in_=s.pb[6][0:128, 0:512])
                ev[f"gcp_{sg}_{h}"] = s.step(SV, inst)
            self.on("v", ema_c)
            pbev[6] = ev[f"gcp_{sg}_{h}"]

            def ema_e(eng, w, sg=sg, h=h, SE=SE, pe=pe):
                w(ev[f"gcp_{sg}_{h}"])
                w(ev[f"cvd_{sg*24 + 23}"])
                w(ev[f"ktv_{sg}_{2}"]); w(ev[f"ktv_{sg}_{3}"])
                eng.drain()
                for bb in range(2):
                    b = 2 * h + bb
                    vsl = s.v_fm[:, 512 * b:512 * (b + 1)]
                    eng.tensor_tensor(out=s.d1b[pe][:, :], in0=s.gbc[pe][:, :], in1=vsl,
                                      op=ALUc.mult)
                    eng.drain()
                    eng.tensor_tensor(out=s.d1b[pe][:, :], in0=vsl, in1=s.d1b[pe][:, :],
                                      op=ALUc.subtract)
                    eng.drain()
                    if g == 0:
                        eng.memset(s.ema_st[:, b:b + 1], 0.0)
                        eng.drain()
                    eng.tensor_tensor_scan(out=vsl, data0=s.gbc[pe][:, :], data1=s.d1b[pe][:, :],
                                           initial=s.ema_st[:, b:b + 1],
                                           op0=ALUc.mult, op1=ALUc.add)
                    eng.drain()
                    eng.tensor_copy(out=s.ema_st[:, b:b + 1], in_=vsl[:, 511:512])
                    eng.drain()
                inst = eng.tensor_copy(out=s.gbc[pe][:, 0:1], in_=s.gbc[pe][:, 0:1])
                ev[f"ema_{sg}_{h}"] = s.step(SE, inst)
            self.on(eng_n, ema_e)

        for ct in range(GROUP):
            first_chunk = (g == 0 and ct == 0)
            o = 16 * ct
            tglob = sg * GROUP + ct
            cp = ct % 2

            for h in range(4):
                chi = ct * 4 + h

                def jt_t(eng, w, sg=sg, ct=ct, h=h, chi=chi):
                    w(ev[f"wt_{sg}_{chi}"])
                    if pbev[4] is not None:
                        w(pbev[4])
                    inst = eng.matmul(s.pb[4][0:128, 0:128], s.Aoff[:, 128 * chi:128 * (chi + 1)],
                                      s.Wt[:, 128 * chi:128 * (chi + 1)], start=True, stop=True)
                    ev[f"jtmm_{sg}_{chi}"] = s.step(ST, inst)
                    ev[f"jt_{sg}_{chi}"] = ev[f"jtmm_{sg}_{chi}"]
                self.on("t", jt_t)

                def jt_v(eng, w, sg=sg, chi=chi):
                    w(ev[f"jtmm_{sg}_{chi}"])
                    inst = eng.tensor_copy(out=s.JT_sb[:, :], in_=s.pb[4][0:128, 0:128])
                    ev[f"jtev_{sg}_{chi}"] = s.step(SV, inst)
                self.on("v", jt_v)
                pbev[4] = ev[f"jtev_{sg}_{chi}"]

                def bv_v(eng, w, sg=sg, ct=ct, h=h, chi=chi, o=o):
                    w(ev[f"ktv_{sg}_{ct}"]); w(ev[f"colsdone_{sg}_{ct}"])
                    inst = eng.tensor_scalar(out=s.bv_sb[:, :],
                                             in0=s.v_tm[:, ct * 1024 + 256 * h:ct * 1024 + 256 * (h + 1)],
                                             scalar1=s.colsc[:, o + 12 + h:o + 13 + h],
                                             scalar2=None, op0=ALUc.mult)
                    ev[f"bv_{sg}_{chi}"] = s.step(SV, inst)
                self.on("v", bv_v)

                def g_t(eng, w, sg=sg, ct=ct, h=h, chi=chi, first_chunk=first_chunk):
                    if pbev[5] is not None:
                        w(pbev[5])
                    if not first_chunk:
                        w(ev[f"sev_{sg}_{ct-1}_{h}" if ct > 0 else f"sev_{sg-1}_{GROUP-1}_{h}"])
                        for bb in range(2):
                            sl_ = slice(512 * (2 * h + bb) + 128 * ct,
                                        512 * (2 * h + bb) + 128 * (ct + 1))
                            inst = eng.matmul(s.pb[5][0:128, 0:256], s.k_fm[:, sl_],
                                              s.S_sb[:, 512 * h + 256 * bb:512 * h + 256 * (bb + 1)],
                                              start=(bb == 0), stop=(bb == 1))
                    else:
                        inst = eng.matmul(s.pb[5][0:128, 0:256], s.zerob[0:1, 0:128],
                                          s.zerob[0:1, 0:256], start=True, stop=True)
                    ev[f"gmm_{sg}_{chi}"] = s.step(ST, inst)
                self.on("t", g_t)

                def g_v(eng, w, sg=sg, ct=ct, h=h, chi=chi, o=o):
                    w(ev[f"gmm_{sg}_{chi}"]); w(ev[f"bv_{sg}_{chi}"])
                    inst = eng.scalar_tensor_tensor(out=s.G_sb[:, :], in0=s.pb[5][0:128, 0:256],
                                                    scalar=s.colsc[:, o + 8 + h:o + 9 + h],
                                                    in1=s.bv_sb[:, :], op0=ALUc.mult, op1=ALUc.add)
                    ev[f"gev_{sg}_{chi}"] = s.step(SV, inst)
                self.on("v", g_v)
                pbev[5] = ev[f"gev_{sg}_{chi}"]

                def u_t0(eng, w, sg=sg, chi=chi):
                    w(ev[f"gev_{sg}_{chi}"])
                    if pbev[6] is not None:
                        w(pbev[6])
                    inst = eng.matmul(s.pb[6][0:128, 0:256], s.Wt[:, 128 * chi:128 * (chi + 1)],
                                      s.G_sb[:, :], start=True, stop=True)
                    ev[f"g0mm_{sg}_{chi}"] = s.step(ST, inst)
                    ev[f"g0_{sg}_{chi}"] = ev[f"g0mm_{sg}_{chi}"]
                self.on("t", u_t0)

                def u_v0(eng, w, sg=sg, chi=chi):
                    w(ev[f"g0mm_{sg}_{chi}"])
                    inst = eng.tensor_copy(out=s.g0_sb[:, :], in_=s.pb[6][0:128, 0:256])
                    ev[f"g0ev_{sg}_{chi}"] = s.step(SV, inst)
                self.on("v", u_v0)

                def u_t1(eng, w, sg=sg, chi=chi):
                    w(ev[f"g0ev_{sg}_{chi}"]); w(ev[f"jtev_{sg}_{chi}"])
                    inst = eng.matmul(s.pb[6][0:128, 256:512], s.JT_sb[:, :], s.g0_sb[:, :],
                                      start=True, stop=True)
                    ev[f"h1mm_{sg}_{chi}"] = s.step(ST, inst)
                self.on("t", u_t1)

                def u_v1(eng, w, sg=sg, chi=chi):
                    w(ev[f"h1mm_{sg}_{chi}"])
                    inst = eng.tensor_tensor(out=s.h1_sb[:, :], in0=s.pb[6][0:128, 256:512],
                                             in1=s.g0_sb[:, :], op=ALUc.add)
                    ev[f"h1ev_{sg}_{chi}"] = s.step(SV, inst)
                self.on("v", u_v1)

                def u_t2(eng, w, sg=sg, chi=chi):
                    w(ev[f"h1ev_{sg}_{chi}"])
                    inst = eng.matmul(s.pb[6][0:128, 0:256], s.JT_sb[:, :], s.h1_sb[:, :],
                                      start=True, stop=True)
                    ev[f"h2mm_{sg}_{chi}"] = s.step(ST, inst)
                self.on("t", u_t2)

                def u_v2(eng, w, sg=sg, chi=chi):
                    w(ev[f"h2mm_{sg}_{chi}"])
                    inst = eng.tensor_tensor(out=s.h1_sb[:, :], in0=s.pb[6][0:128, 0:256],
                                             in1=s.g0_sb[:, :], op=ALUc.add)
                    ev[f"h2ev_{sg}_{chi}"] = s.step(SV, inst)
                self.on("v", u_v2)

                def u_t3(eng, w, sg=sg, chi=chi):
                    w(ev[f"h2ev_{sg}_{chi}"])
                    inst = eng.matmul(s.pb[6][0:128, 256:512], s.JT_sb[:, :], s.h1_sb[:, :],
                                      start=True, stop=True)
                    ev[f"u3mm_{sg}_{chi}"] = s.step(ST, inst)
                self.on("t", u_t3)

                def u_v3(eng, w, sg=sg, chi=chi):
                    w(ev[f"u3mm_{sg}_{chi}"])
                    inst = eng.tensor_tensor(out=s.u2_sb[:, :], in0=s.pb[6][0:128, 256:512],
                                             in1=s.g0_sb[:, :], op=ALUc.add)
                    ev[f"uev_{sg}_{chi}"] = s.step(SV, inst)
                self.on("v", u_v3)
                pbev[6] = ev[f"uev_{sg}_{chi}"]

                def at_t(eng, w, sg=sg, ct=ct, h=h, chi=chi):
                    w(ev[f"cvd_{sg*24 + 7}"]); w(ev[f"cvd_{sg*24 + 15}"])
                    if pbev[4] is not None:
                        w(pbev[4])
                    for bb in range(2):
                        sl_ = slice(512 * (2 * h + bb) + 128 * ct, 512 * (2 * h + bb) + 128 * (ct + 1))
                        inst = eng.matmul(s.pb[4][0:128, 0:128], s.k_fm[:, sl_], s.q_fm[:, sl_],
                                          start=(bb == 0), stop=(bb == 1))
                    ev[f"atmm_{sg}_{chi}"] = s.step(ST, inst)
                self.on("t", at_t)

                def at_v(eng, w, sg=sg, ct=ct, h=h, chi=chi, o=o):
                    w(ev[f"atmm_{sg}_{chi}"])
                    inst = eng.tensor_tensor(out=s.aT_sb[:, :], in0=s.pb[4][0:128, 0:128],
                                             in1=s.mask_ui[:, :], op=ALUc.mult)
                    ev[f"atev_{sg}_{chi}"] = s.step(SV, inst)
                self.on("v", at_v)
                pbev[4] = ev[f"atev_{sg}_{chi}"]

                def o_t(eng, w, sg=sg, ct=ct, h=h, chi=chi, first_chunk=first_chunk):
                    w(ev[f"uev_{sg}_{chi}"]); w(ev[f"atev_{sg}_{chi}"])
                    if pbev[7] is not None:
                        w(pbev[7])
                    if not first_chunk:
                        for bb in range(2):
                            sl_ = slice(512 * (2 * h + bb) + 128 * ct,
                                        512 * (2 * h + bb) + 128 * (ct + 1))
                            eng.matmul(s.pb[7][0:128, 0:256], s.q_fm[:, sl_],
                                       s.S_sb[:, 512 * h + 256 * bb:512 * h + 256 * (bb + 1)],
                                       start=(bb == 0), stop=False)
                        inst = eng.matmul(s.pb[7][0:128, 0:256], s.aT_sb[:, :], s.u2_sb[:, :],
                                          start=False, stop=True)
                    else:
                        inst = eng.matmul(s.pb[7][0:128, 0:256], s.aT_sb[:, :], s.u2_sb[:, :],
                                          start=True, stop=True)
                    ev[f"omm_{sg}_{chi}"] = s.step(ST, inst)
                self.on("t", o_t)

                def o_v(eng, w, sg=sg, ct=ct, h=h, chi=chi, o=o):
                    w(ev[f"omm_{sg}_{chi}"])
                    tg_ = sg * GROUP + ct
                    if tg_ >= 1:
                        w(ev[f"onft_{(tg_-1)//GROUP}_{(tg_-1)%GROUP}"])
                    inst = eng.tensor_scalar(out=s.o_tm[:, 256 * h:256 * (h + 1)],
                                             in0=s.pb[7][0:128, 0:256],
                                             scalar1=s.colsc[:, o + 4 + h:o + 5 + h],
                                             scalar2=None, op0=ALUc.mult)
                    ev[f"oev_{sg}_{chi}"] = s.step(SV, inst)
                self.on("v", o_v)
                pbev[7] = ev[f"oev_{sg}_{chi}"]

                def s_t(eng, w, sg=sg, ct=ct, h=h, chi=chi, first_chunk=first_chunk):
                    w(ev[f"uev_{sg}_{chi}"]); w(ev[f"ktv_{sg}_{ct}"])
                    if sq == 0 and g == 0 and ct == 0:
                        w(ev[f"hgel_{8*32-1}"])
                        w(ev["bcev_7"])
                    if s.pbS[h] is not None:
                        w(s.pbS[h])
                    for mm_ in range(2):
                        inst = eng.matmul(
                            s.pb[h][0:128, 256 * mm_:256 * (mm_ + 1)],
                            s.k_tm[:, ct * 1024 + 256 * h + 128 * mm_:
                                   ct * 1024 + 256 * h + 128 * (mm_ + 1)],
                            s.u2_sb[:, :],
                            start=(mm_ == 0), stop=(mm_ == 1))
                    ev[f"smm_{sg}_{chi}"] = s.step(ST, inst)
                self.on("t", s_t)

                def s_va(eng, w, sg=sg, ct=ct, h=h, chi=chi, first_chunk=first_chunk):
                    w(ev[f"smm_{sg}_{chi}"])
                    if first_chunk:
                        inst = eng.tensor_copy(out=s.S_f32[:, 512 * h:512 * (h + 1)],
                                               in_=s.pb[h][0:128, 0:512])
                    else:
                        inst = eng.tensor_tensor(out=s.S_f32[:, 512 * h:512 * (h + 1)],
                                                 in0=s.S_f32[:, 512 * h:512 * (h + 1)],
                                                 in1=s.pb[h][0:128, 0:512], op=ALUc.add)
                    ev[f"sadd_{sg}_{chi}"] = s.step(SV, inst)
                self.on("v", s_va)
                s.pbS[h] = ev[f"sadd_{sg}_{chi}"]

                def s_sv(eng, w, sg=sg, ct=ct, h=h, chi=chi):
                    w(ev[f"sadd_{sg}_{chi}"])
                    inst = eng.copy(out=s.S_sb[:, 512 * h:512 * (h + 1)],
                                    in_=s.S_f32[:, 512 * h:512 * (h + 1)])
                    ev[f"sev_{sg}_{ct}_{h}"] = s.step(SS, inst)
                self.on("s", s_sv)
                if ct == GROUP - 1 and h == 3:
                    ev[f"supd_{sg}_{GROUP-1}_3"] = ev[f"sev_{sg}_{ct}_{h}"]

            def ep_sc1(eng, w, sg=sg, ct=ct, cp=cp):
                for h in range(4):
                    w(ev[f"ema_{sg}_{h}"])
                w(ev[f"oev_{sg}_{ct*4+3}"])
                tg_ = sg * GROUP + ct
                if tg_ >= 1:
                    w(ev[f"cmb_{(tg_-1)//GROUP}_{(tg_-1)%GROUP}"])
                for b in range(8):
                    s.dma(eng, qoe[cp], s.ema_t[:, 128 * b:128 * (b + 1)],
                          s.v_fm[:, 512 * b + 128 * ct:512 * b + 128 * (ct + 1)], transpose=True)
                ev[f"emat_{sg}_{ct}"] = qoe[cp].evt()
            self.on("s", ep_sc1)

            def ep_v1(eng, w, sg=sg, ct=ct, o=o, tglob=tglob):
                w(ev[f"emat_{sg}_{ct}"])
                w(ev[f"oev_{sg}_{ct*4+3}"])
                eng.tensor_tensor(out=s.ema_t[:, :], in0=s.ema_t[:, :], in1=s.o_tm[:, :],
                                  op=ALUc.subtract)
                eng.drain()
                for h in range(4):
                    inst = eng.scalar_tensor_tensor(out=s.o_tm[:, 256 * h:256 * (h + 1)],
                                                    in0=s.ema_t[:, 256 * h:256 * (h + 1)],
                                                    scalar=s.smcol[:, 12 * ct + 8 + h:12 * ct + 9 + h],
                                                    in1=s.o_tm[:, 256 * h:256 * (h + 1)],
                                                    op0=ALUc.mult, op1=ALUc.add)
                    eng.drain()
                ev[f"cmb_{sg}_{ct}"] = s.step(SV, inst)
            self.on("v", ep_v1)

            def ep_s1(eng, w, sg=sg, ct=ct):
                w(ev[f"cmb_{sg}_{ct}"])
                for h in range(4):
                    eng.activation(out=s.sqs[:, :], in_=s.o_tm[:, 256 * h:256 * (h + 1)],
                                   func=AF.Square, scale=1.0 / 16.0,
                                   accum_out=s.t_s[4][:, h:h + 1])
                inst = eng.activation(out=s.t_s[5][:, 4:8], in_=s.t_s[4][:, 0:4], func=AF.Sqrt,
                                      bias=s.t_zero[:, 121:122])
                ev[f"rmssq_{sg}_{ct}"] = s.step(SS, inst)
            self.on("s", ep_s1)

            def ep_v2(eng, w, sg=sg, ct=ct):
                w(ev[f"rmssq_{sg}_{ct}"])
                eng.reciprocal(out=s.t_s[5][:, 0:4], in_=s.t_s[5][:, 4:8])
                eng.drain()
                for h in range(4):
                    eng.tensor_scalar(out=s.o_tm[:, 256 * h:256 * (h + 1)],
                                      in0=s.o_tm[:, 256 * h:256 * (h + 1)],
                                      scalar1=s.t_s[5][:, h:h + 1], scalar2=None, op0=ALUc.mult)
                    eng.drain()
                inst = eng.tensor_copy(out=s.o_tm[:, 0:1], in_=s.o_tm[:, 0:1])
                ev[f"onrm_{sg}_{ct}"] = s.step(SV, inst)
            self.on("v", ep_v2)

            def ep_sc2(eng, w, sg=sg, ct=ct, cp=cp):
                w(ev[f"onrm_{sg}_{ct}"])
                tg_ = sg * GROUP + ct
                if tg_ >= 1:
                    w(ev[f"womm_{(tg_-1)//GROUP}_{(tg_-1)%GROUP}"])
                for b in range(8):
                    s.dma(eng, qoe[cp], s.onf[:, 128 * b:128 * (b + 1)],
                          s.o_tm[:, 128 * b:128 * (b + 1)], transpose=True)
                ev[f"onft_{sg}_{ct}"] = qoe[cp].evt()
            self.on("s", ep_sc2)

            def ep_xr(eng, w, sg=sg, ct=ct, cp=cp, tglob=tglob):
                if sg * GROUP + ct >= 2:
                    w(ev[f"ost_{sg*GROUP+ct-2}"])
                else:
                    w(ev["eall"])
                ev[f"xr_{tglob}"] = s.dma(eng, qxr[cp], s.t_x[cp][:, :],
                                          s.x_own[128 * tglob:128 * (tglob + 1), :])
            self.on("sy", ep_xr)

            def ep_ge(eng, w, sg=sg, ct=ct, cp=cp, tglob=tglob):
                w(ev["eall"]); w(ev["gxi"])
                if sg * GROUP + ct >= 2:
                    w(ev[f"epev_{sg*GROUP+ct-2}"])
                ev[f"ge_{tglob}"] = s.idma(eng, qge[cp],
                                           out=s.gE[cp][:, :], out_offset=None,
                                           in_=s.cc_e_out.ap(),
                                           in_offset=IOoA(ap=s.r_gidx[:, tglob:tglob + 1], axis=0))
            self.on("g", ep_ge)

            def ep_v3(eng, w, sg=sg, ct=ct, cp=cp, tglob=tglob):
                w(ev[f"xr_{tglob}"]); w(ev[f"ge_{tglob}"]); w(ev["stats"])
                if tglob >= 1:
                    w(ev[f"womm_{(tglob-1)//GROUP}_{(tglob-1)%GROUP}"])
                eng.tensor_scalar(out=s.t_s[2][:, 4:5], in0=s.r_rs[:, tglob:tglob + 1],
                                  scalar1=1.0, scalar2=None, op0=ALUc.add)
                eng.tensor_tensor(out=s.t_s[2][:, 5:6], in0=s.r_mu[:, tglob:tglob + 1],
                                  in1=s.r_rs[:, tglob:tglob + 1], op=ALUc.mult)
                eng.drain()
                eng.tensor_scalar(out=s.t_s[2][:, 5:6], in0=s.t_s[2][:, 5:6],
                                  scalar1=-1.0, scalar2=None, op0=ALUc.mult)
                eng.drain()
                eng.tensor_scalar(out=s.t_a[:, :], in0=s.t_x[cp][:, :],
                                  scalar1=s.t_s[2][:, 4:5], scalar2=s.t_s[2][:, 5:6],
                                  op0=ALUc.mult, op1=ALUc.add)
                eng.drain()
                inst = eng.tensor_scalar(out=s.gE[cp][:, :], in0=s.gE[cp][:, :],
                                         scalar1=s.r_wgt[:, tglob:tglob + 1], scalar2=None,
                                         op0=ALUc.mult)
                ev[f"res_{sg}_{ct}"] = s.step(SV, inst)
            self.on("v", ep_v3)

            def ep_wo(eng, w, sg=sg, ct=ct, cp=cp, tglob=tglob):
                w(ev[f"onft_{sg}_{ct}"]); w(ev["wofold"]); w(ev[f"res_{sg}_{ct}"])
                if pbev[5] is not None:
                    w(pbev[5])
                if pbev[7] is not None:
                    w(pbev[7])
                for nn_ in range(2):
                    psl = s.pb[5 if nn_ == 0 else 7]
                    for kc in range(8):
                        eng.matmul(psl[0:128, 0:512], s.onf[:, 128 * kc:128 * (kc + 1)],
                                   s.WoTs[:, kc * D + 512 * nn_:kc * D + 512 * (nn_ + 1)],
                                   start=(kc == 0), stop=False)
                    eng.matmul(psl[0:128, 0:512], s.ident[:, :],
                               s.t_a[:, 512 * nn_:512 * (nn_ + 1)], start=False, stop=False)
                    inst = eng.matmul(psl[0:128, 0:512], s.identb[:, :],
                                      s.gE[cp][:, 512 * nn_:512 * (nn_ + 1)], start=False, stop=True)
                ev[f"womm_{sg}_{ct}"] = s.step(ST, inst)
            self.on("t", ep_wo)

            def ep_ev(eng, w, sg=sg, ct=ct, cp=cp, tglob=tglob):
                w(ev[f"womm_{sg}_{ct}"])
                eng.copy(out=s.t_x[cp][:, 0:512], in_=s.pb[5][0:128, 0:512])
                inst = eng.copy(out=s.t_x[cp][:, 512:1024], in_=s.pb[7][0:128, 0:512])
                ev[f"epev_{tglob}"] = s.step(SS, inst)
            self.on("s", ep_ev)
            pbev[5] = ev[f"epev_{tglob}"]
            pbev[7] = ev[f"epev_{tglob}"]

            def ep_st(eng, w, sg=sg, ct=ct, cp=cp, tglob=tglob):
                w(ev[f"epev_{tglob}"])
                ev[f"ost_{tglob}"] = s.dma(eng, qos[cp], s.o_out[128 * tglob:128 * (tglob + 1), :],
                                           s.t_x[cp][:, :])
            self.on("sy", ep_st)

    def _emit_combine(self):
        pass

    def _emit_taps_and_finish(self):
        s = self
        ev = s.ev
        taps = []
        if "stats" in self.taps:
            taps += [("r_mu", s.r_mu, [128, NTILE], F32, "stats"),
                     ("r_var", s.r_var, [128, NTILE], F32, "stats"),
                     ("r_rs", s.r_rs, [128, NTILE], F32, "stats"),
                     ("r_rs2", s.r_rs2, [128, NTILE], F32, "stats")]
        if "snap" in self.taps and self.stage >= 1:
            taps += [("snap_ps", s.snap_ps, [128, 8], F32, f"selr_{self.tsnap}"),
                     ("snap_oh", s.snap_oh, [128, 8], BF16, f"selr_{self.tsnap}"),
                     ("snap_l8", s.snap_l8, [128, 8], F32, f"selr_{self.tsnap}")]
        if "route" in self.taps and self.stage >= 1:
            taps += [("r_top1", s.r_top1, [128, NTILE], F32, f"selr_{NTILE-1}"),
                     ("r_selr", s.r_selr, [128, NTILE], F32, f"selr_{NTILE-1}"),
                     ("r_wgt", s.r_wgt, [128, NTILE], F32,
                      f"p2v_{NTILE-1}" if self.stage >= 2 else f"wgt_{NTILE-1}"),
                     ("cntrow", s.cntrow, [1, 8 * (NTILE + 1)], F32, "cntincl")]
        if "pass2" in self.taps and self.stage >= 2:
            if self.sub & 2:
                taps += [("r_rank", s.r_rank, [128, NTILE], F32, f"p2v_{NTILE-1}"),
                         ("r_keep", s.r_keep, [128, NTILE], F32, f"p2v_{NTILE-1}")]
            taps += [("cnt8", s.cnt8, [8, 8], F32, "cnt8")]
            if self.sub & 8:
                taps += [("slotid", s.slotid, [128, 16], F32, "slotid")]

        if "attn" in self.taps and self.stage >= 4:
            s_ = self
            ataps = [
                ("q_fm", s_.q_fm, [128, 4096], BF16, "cvd_191"),
                ("k_fm", s_.k_fm, [128, 4096], BF16, "cvd_191"),
                ("v_fm", s_.v_fm, [128, 4096], BF16, "ema_7_3"),
                ("k_tm", s_.k_tm, [128, 4096], BF16, "ktv_7_3"),
                ("v_tm", s_.v_tm, [128, 4096], BF16, "ktv_7_3"),
                ("smrow", s_.smrow, [12, 512], F32, "smsig_7"),
                ("smcol", s_.smcol, [128, 48], F32, "smc_7_3"),
                ("colsc", s_.colsc, [128, 64], F32, "colsdone_7_3"),
                ("Aoff", s_.Aoff, [128, 2048], BF16, "aev_7_15"),
                ("Wt", s_.Wt, [128, 2048], BF16, "wt_7_15"),
                ("fs_t", s_.fs_t, [128, 1024], F32, "fsv_7"),
                ("S_f32", s_.S_f32, [128, 2048], F32, "sadd_7_15"),
                ("G_sb", s_.G_sb, [128, 256], BF16, "gev_7_15"),
                ("g0_sb", s_.g0_sb, [128, 256], BF16, "g0ev_7_15"),
                ("u2_sb", s_.u2_sb, [128, 256], BF16, "uev_7_15"),
                ("aT_sb", s_.aT_sb, [128, 128], BF16, "atev_7_15"),
                ("o_tm", s_.o_tm, [128, 1024], BF16, "onrm_7_3"),
                ("onf", s_.onf, [128, 1024], BF16, "onft_7_3"),
                ("ema_st", s_.ema_st, [128, 8], F32, "ema_7_3"),
                ("gdec", s_.gdec, [1, 2048], BF16, "gdec_7"),
                ("t_a", s_.t_a, [128, 1024], F32, "res_7_3"),
            ]
            taps += ataps
        if "ffn" in self.taps and self.stage >= 3:
            def tap_eout(eng, w):
                w(ev["eall"])
                out = self.dout("tap_eall", [NEXP * CAP, D], BF16)
                ct = s.ctr("q_tapff")
                for i in range(16):
                    s.dma(eng, ct, out[1024 * i:1024 * (i + 1), :],
                          s.cc_e_out.ap()[1024 * i:1024 * (i + 1), :])
            self.on("sy", tap_eout)

        def tap_sy(eng, w):
            ct = s.ctr("q_tap")
            for name, src, shape, dt, evname in taps:
                w(ev[evname])
                out = self.dout("tap_" + name, shape, dt)
                s.dma(eng, ct, out[:, :], src[:, :])
            if self.stage < 4:
                w(ev.get("gwsplit") or (s.S["v"], 0))
                for t in range(NTILE):
                    s.dma(eng, ct, s.o_out[128 * t:128 * (t + 1), 0:129], s.t_zero[:, :])
            eng.wait_ge(ct.sem, ct.n)
            for qn, ctr in list(s._ctrs.items()):
                if qn.startswith("q_") and ctr.n > 0:
                    eng.wait_ge(ctr.sem, ctr.n)
        self.on("sy", tap_sy)


def build_mk(taps=(), stage=99, tsnap=2, sub=15):
    mk = MK(taps=taps, stage=stage, tsnap=tsnap, sub=sub)
    nc = mk.build()
    return mk, nc


from concourse.bass_utils import run_bass_kernel_spmd as _run_spmd

_CACHE = {}


def kernel(**inputs):
    inputs = {k: np.asarray(v) for k, v in inputs.items()}
    if "nc" not in _CACHE:
        mk, nc = build_mk(taps=(), stage=4)
        _CACHE["nc"] = nc
    nc = _CACHE["nc"]
    in_maps = [host_inputs_for_core(c, inputs) for c in range(NCORES)]
    res = _run_spmd(nc, in_maps, core_ids=list(range(NCORES)))
    out = np.concatenate([np.asarray(res.results[c]["out"]) for c in range(NCORES)], 0)
    return out.reshape(B, L, D).astype(np.float32)


# version 02
# speedup vs baseline: 1.0993x; 1.0993x over previous
import numpy as np
import ml_dtypes
from contextlib import ExitStack

import concourse.bass as bass
import concourse.mybir as mybir

F32 = mybir.dt.float32
BF16 = mybir.dt.bfloat16
U32 = mybir.dt.uint32
ALU = mybir.AluOpType
AF = mybir.ActivationFunctionType
AX = mybir.AxisListType
IOoA = bass.IndirectOffsetOnAxis

B, L, D = 16, 2048, 1024
NCORES = 8
SEQ_PER_CORE = 2
TOK_PER_CORE = SEQ_PER_CORE * L
NTILE = TOK_PER_CORE // 128
NEXP = 8
CAP = 2048
NH = 4
DK = 256
CHUNK = 128
NCHUNK = L // CHUNK
GROUP = 4
NGROUP = NCHUNK // GROUP
EPS_LN = 1e-5
EPS_L2 = 1e-6
EPS_RMS = 1e-5
IDBUF_N = NEXP * CAP + 128


def host_constants(core):
    c = {}
    r = np.arange(128)
    c["c_ident"] = np.eye(128, dtype=np.float32)
    c["c_identb"] = np.eye(128, dtype=ml_dtypes.bfloat16)
    c["c_iota8"] = np.tile(np.arange(8, dtype=np.float32), (128, 1)).copy()
    c["c_u1strict"] = np.triu(np.ones((128, 128)), 1).astype(ml_dtypes.bfloat16)
    c["c_onesc"] = np.ones((128, 1), dtype=ml_dtypes.bfloat16)
    c["c_onesr"] = np.ones((1, 512), dtype=ml_dtypes.bfloat16)
    c["c_mask_sl"] = (r[:, None] > r[None, :]).astype(np.float32)
    c["c_mask_ui"] = (r[:, None] <= r[None, :]).astype(np.float32)
    blk = r // 32
    c["c_mask_bdsl"] = ((blk[:, None] == blk[None, :]) & (r[:, None] > r[None, :])).astype(np.float32)
    c["c_mask_off"] = (blk[:, None] > blk[None, :]).astype(np.float32)
    c["c_corelt"] = (np.arange(8) < core).astype(np.float32).reshape(8, 1)
    c["c_tokid1"] = (core * TOK_PER_CORE + np.arange(TOK_PER_CORE, dtype=np.float32) + 1.0
                     ).reshape(NTILE, 128).T.copy()
    c["c_slotidx"] = (core * CAP + np.arange(CAP, dtype=np.uint32)).reshape(16, 128).T.copy()
    return c


def host_inputs_for_core(core, inp):
    x = np.ascontiguousarray(np.asarray(inp["L0_attn_hidden_states"], dtype=np.float32).reshape(B * L, D))
    d = {}
    d["x_own"] = np.ascontiguousarray(x[core * TOK_PER_CORE:(core + 1) * TOK_PER_CORE])
    d["x_full"] = x
    d["WqT"] = np.ascontiguousarray(np.asarray(inp["L0_attn_Wq"]).T)
    d["WkT"] = np.ascontiguousarray(np.asarray(inp["L0_attn_Wk"]).T)
    d["WvT"] = np.ascontiguousarray(np.asarray(inp["L0_attn_Wv"]).T)
    d["WoT"] = np.ascontiguousarray(np.asarray(inp["L0_attn_Wo"]).T)
    d["WsmT"] = np.ascontiguousarray(np.concatenate(
        [np.asarray(inp["L0_attn_Wb"]).T, np.asarray(inp["L0_attn_Wdec"]).T,
         np.asarray(inp["L0_attn_Wmix"]).T], axis=1))
    d["mixb"] = np.ascontiguousarray(np.concatenate(
        [np.zeros(8, np.float32), np.asarray(inp["L0_attn_mix_bias"], np.float32)]).reshape(12, 1))
    d["mixb_row"] = np.ascontiguousarray(
        np.concatenate([np.zeros(8, np.float32), np.asarray(inp["L0_attn_mix_bias"], np.float32)]).reshape(1, 12))
    for nm, key in (("wq_c", "L0_attn_wq_conv"), ("wk_c", "L0_attn_wk_conv"), ("wv_c", "L0_attn_wv_conv")):
        w = np.asarray(inp[key])
        d[nm] = np.ascontiguousarray(w.reshape(8, 128, 4).transpose(1, 0, 2).reshape(128, 32))
    d["onorm2"] = np.ascontiguousarray(np.asarray(inp["L0_attn_onorm_w"]).reshape(2, 128).T)
    d["gwT"] = np.ascontiguousarray(np.asarray(inp["L0_ffn_gate_weight"]).T)
    d["ln_g"] = np.ascontiguousarray(np.asarray(inp["L0_ffn_ln_gamma"]).reshape(8, 128).T)
    d["ln_b"] = np.ascontiguousarray(np.asarray(inp["L0_ffn_ln_beta"]).reshape(8, 128).T)
    d["wi_e"] = np.ascontiguousarray(np.asarray(inp["L0_ffn_wi"])[core])
    d["wo_e"] = np.ascontiguousarray(np.asarray(inp["L0_ffn_wo"])[core])
    d.update(host_constants(core))
    return d


class Ctr:
    def __init__(self, sem, name):
        self.sem = sem
        self.n = 0
        self.name = name

    def evt(self):
        return (self, self.n)


def _wait(eng, wstate, event):
    ctr, val = event
    if val <= 0:
        return
    if wstate.get(ctr.name, -1) >= val:
        return
    wstate[ctr.name] = val
    eng.wait_ge(ctr.sem, val)



def _ap_name(a):
    try:
        return a.tensor.name
    except Exception:
        return None


class DrainGuard:

    _READS = {
        "tensor_copy": ("in_",), "tensor_tensor": ("in0", "in1"),
        "tensor_scalar": ("in0", "scalar1", "scalar2"),
        "scalar_tensor_tensor": ("in0", "scalar", "in1"),
        "tensor_reduce": ("in_",), "tensor_tensor_scan": ("data0", "data1", "initial"),
        "reciprocal": ("in_",), "bn_stats": ("in_",), "bn_aggr": ("in_",),
        "max": ("in_",), "max_index": ("in_max", "in_values"),
        "activation": ("in_", "bias", "scale"), "copy": ("in_",),
        "add": ("in_",), "mul": ("in_",), "iota": (), "memset": (),
        "affine_select": ("in_",), "partition_broadcast": ("in_ap",),
        "partition_all_reduce": ("in_ap",),
    }
    _WRITES = {
        "tensor_copy": ("out",), "tensor_tensor": ("out",), "tensor_scalar": ("out", "accum_out"),
        "scalar_tensor_tensor": ("out", "accum_out"), "tensor_reduce": ("out",),
        "tensor_tensor_scan": ("out",), "reciprocal": ("out",), "bn_stats": ("out",),
        "bn_aggr": ("out",), "max": ("out",), "max_index": ("out",),
        "activation": ("out", "accum_out"), "copy": ("out",), "add": ("out",),
        "mul": ("out",), "iota": ("out",), "memset": ("ap",), "affine_select": ("out",),
        "partition_broadcast": ("out_ap",), "partition_all_reduce": ("out_ap",),
    }

    _SPECIAL = {"bn_stats", "bn_aggr", "max", "max_index", "reciprocal",
                "tensor_tensor_scan", "iota", "affine_select"}

    def __init__(self, eng):
        self._eng = eng
        self._w_ord = {}
        self._r_ord = {}
        self._ord = 0

    def drain(self):
        self._eng.drain()
        self._w_ord.clear()
        self._r_ord.clear()

    def __getattr__(self, name):
        eng = object.__getattribute__(self, "_eng")
        attr = getattr(eng, name)
        if name not in self._READS:
            return attr
        rkeys = self._READS[name]
        wkeys = self._WRITES[name]

        def wrapper(*args, **kw):
            import inspect as _i
            if args:
                try:
                    ba = _i.signature(attr).bind(*args, **kw)
                    allkw = dict(ba.arguments)
                except (TypeError, ValueError):
                    allkw = dict(kw)
            else:
                allkw = dict(kw)
            special = name in self._SPECIAL or allkw.get("accum_out") is not None \
                or "is_equal" in str(allkw.get("op0", "")) or "is_equal" in str(allkw.get("op", ""))
            rn = {n for k in rkeys for n in [_ap_name(allkw.get(k))] if n}
            wn = {n for k in wkeys for n in [_ap_name(allkw.get(k))] if n}
            cur = self._ord
            need = False

            def close(rec):
                if rec is None:
                    return False
                o_, sp_ = rec
                return special or sp_ or (cur - o_) <= 2

            for n in rn:
                if close(self._w_ord.get(n)):
                    need = True
            for n in wn:
                if close(self._w_ord.get(n)) or close(self._r_ord.get(n)):
                    need = True
            if need:
                self.drain()
            inst = attr(*args, **kw)
            self._ord += 1
            cur = self._ord
            for n in wn:
                self._w_ord[n] = (cur, special)
            for n in rn:
                self._r_ord[n] = (cur, special)
            return inst
        return wrapper


class MK:
    ENG = {"t": "tensor", "v": "vector", "s": "scalar", "g": "gpsimd", "sy": "sync"}

    def __init__(self, taps=(), stage=99, tsnap=2, sub=15):
        self.taps = set(taps)
        self.sub = sub
        self.stage = stage
        self.tsnap = tsnap
        self.nc = bass.Bass("TRN2", target_bir_lowering=False)
        self.es = ExitStack()
        self._ctrs = {}

    def din(self, name, shape, dt=F32):
        return self.nc.dram_tensor(name, list(shape), dt, kind="ExternalInput").ap()

    def dout(self, name, shape, dt=F32):
        return self.nc.dram_tensor(name, list(shape), dt, kind="ExternalOutput").ap()

    def sb(self, name, shape, dt=F32):
        return self.es.enter_context(self.nc.sbuf_tensor(name, list(shape), dt))

    def psum(self, name, shape, dt=F32):
        return self.es.enter_context(self.nc.psum_tensor(name, list(shape), dt))

    def ctr(self, name):
        if name not in self._ctrs:
            sem = self.es.enter_context(self.nc.semaphore("sm_" + name))
            self._ctrs[name] = Ctr(sem, name)
        return self._ctrs[name]

    def on(self, which, fn):
        wstate = self.wd[which]

        def body(eng):
            if which in ("v", "s", "g"):
                eng = DrainGuard(eng)
                eng.drain()
            fn(eng, lambda ev: _wait(eng, wstate, ev))
        getattr(self.blk, self.ENG[which])(body)

    def dma(self, eng, ctr, out, in_, **kw):
        inst = eng.dma_start(out=out, in_=in_, **kw)
        inst.then_inc(ctr.sem, 16)
        ctr.n += 16
        return ctr.evt()

    def idma(self, eng, ctr, **kw):
        inst = eng.indirect_dma_start(**kw)
        inst.then_inc(ctr.sem, 16)
        ctr.n += 16
        return ctr.evt()

    def step(self, ctr, inst, k=1):
        inst.then_inc(ctr.sem, k)
        ctr.n += k
        return ctr.evt()

    def build(self):
        s = self
        s.x_own = s.din("x_own", [TOK_PER_CORE, D])
        s.x_full = s.din("x_full", [B * L, D])
        s.i_WqT = s.din("WqT", [D, D]); s.i_WkT = s.din("WkT", [D, D])
        s.i_WvT = s.din("WvT", [D, D]); s.i_WoT = s.din("WoT", [D, D])
        s.i_WsmT = s.din("WsmT", [D, 12])
        s.i_gwT = s.din("gwT", [D, 8])
        s.i_wi = s.din("wi_e", [D, 4096]); s.i_wo = s.din("wo_e", [4096, D])
        s.i_mixb = s.din("mixb", [12, 1])
        s.i_mixbr = s.din("mixb_row", [1, 12])
        s.i_convw = {k: s.din(k, [128, 32]) for k in ("wq_c", "wk_c", "wv_c")}
        s.i_onorm2 = s.din("onorm2", [128, 2])
        s.i_lng = s.din("ln_g", [128, 8]); s.i_lnb = s.din("ln_b", [128, 8])
        s.i_ident = s.din("c_ident", [128, 128])
        s.i_iota8 = s.din("c_iota8", [128, 8])
        s.i_msl = s.din("c_mask_sl", [128, 128]); s.i_mui = s.din("c_mask_ui", [128, 128])
        s.i_mbd = s.din("c_mask_bdsl", [128, 128]); s.i_moff = s.din("c_mask_off", [128, 128])
        s.i_corelt = s.din("c_corelt", [8, 1])
        s.i_tokid1 = s.din("c_tokid1", [128, NTILE])
        s.i_identb = s.din("c_identb", [128, 128], BF16)
        s.i_u1 = s.din("c_u1strict", [128, 128], BF16)
        s.i_onesc = s.din("c_onesc", [128, 1], BF16)
        s.i_onesr = s.din("c_onesr", [1, 512], BF16)
        s.i_slotidx = s.din("c_slotidx", [128, 16], U32)
        s.o_out = s.dout("out", [TOK_PER_CORE, D])

        s.cc_cnt_in = s.nc.dram_tensor("cc_cnt_in", [1, 8], F32)
        s.cc_cnt_out = s.nc.dram_tensor("cc_cnt_out", [8, 8], F32)
        s.cc_id_in = s.nc.dram_tensor("cc_id_in", [IDBUF_N], F32)
        s.cc_id_out = s.nc.dram_tensor("cc_id_out", [IDBUF_N], F32)
        s.cc_e_in = s.nc.dram_tensor("cc_e_in", [CAP, D], BF16)
        s.cc_e_out = s.nc.dram_tensor("cc_e_out", [NEXP * CAP, D], BF16)
        s.xh_dram = s.nc.dram_tensor("xh_dram", [NTILE, 8 * 128 * 128], BF16)

        with s.es:
            s._alloc()
            s.blk = s.es.enter_context(s.nc.Block())
            s.wd = {k: {} for k in s.ENG}
            s._emit()
        return s.nc

    def _alloc(self):
        sb = self.sb
        s = self
        s.ident = sb("identS", [128, 128])
        s.identb = sb("identbS", [128, 128], BF16)
        s.iota8 = sb("iota8S", [128, 8])
        s.u1strict = sb("u1S", [128, 128], BF16)
        s.onesc = sb("onescS", [128, 1], BF16)
        s.onesr = sb("onesrS", [1, 512], BF16)
        s.mask_sl = sb("mslS", [128, 128])
        s.mask_ui = sb("muiS", [128, 128])
        s.mask_bdsl = sb("mbdS", [128, 128])
        s.mask_off = sb("moffS", [128, 128])
        s.corelt = sb("coreltS", [8, 1])
        s.tokid1 = sb("tokid1S", [128, NTILE])
        s.slotidx = sb("slotidxS", [128, 16], U32)
        s.onorm2s = sb("onorm2S", [128, 2])
        s.lng = sb("lngS", [128, 8])
        s.lnb = sb("lnbS", [128, 8])
        s.lnbb = sb("lnbbS", [128, 8], BF16)
        s.mixbS = sb("mixbSS", [12, 1])
        s.mixbR = sb("mixbRS", [1, 12])
        s.convw = {k: sb(k + "S", [128, 32]) for k in ("wq_c", "wk_c", "wv_c")}

        s.WsmTs = sb("WsmTsS", [128, 8 * 12], BF16)
        s.gwf = sb("gwfS", [128, 8 * 8])
        s.gwraw = sb("gwrawS", [128, 8 * 8])
        s.gwh = sb("gwhS", [128, 8 * 8], BF16)
        s.gwl = sb("gwlS", [128, 8 * 8], BF16)
        s.bconst = sb("bconstS", [1, 4096], BF16)

        s.r_top1 = sb("r_top1", [128, NTILE])
        s.r_selr = sb("r_selr", [128, NTILE])
        s.r_wgt = sb("r_wgt", [128, NTILE])
        s.r_mu = sb("r_mu", [128, NTILE])
        s.r_var = sb("r_var", [128, NTILE])
        s.r_rs = sb("r_rs", [128, NTILE])
        s.r_rs2 = sb("r_rs2", [128, NTILE])
        s.r_keep = sb("r_keep", [128, NTILE])
        s.r_rank = sb("r_rank", [128, NTILE])
        s.r_gidx = sb("r_gidx", [128, NTILE], U32)
        s.r_sidx = sb("r_sidx", [128, NTILE], U32)

        s.t_w1 = sb("t_w1", [128, NTILE])
        s.t_w2 = sb("t_w2", [128, NTILE])
        s.cntrow = sb("cntrow", [1, 8 * (NTILE + 1)])
        s.cnt8 = sb("cnt8", [8, 8])
        s.prefrow = sb("prefrow", [1, 8])
        s.cbc = sb("cbc", [128, 8])
        s.crow = sb("crow", [1, 8])


        s.t_x = [sb(f"t_x{i}", [128, D]) for i in range(2)]
        s.t_a = sb("t_a", [128, D])
        s.t_bf = [sb(f"t_bf{i}", [128, D], BF16) for i in range(4)]
        s.t_bn = sb("t_bn", [128, 12])
        s.t_ag = sb("t_ag", [128, 2])
        s.t_p8 = [sb(f"t_p8{i}", [8, 128]) for i in range(2)]
        s.t_l8 = [sb(f"t_l8{i}", [128, 8]) for i in range(2)]
        s.t_mx = [sb(f"t_mx{i}", [128, 8]) for i in range(2)]
        s.t_ai = [sb(f"t_ai{i}", [128, 8], U32) for i in range(2)]
        s.t_oh = [sb(f"t_oh{i}", [128, 8], BF16) for i in range(2)]
        s.t_ohf = [sb(f"t_ohf{i}", [128, 8]) for i in range(2)]
        s.t_s = [sb(f"t_s{i}", [128, 8]) for i in range(6)]
        s.t_zero = sb("t_zero", [128, 129])
        s.slotid = sb("slotid", [128, 16])
        s.gxi = sb("gxi", [128, 16], U32)
        s.f_mu = sb("f_mu", [128, 16])
        s.f_var = sb("f_var", [128, 16])
        s.f_sc = sb("f_sc", [128, 16])
        if "snap" in self.taps:
            s.snap_ps = sb("snap_ps", [128, 8])
            s.snap_oh = sb("snap_oh", [128, 8], BF16)
            s.snap_l8 = sb("snap_l8", [128, 8])

        s.pb = [self.psum(f"pb{i}", [128, 512]) for i in range(8)]

        s.es_s1x = ExitStack()
        s.xhb = [s.es_s1x.enter_context(s.nc.sbuf_tensor(f"xhb{i}", [128, 8 * 128], BF16))
                 for i in range(2)]
        s.xlb = [s.es_s1x.enter_context(s.nc.sbuf_tensor(f"xlb{i}", [128, 8 * 128], BF16))
                 for i in range(2)]
        s.es_s1x.close()

        s.es_ffn = ExitStack()

        def sbf(name, shape, dt=F32):
            return s.es_ffn.enter_context(s.nc.sbuf_tensor(name, list(shape), dt))
        s.d_t = [sbf(f"d_t{i}", [128, 8 * 256], BF16) for i in range(2)]
        s.h_t = [sbf("h_t0", [128, 32 * 256], BF16)]
        s.e_outb = [sbf(f"e_outb{i}", [128, D], BF16) for i in range(2)]
        s.f_x = [sbf(f"f_x{i}", [128, D]) for i in range(2)]
        s.wis = sbf("wisS", [128, 8 * 4096], BF16)
        s.wos = sbf("wosS", [128, 32 * D], BF16)
        self._ffn_freed = False

    def _emit(self):
        s = self
        ev = {}
        s.ev = ev
        C = s.ctr
        s.S = {k: C(k) for k in ("t", "v", "s", "g")}

        def ld_sy(eng, w):
            cc = C("q_const")
            for src, dst in [(s.i_ident, s.ident), (s.i_iota8, s.iota8), (s.i_msl, s.mask_sl),
                             (s.i_mui, s.mask_ui), (s.i_mbd, s.mask_bdsl), (s.i_moff, s.mask_off),
                             (s.i_corelt, s.corelt), (s.i_tokid1, s.tokid1), (s.i_identb, s.identb),
                             (s.i_u1, s.u1strict), (s.i_onesc, s.onesc), (s.i_onesr, s.onesr),
                             (s.i_slotidx, s.slotidx),
                             (s.i_onorm2, s.onorm2s), (s.i_lng, s.lng), (s.i_lnb, s.lnb),
                             (s.i_mixb, s.mixbS), (s.i_mixbr, s.mixbR)]:
                s.dma(eng, cc, dst[:, :], src[:, :])
            for k in ("wq_c", "wk_c", "wv_c"):
                s.dma(eng, cc, s.convw[k][:, :], s.i_convw[k][:, :])
            cg = C("q_gwraw")
            for kc in range(8):
                s.dma(eng, cg, s.gwraw[:, 8 * kc:8 * kc + 8], s.i_gwT[128 * kc:128 * (kc + 1), :])
            ev["consts"] = cc.evt()
            ev["gwraw"] = cg.evt()
        self.on("sy", ld_sy)

        def ld_gp(eng, w):
            cw = C("q_wis")
            for kc in range(8):
                s.dma(eng, cw, s.wis[:, kc * 4096:(kc + 1) * 4096], s.i_wi[128 * kc:128 * (kc + 1), :])
            ev["wis"] = cw.evt()
            co = C("q_wos")
            for kc in range(32):
                s.dma(eng, co, s.wos[:, kc * D:(kc + 1) * D], s.i_wo[128 * kc:128 * (kc + 1), :])
            ev["wos"] = co.evt()
            cf = C("q_gwf")
            for kc in range(8):
                s.dma(eng, cf, s.gwf[:, 8 * kc:8 * kc + 8], s.i_gwT[128 * kc:128 * (kc + 1), :])
            ev["gwf_raw"] = cf.evt()
            cm = C("q_wsm")
            for kc in range(8):
                s.dma(eng, cm, s.WsmTs[:, kc * 12:(kc + 1) * 12], s.i_WsmT[128 * kc:128 * (kc + 1), :])
            ev["wsm"] = cm.evt()
        self.on("g", ld_gp)

        def prep_v(eng, w):
            w(ev["consts"]); w(ev["gwf_raw"])
            for kc in range(8):
                eng.tensor_scalar(out=s.gwf[:, 8 * kc:8 * kc + 8], in0=s.gwf[:, 8 * kc:8 * kc + 8],
                                  scalar1=s.lng[:, kc:kc + 1], scalar2=None, op0=ALU.mult)
            eng.tensor_copy(out=s.lnbb[:, :], in_=s.lnb[:, :])
            eng.drain()
            eng.tensor_copy(out=s.gwh[:, :], in_=s.gwf[:, :])
            eng.drain()
            eng.tensor_tensor(out=s.gwl[:, :], in0=s.gwf[:, :], in1=s.gwh[:, :], op=ALU.subtract)
            ev["gwsplit"] = s.step(s.S["v"], eng.memset(s.t_zero[:, :], 0.0))
        self.on("v", prep_v)

        def crow_t(eng, w):
            w(ev["gwraw"]); w(ev["consts"])
            ps = s.pb[7]
            for kc in range(8):
                inst = eng.matmul(ps[0:1, 0:8], s.lnb[:, kc:kc + 1], s.gwraw[:, 8 * kc:8 * kc + 8],
                                  start=(kc == 0), stop=(kc == 7))
            ev["crow_mm"] = s.step(s.S["t"], inst)
        self.on("t", crow_t)

        def crow_v(eng, w):
            w(ev["crow_mm"])
            ev["crow"] = s.step(s.S["v"], eng.tensor_copy(out=s.crow[:, :], in_=s.pb[7][0:1, 0:8]))
        self.on("v", crow_v)

        def cbc_t(eng, w):
            w(ev["crow"])
            inst = eng.matmul(s.pb[7][0:128, 8:16], s.mask_ui[0:1, 0:128], s.crow[0:1, 0:8],
                              start=True, stop=True)
            ev["cbc_mm"] = s.step(s.S["t"], inst)
        self.on("t", cbc_t)

        def cbc_v(eng, w):
            w(ev["cbc_mm"])
            ev["cbc"] = s.step(s.S["v"], eng.tensor_copy(out=s.cbc[:, :], in_=s.pb[7][0:128, 8:16]))
        self.on("v", cbc_v)

        if self.stage >= 1:
            self._emit_s1()
        if self.stage >= 2:
            self._emit_cc1()
        if self.stage >= 3:
            self._emit_ffn()
        if self.stage >= 4:
            self._emit_attn()
        if self.stage >= 5:
            self._emit_combine()
        if not self._ffn_freed:
            self.es_ffn.close()
            self._ffn_freed = True
        self._emit_taps_and_finish()

    def _emit_s1(self):
        s = self
        ev = s.ev
        C = s.ctr
        SV, ST, SS = s.S["v"], s.S["t"], s.S["s"]
        qx = [C("q_x0"), C("q_x1")]
        qtp = [C("q_tp0"), C("q_tp1")]
        qxs = [C("q_xs0"), C("q_xs1")]

        for t in range(NTILE):
            r = t % 2

            def pa_sy(eng, w, t=t, r=r):
                if t >= 2:
                    w(ev[f"statdone_{t-2}"])
                ev[f"xa_{t}"] = s.dma(eng, qx[r], s.t_x[r][:, :], s.x_own[128 * t:128 * (t + 1), :])
            self.on("sy", pa_sy)

            def pa_v(eng, w, t=t, r=r):
                w(ev[f"xa_{t}"])
                eng.bn_stats(out=s.t_bn[:, 0:6], in_=s.t_x[r][:, 0:512])
                eng.bn_stats(out=s.t_bn[:, 6:12], in_=s.t_x[r][:, 512:1024])
                eng.drain()
                eng.bn_aggr(out=s.t_ag[:, :], in_=s.t_bn[:, :])
                eng.drain()
                eng.tensor_copy(out=s.r_mu[:, t:t + 1], in_=s.t_ag[:, 0:1])
                ev[f"statdone_{t}"] = s.step(SV, eng.tensor_copy(out=s.r_var[:, t:t + 1],
                                                                 in_=s.t_ag[:, 1:2]))
            self.on("v", pa_v)

        def pb_v1(eng, w):
            ev["vareps"] = s.step(SV, eng.tensor_scalar(out=s.t_w1[:, :], in0=s.r_var[:, :],
                                                        scalar1=EPS_LN, scalar2=None, op0=ALU.add))
        self.on("v", pb_v1)

        def pb_s1(eng, w):
            w(ev["vareps"])
            ev["sqrt1"] = s.step(SS, eng.activation(out=s.t_w2[:, :], in_=s.t_w1[:, :], func=AF.Sqrt))
        self.on("s", pb_s1)

        def pb_v2(eng, w):
            w(ev["sqrt1"])
            eng.reciprocal(out=s.r_rs[:, :], in_=s.t_w2[:, :])
            eng.drain()
            eng.tensor_tensor(out=s.t_w1[:, :], in0=s.r_rs[:, :], in1=s.r_rs[:, :], op=ALU.mult)
            ev["v2e"] = s.step(SV, eng.tensor_scalar(out=s.t_w1[:, :], in0=s.t_w1[:, :],
                                                     scalar1=-EPS_LN, scalar2=(1.0 + EPS_LN),
                                                     op0=ALU.mult, op1=ALU.add))
        self.on("v", pb_v2)

        def pb_s2(eng, w):
            w(ev["v2e"])
            ev["sqrt2"] = s.step(SS, eng.activation(out=s.t_w2[:, :], in_=s.t_w1[:, :], func=AF.Sqrt))
        self.on("s", pb_s2)

        def pb_v3(eng, w):
            w(ev["sqrt2"])
            eng.reciprocal(out=s.r_rs2[:, :], in_=s.t_w2[:, :])
            eng.drain()
            ev["stats"] = s.step(SV, eng.tensor_copy(out=s.t_s[0][:, 0:1], in_=s.r_rs2[:, 0:1]))
        self.on("v", pb_v3)

        for t in range(NTILE):
            r = t % 2

            def pbt_sy(eng, w, t=t, r=r):
                if t >= 2:
                    w(ev[f"xbf_{t-2}"])
                else:
                    w(ev[f"statdone_{NTILE - 2 + t}"])
                ev[f"xb_{t}"] = s.dma(eng, qx[r], s.t_x[r][:, :], s.x_own[128 * t:128 * (t + 1), :])
            self.on("sy", pbt_sy)

            def pbt_v(eng, w, t=t, r=r):
                w(ev[f"xb_{t}"]); w(ev["stats"])
                if t >= 2:
                    w(ev[f"xpose_{t-2}"])
                eng.tensor_scalar(out=s.t_bf[r][:, :], in0=s.t_x[r][:, :], scalar1=s.r_mu[:, t:t + 1],
                                  scalar2=s.r_rs[:, t:t + 1], op0=ALU.subtract, op1=ALU.mult)
                eng.tensor_scalar(out=s.t_a[:, :], in0=s.t_x[r][:, :], scalar1=s.r_mu[:, t:t + 1],
                                  scalar2=s.r_rs[:, t:t + 1], op0=ALU.subtract, op1=ALU.mult)
                ev[f"xbf_{t}"] = s.step(SV, eng.tensor_tensor(out=s.t_bf[2 + r][:, :], in0=s.t_a[:, :],
                                                              in1=s.t_bf[r][:, :], op=ALU.subtract))
            self.on("v", pbt_v)

            def pbt_sc(eng, w, t=t, r=r):
                w(ev[f"xbf_{t}"])
                if t >= 2:
                    w(ev[f"lgmm_{t-2}"])
                    w(ev[f"xhst_{t-2}"])
                for kc in range(8):
                    s.dma(eng, qtp[r], s.xhb[r][:, 128 * kc:128 * (kc + 1)],
                          s.t_bf[r][:, 128 * kc:128 * (kc + 1)], transpose=True)
                    s.dma(eng, qtp[r], s.xlb[r][:, 128 * kc:128 * (kc + 1)],
                          s.t_bf[2 + r][:, 128 * kc:128 * (kc + 1)], transpose=True)
                ev[f"xpose_{t}"] = qtp[r].evt()
            self.on("s", pbt_sc)

            def pbt_st(eng, w, t=t, r=r):
                w(ev[f"xpose_{t}"])
                ev[f"xhst_{t}"] = s.dma(eng, qxs[r], s.xh_dram.ap()[t, :], s.xhb[r][:, :])
            self.on("sy", pbt_st)

            def pbt_t1(eng, w, t=t, r=r):
                w(ev[f"xpose_{t}"]); w(ev["gwsplit"]); w(ev["cbc"])
                if t >= 2:
                    w(ev[f"pev_{t-2}"])
                psl = s.pb[4 if r == 0 else 7]
                first = True
                for kc in range(8):
                    xs = s.xhb[r][:, 128 * kc:128 * (kc + 1)]
                    xl = s.xlb[r][:, 128 * kc:128 * (kc + 1)]
                    eng.matmul(psl[0:8, 0:128], s.gwh[:, 8 * kc:8 * kc + 8], xs, start=first, stop=False)
                    first = False
                    eng.matmul(psl[0:8, 0:128], s.gwl[:, 8 * kc:8 * kc + 8], xs, start=False, stop=False)
                    inst = eng.matmul(psl[0:8, 0:128], s.gwh[:, 8 * kc:8 * kc + 8], xl, start=False,
                                      stop=(kc == 7))
                ev[f"lgmm_{t}"] = s.step(ST, inst)
            self.on("t", pbt_t1)

            def pbt_v2(eng, w, t=t, r=r):
                w(ev[f"lgmm_{t}"])
                psl = s.pb[4 if r == 0 else 7]
                ev[f"pev_{t}"] = s.step(SV, eng.tensor_copy(out=s.t_p8[r][:, :], in_=psl[0:8, 0:128]))
            self.on("v", pbt_v2)

            def pbt_t2(eng, w, t=t, r=r):
                w(ev[f"pev_{t}"])
                if t >= 2:
                    w(ev[f"rout_{t-2}"])
                inst = eng.matmul(s.pb[5][0:128, 8 * r:8 * r + 8], s.t_p8[r][:, :], s.ident[0:8, 0:8],
                                  start=True, stop=True)
                ev[f"ptm_{t}"] = s.step(ST, inst)
            self.on("t", pbt_t2)

            def pbt_v3(eng, w, t=t, r=r):
                w(ev[f"ptm_{t}"])
                eng.scalar_tensor_tensor(out=s.t_l8[r][:, :], in0=s.pb[5][0:128, 8 * r:8 * r + 8],
                                         scalar=s.r_rs2[:, t:t + 1], in1=s.cbc[:, :],
                                         op0=ALU.mult, op1=ALU.add)
                eng.drain()
                eng.max(out=s.t_mx[r][:, :], in_=s.t_l8[r][:, :])
                eng.drain()
                eng.max_index(out=s.t_ai[r][:, :], in_max=s.t_mx[r][:, :], in_values=s.t_l8[r][:, :])
                eng.drain()
                eng.tensor_copy(out=s.r_top1[:, t:t + 1], in_=s.t_ai[r][:, 0:1])
                eng.tensor_scalar(out=s.t_s[0][:, r:r + 1], in0=s.t_mx[r][:, 0:1], scalar1=-1.0,
                                  scalar2=None, op0=ALU.mult)
                eng.tensor_scalar(out=s.t_oh[r][:, :], in0=s.iota8[:, :],
                                  scalar1=s.r_top1[:, t:t + 1], scalar2=None, op0=ALU.is_equal)
                eng.drain()
                ev[f"rout_{t}"] = s.step(SV, eng.tensor_copy(out=s.t_ohf[r][:, :], in_=s.t_oh[r][:, :]))
            self.on("v", pbt_v3)

            def pbt_s2(eng, w, t=t, r=r):
                w(ev[f"rout_{t}"])
                ev[f"esum_{t}"] = s.step(SS, eng.activation(
                    out=s.t_s[1][:, 0:8], in_=s.t_l8[r][:, :], func=AF.Exp,
                    bias=s.t_s[0][:, r:r + 1], accum_out=s.t_s[2][:, r:r + 1]))
            self.on("s", pbt_s2)

            def pbt_v4(eng, w, t=t, r=r):
                w(ev[f"esum_{t}"])
                ev[f"wgt_{t}"] = s.step(SV, eng.reciprocal(out=s.r_wgt[:, t:t + 1],
                                                           in_=s.t_s[2][:, r:r + 1]))
            self.on("v", pbt_v4)

            def pbt_t3(eng, w, t=t, r=r):
                w(ev[f"rout_{t}"])
                if t >= 2:
                    w(ev[f"selr_{t-2}"])
                eng.matmul(s.pb[5][0:128, 16 + 8 * r:24 + 8 * r], s.u1strict[:, :], s.t_oh[r][:, :],
                           start=True, stop=True)
                inst = eng.matmul(s.pb[6][0:1, 8 * t:8 * t + 8], s.onesc[:, 0:1], s.t_oh[r][:, :],
                                  start=(t == 0), stop=(t == NTILE - 1))
                ev[f"cnt_{t}"] = s.step(ST, inst)
            self.on("t", pbt_t3)

            def pbt_v5(eng, w, t=t, r=r):
                w(ev[f"cnt_{t}"])
                eng.tensor_copy(out=s.t_s[3][:, 0:8], in_=s.pb[5][0:128, 16 + 8 * r:24 + 8 * r])
                eng.drain()
                eng.tensor_tensor(out=s.t_s[3][:, 0:8], in0=s.t_s[3][:, 0:8],
                                  in1=s.t_ohf[r][:, :], op=ALU.mult)
                eng.drain()
                eng.tensor_reduce(out=s.t_s[4][:, 0:1], in_=s.t_s[3][:, 0:8],
                                  axis=AX.X, op=ALU.add)
                eng.drain()
                eng.tensor_copy(out=s.r_selr[:, t:t + 1], in_=s.t_s[4][:, 0:1])
                if t == self.tsnap and "snap" in self.taps:
                    eng.tensor_copy(out=s.snap_ps[:, 0:8], in_=s.pb[5][0:128, 16 + 8 * r:24 + 8 * r])
                    eng.tensor_copy(out=s.snap_oh[:, :], in_=s.t_oh[r][:, :])
                    eng.tensor_copy(out=s.snap_l8[:, :], in_=s.t_l8[r][:, :])
                ev[f"selr_{t}"] = s.step(SV, eng.tensor_copy(out=s.t_s[3][:, 0:1], in_=s.t_s[3][:, 0:1]))
            self.on("v", pbt_v5)

        def cnts_v(eng, w):
            w(ev[f"cnt_{NTILE-1}"])
            eng.memset(s.cntrow[:, 0:8], 0.0)
            eng.tensor_copy(out=s.cntrow[:, 8:8 + 8 * NTILE], in_=s.pb[6][0:1, 0:8 * NTILE])
            for t in range(NTILE):
                eng.drain()
                inst = eng.tensor_tensor(out=s.cntrow[:, 8 * (t + 1):8 * (t + 2)],
                                         in0=s.cntrow[:, 8 * (t + 1):8 * (t + 2)],
                                         in1=s.cntrow[:, 8 * t:8 * (t + 1)], op=ALU.add)
            eng.drain()
            ev["cntincl"] = s.step(SV, inst)
        self.on("v", cnts_v)

    def _emit_cc1(self):
        s = self
        ev = s.ev
        C = s.ctr
        SV, ST = s.S["v"], s.S["t"]
        ccs = C("cc")

        def cc_g(eng, w):
            w(ev["cntincl"])
            c1 = C("q_ccin")
            s.dma(eng, c1, s.cc_cnt_in.ap(), s.cntrow[:, 8 * NTILE:8 * (NTILE + 1)])
            eng.wait_ge(c1.sem, c1.n)
            if self.sub & 1:
                inst = eng.collective_compute("AllGather", ALU.bypass,
                                              replica_groups=[list(range(NCORES))],
                                              ins=[s.cc_cnt_in.ap().opt()],
                                              outs=[s.cc_cnt_out.ap().opt()])
                s.step(ccs, inst)
                eng.wait_ge(ccs.sem, ccs.n)
                c2 = C("q_ccout")
                ev["cnt8"] = s.dma(eng, c2, s.cnt8[:, :], s.cc_cnt_out.ap())
            else:
                c2 = C("q_ccout")
                for cc_ in range(8):
                    ev["cnt8"] = s.dma(eng, c2, s.cnt8[cc_:cc_ + 1, :], s.cc_cnt_in.ap())
        self.on("g", cc_g)

        def pref_t(eng, w):
            w(ev["cnt8"])
            inst = eng.matmul(s.pb[5][0:1, 32:40], s.corelt[:, :], s.cnt8[:, :], start=True, stop=True)
            ev["prefmm"] = s.step(ST, inst)
        self.on("t", pref_t)

        def pref_v(eng, w):
            w(ev["prefmm"])
            ev["prefrow"] = s.step(SV, eng.tensor_copy(out=s.prefrow[:, :], in_=s.pb[5][0:1, 32:40]))
        self.on("v", pref_v)

        for t in range(NTILE if (self.sub & 2) else 0):
            r = t % 2

            def p2_t(eng, w, t=t, r=r):
                w(ev["prefrow"])
                if t >= 1 and not (self.sub & 16):
                    w(ev[f"p2v_{t-1}"])
                ps = s.pb[5]
                sl = slice(40 + 8 * r, 48 + 8 * r)
                eng.matmul(ps[0:128, sl], s.mask_ui[0:1, 0:128], s.cntrow[0:1, 8 * t:8 * (t + 1)],
                           start=True, stop=False)
                inst = eng.matmul(ps[0:128, sl], s.mask_ui[0:1, 0:128], s.prefrow[0:1, 0:8],
                                  start=False, stop=True)
                ev[f"p2mm_{t}"] = s.step(ST, inst)
            self.on("t", p2_t)

            if self.sub & 16:
                continue

            def p2_v(eng, w, t=t, r=r):
                w(ev[f"p2mm_{t}"])
                sl = slice(40 + 8 * r, 48 + 8 * r)
                if self.sub & 64:
                    ev[f"p2v_{t}"] = s.step(SV, eng.tensor_copy(out=s.t_s[5][:, 0:1],
                                                                in_=s.iota8[:, 0:1]))
                    return
                eng.tensor_scalar(out=s.t_oh[r][:, :], in0=s.iota8[:, :], scalar1=s.r_top1[:, t:t + 1],
                                  scalar2=None, op0=ALU.is_equal)
                eng.tensor_copy(out=s.t_ohf[r][:, :], in_=s.t_oh[r][:, :])
                if self.sub & 128:
                    ev[f"p2v_{t}"] = s.step(SV, eng.tensor_copy(out=s.t_s[5][:, 0:1],
                                                                in_=s.iota8[:, 0:1]))
                    return
                eng.tensor_copy(out=s.t_s[3][:, 0:8], in_=s.pb[5][0:128, sl])
                eng.tensor_tensor(out=s.t_s[3][:, 0:8], in0=s.t_s[3][:, 0:8], in1=s.t_ohf[r][:, :],
                                  op=ALU.mult)
                eng.drain()
                eng.tensor_reduce(out=s.t_s[4][:, 0:1], in_=s.t_s[3][:, 0:8], axis=AX.X, op=ALU.add)
                eng.drain()
                if self.sub & 256:
                    ev[f"p2v_{t}"] = s.step(SV, eng.tensor_copy(out=s.t_s[5][:, 0:1],
                                                                in_=s.iota8[:, 0:1]))
                    return
                eng.tensor_tensor(out=s.r_rank[:, t:t + 1], in0=s.r_selr[:, t:t + 1],
                                  in1=s.t_s[4][:, 0:1], op=ALU.add)
                if self.sub & 32:
                    ev[f"p2v_{t}"] = s.step(SV, eng.tensor_copy(out=s.r_sidx[:, t:t + 1],
                                                                in_=s.t_s[5][:, 0:1]))
                    return
                eng.tensor_scalar(out=s.r_keep[:, t:t + 1], in0=s.r_rank[:, t:t + 1],
                                  scalar1=float(CAP) - 0.5, scalar2=None, op0=ALU.is_le)
                eng.tensor_tensor(out=s.r_wgt[:, t:t + 1], in0=s.r_wgt[:, t:t + 1],
                                  in1=s.r_keep[:, t:t + 1], op=ALU.mult)
                eng.tensor_scalar(out=s.t_s[4][:, 0:1], in0=s.r_rank[:, t:t + 1],
                                  scalar1=float(CAP - 1), scalar2=None, op0=ALU.min)
                eng.scalar_tensor_tensor(out=s.t_s[5][:, 0:1], in0=s.r_top1[:, t:t + 1],
                                         scalar=float(CAP), in1=s.t_s[4][:, 0:1],
                                         op0=ALU.mult, op1=ALU.add)
                eng.tensor_copy(out=s.r_gidx[:, t:t + 1], in_=s.t_s[5][:, 0:1])
                eng.scalar_tensor_tensor(out=s.t_s[5][:, 0:1], in0=s.r_top1[:, t:t + 1],
                                         scalar=float(CAP), in1=s.r_rank[:, t:t + 1],
                                         op0=ALU.mult, op1=ALU.add)
                eng.scalar_tensor_tensor(out=s.t_s[5][:, 0:1], in0=s.r_keep[:, t:t + 1],
                                         scalar=-100000.0, in1=s.t_s[5][:, 0:1],
                                         op0=ALU.mult, op1=ALU.add)
                eng.tensor_scalar(out=s.t_s[5][:, 0:1], in0=s.t_s[5][:, 0:1], scalar1=100000.0,
                                  scalar2=None, op0=ALU.add)
                ev[f"p2v_{t}"] = s.step(SV, eng.tensor_copy(out=s.r_sidx[:, t:t + 1],
                                                            in_=s.t_s[5][:, 0:1]))
            self.on("v", p2_v)

        if not (self.sub & 4):
            return

        def idz_sy(eng, w):
            w(ev["gwsplit"])
            cz = C("q_idz")
            idv = s.cc_id_in.ap().rearrange("(p f) -> p f", p=128)
            ev["idz"] = s.dma(eng, cz, idv, s.t_zero[:, 0:IDBUF_N // 128])
        self.on("sy", idz_sy)

        def scat_g(eng, w):
            w(ev["idz"])
            w(ev[f"p2v_{NTILE-1}"])
            csc = C("q_scat")
            for t in range(NTILE):
                s.idma(eng, csc,
                       out=s.cc_id_in.ap().rearrange("(n o) -> n o", o=1),
                       out_offset=IOoA(ap=s.r_sidx[:, t:t + 1], axis=0),
                       in_=s.tokid1[:, t:t + 1], in_offset=None,
                       bounds_check=NEXP * CAP - 1, oob_is_err=False)
            eng.wait_ge(csc.sem, csc.n)
            ccs = s.ctr("cc")
            inst = eng.collective_compute("AllReduce", ALU.add, replica_groups=[list(range(NCORES))],
                                          ins=[s.cc_id_in.ap().opt()], outs=[s.cc_id_out.ap().opt()])
            s.step(ccs, inst)
            eng.wait_ge(ccs.sem, ccs.n)
            if self.sub & 8:
                cgt = C("q_slotg")
                for c in range(16):
                    s.idma(eng, cgt,
                           out=s.slotid[:, c:c + 1], out_offset=None,
                           in_=s.cc_id_out.ap().rearrange("(n o) -> n o", o=1),
                           in_offset=IOoA(ap=s.slotidx[:, c:c + 1], axis=0))
                ev["slotid"] = cgt.evt()
        self.on("g", scat_g)

        if not (self.sub & 8):
            return

        def gxi_v(eng, w):
            w(ev["slotid"])
            eng.tensor_scalar(out=s.f_sc[:, :], in0=s.slotid[:, :], scalar1=-1.0, scalar2=0.0,
                              op0=ALU.add, op1=ALU.max)
            ev["gxi"] = s.step(s.S["v"], eng.tensor_copy(out=s.gxi[:, :], in_=s.f_sc[:, :]))
        self.on("v", gxi_v)

    def _emit_ffn(self):
        s = self
        ev = s.ev
        C = s.ctr
        SV, ST, SS, SG = s.S["v"], s.S["t"], s.S["s"], s.S["g"]

        for ch in range(8):
            def bc_t1(eng, w, ch=ch):
                w(ev["wis"]); w(ev["consts"])
                if ch > 0:
                    w(ev[f"bcev_{ch-1}"])
                for kc in range(8):
                    inst = eng.matmul(s.pb[0][0:1, 0:512], s.lnbb[:, kc:kc + 1],
                                      s.wis[:, kc * 4096 + 512 * ch: kc * 4096 + 512 * (ch + 1)],
                                      start=(kc == 0), stop=(kc == 7))
                ev[f"bcmm_{ch}"] = s.step(ST, inst)
            self.on("t", bc_t1)

            def bc_v1(eng, w, ch=ch):
                w(ev[f"bcmm_{ch}"])
                ev[f"bcev_{ch}"] = s.step(SV, eng.tensor_copy(out=s.bconst[:, 512 * ch:512 * (ch + 1)],
                                                              in_=s.pb[0][0:1, 0:512]))
            self.on("v", bc_v1)

        def wis_v(eng, w):
            w(ev["bcev_7"]); w(ev["consts"])
            for kc in range(8):
                inst = eng.tensor_scalar(out=s.wis[:, kc * 4096:(kc + 1) * 4096],
                                         in0=s.wis[:, kc * 4096:(kc + 1) * 4096],
                                         scalar1=s.lng[:, kc:kc + 1], scalar2=None, op0=ALU.mult)
            ev["wis_scaled"] = s.step(SV, inst)
        self.on("v", wis_v)

        qgx = [C("q_gx0"), C("q_gx1")]
        qdt = [C("q_dt0"), C("q_dt1")]
        qes = [C("q_es0"), C("q_es1")]
        NT2 = 8

        def emit_tile(c):
            rc = c % 2

            def fg_g(eng, w, c=c, rc=rc):
                w(ev["gxi"])
                if c >= 2:
                    w(ev[f"fd_{c-2}"])
                ev[f"fgx_{c}"] = s.idma(eng, qgx[rc],
                                        out=s.f_x[rc][:, :], out_offset=None,
                                        in_=s.x_full[:, :],
                                        in_offset=IOoA(ap=s.gxi[:, c:c + 1], axis=0))
            self.on("g", fg_g)

            def fs_v(eng, w, c=c, rc=rc):
                w(ev[f"fgx_{c}"])
                eng.bn_stats(out=s.t_bn[:, 0:6], in_=s.f_x[rc][:, 0:512])
                eng.bn_stats(out=s.t_bn[:, 6:12], in_=s.f_x[rc][:, 512:1024])
                eng.drain()
                eng.bn_aggr(out=s.t_ag[:, :], in_=s.t_bn[:, :])
                eng.drain()
                eng.tensor_copy(out=s.f_mu[:, c:c + 1], in_=s.t_ag[:, 0:1])
                ev[f"fstat_{c}"] = s.step(SV, eng.tensor_scalar(
                    out=s.t_s[0][:, rc:rc + 1], in0=s.t_ag[:, 1:2],
                    scalar1=EPS_LN, scalar2=None, op0=ALU.add))
            self.on("v", fs_v)

            def fs_s(eng, w, c=c, rc=rc):
                w(ev[f"fstat_{c}"])
                ev[f"fsq_{c}"] = s.step(SS, eng.activation(out=s.t_s[1][:, rc:rc + 1],
                                                           in_=s.t_s[0][:, rc:rc + 1], func=AF.Sqrt))
            self.on("s", fs_s)

            def fs_v2(eng, w, c=c, rc=rc):
                w(ev[f"fsq_{c}"])
                eng.reciprocal(out=s.t_s[2][:, rc:rc + 1], in_=s.t_s[1][:, rc:rc + 1])
                eng.drain()
                eng.tensor_tensor(out=s.t_s[3][:, rc:rc + 1], in0=s.t_s[2][:, rc:rc + 1],
                                  in1=s.t_s[2][:, rc:rc + 1], op=ALU.mult)
                eng.tensor_scalar(out=s.t_s[3][:, rc:rc + 1], in0=s.t_s[3][:, rc:rc + 1],
                                  scalar1=-EPS_LN, scalar2=(1.0 + EPS_LN), op0=ALU.mult, op1=ALU.add)
                ev[f"fv2_{c}"] = s.step(SV, eng.tensor_copy(out=s.t_s[4][:, rc:rc + 1],
                                                            in_=s.t_s[3][:, rc:rc + 1]))
            self.on("v", fs_v2)

            def fs_s2(eng, w, c=c, rc=rc):
                w(ev[f"fv2_{c}"])
                ev[f"fsq2_{c}"] = s.step(SS, eng.activation(out=s.t_s[5][:, rc:rc + 1],
                                                            in_=s.t_s[4][:, rc:rc + 1], func=AF.Sqrt))
            self.on("s", fs_s2)

            def fd_v(eng, w, c=c, rc=rc):
                w(ev[f"fsq2_{c}"])
                if c >= 4:
                    w(ev[f"hmm_{32 * (c // 2 - 2) + 31}"])
                eng.reciprocal(out=s.t_s[4][:, rc:rc + 1], in_=s.t_s[5][:, rc:rc + 1])
                eng.drain()
                eng.tensor_tensor(out=s.f_sc[:, c:c + 1], in0=s.t_s[2][:, rc:rc + 1],
                                  in1=s.t_s[4][:, rc:rc + 1], op=ALU.mult)
                eng.drain()
                ev[f"fd_{c}"] = s.step(SV, eng.tensor_scalar(
                    out=s.t_bf[rc][:, :], in0=s.f_x[rc][:, :],
                    scalar1=s.f_mu[:, c:c + 1], scalar2=s.f_sc[:, c:c + 1],
                    op0=ALU.subtract, op1=ALU.mult))
            self.on("v", fd_v)

            def fd_sc(eng, w, c=c, rc=rc):
                w(ev[f"fd_{c}"])
                if c < 4 and self.stage >= 1:
                    w(ev[f"xhst_{NTILE-2}"]); w(ev[f"xhst_{NTILE-1}"])
                    w(ev[f"lgmm_{NTILE-1}"])
                if c >= 4:
                    w(ev[f"hmm_{32 * (c // 2 - 2) + 31}"])
                st2 = (c // 2) % 2
                for kc in range(8):
                    s.dma(eng, qdt[rc], s.d_t[st2][:, kc * 256 + rc * 128: kc * 256 + rc * 128 + 128],
                          s.t_bf[rc][:, 128 * kc:128 * (kc + 1)], transpose=True)
                ev[f"fdt_{c}"] = qdt[rc].evt()
            self.on("s", fd_sc)

        def emit_step(st):
            p = st % 2

            for ic4 in range(8):
                def fh_t(eng, w, st=st, p=p, ic4=ic4):
                    if ic4 == 0:
                        w(ev[f"fdt_{2*st}"]); w(ev[f"fdt_{2*st+1}"]); w(ev["wis_scaled"])
                        if st == 0:
                            w(ev[f"pev_{NTILE-1}"])
                    for ic in range(4 * ic4, 4 * ic4 + 4):
                        g = 32 * st + ic
                        if g >= 4:
                            w(ev[f"hgel_{g-4}"])
                        psl = s.pb[(4, 7, 1, 2)[ic % 4]]
                        for kc in range(8):
                            eng.matmul(psl[0:128, 0:256],
                                       s.wis[:, kc * 4096 + 128 * ic:kc * 4096 + 128 * (ic + 1)],
                                       s.d_t[p][:, kc * 256:(kc + 1) * 256], start=(kc == 0), stop=False)
                        inst = eng.matmul(psl[0:128, 0:256], s.bconst[0:1, 128 * ic:128 * (ic + 1)],
                                          s.onesr[0:1, 0:256], start=False, stop=True)
                        ev[f"hmm_{g}"] = s.step(ST, inst)
                self.on("t", fh_t)

                def fh_s(eng, w, st=st, p=p, ic4=ic4):
                    for ic in range(4 * ic4, 4 * ic4 + 4):
                        g = 32 * st + ic
                        w(ev[f"hmm_{g}"])
                        psl = s.pb[(4, 7, 1, 2)[ic % 4]]
                        inst = eng.activation(out=s.h_t[0][:, 256 * ic:256 * (ic + 1)],
                                              in_=psl[0:128, 0:256], func=AF.Gelu)
                        ev[f"hgel_{g}"] = s.step(SS, inst)
                self.on("s", fh_s)

            for mt in range(2):
                def fe_t(eng, w, st=st, p=p, mt=mt):
                    if mt == 0:
                        w(ev[f"hgel_{32*st+31}"]); w(ev["wos"])
                    if st == 0 and mt == 0:
                        w(ev[f"p2v_{NTILE-1}"]); w(ev["cntincl"])
                    prev = (st, mt - 1) if mt == 1 else (st - 1, 1)
                    if prev[0] >= 0:
                        w(ev[f"eoev_{prev[0]}_{prev[1]}"])
                    for nn_ in range(2):
                        psl = s.pb[5 if nn_ == 0 else 6]
                        for kc in range(32):
                            inst = eng.matmul(psl[0:128, 0:512],
                                              s.h_t[0][:, 256 * kc + 128 * mt:256 * kc + 128 * mt + 128],
                                              s.wos[:, kc * D + 512 * nn_: kc * D + 512 * (nn_ + 1)],
                                              start=(kc == 0), stop=(kc == 31))
                        ev[f"eomm_{st}_{mt}_{nn_}"] = s.step(ST, inst)
                self.on("t", fe_t)

                def fe_v(eng, w, st=st, p=p, mt=mt):
                    w(ev[f"eomm_{st}_{mt}_0"]); w(ev[f"eomm_{st}_{mt}_1"])
                    if st > 0:
                        w(ev[f"eost_{st-1}_{mt}"])
                    eng.tensor_copy(out=s.e_outb[mt][:, 0:512], in_=s.pb[5][0:128, 0:512])
                    inst = eng.tensor_copy(out=s.e_outb[mt][:, 512:1024], in_=s.pb[6][0:128, 0:512])
                    ev[f"eoev_{st}_{mt}"] = s.step(SV, inst)
                self.on("v", fe_v)

                def fe_g(eng, w, st=st, p=p, mt=mt):
                    w(ev[f"eoev_{st}_{mt}"])
                    ev[f"eost_{st}_{mt}"] = s.dma(
                        eng, qes[mt],
                        s.cc_e_in.ap()[256 * st + 128 * mt:256 * st + 128 * (mt + 1), :],
                        s.e_outb[mt][:, :])
                self.on("g", fe_g)

        for st in range(NT2):
            emit_tile(2 * st)
            emit_tile(2 * st + 1)
            emit_step(st)

        def eag_g(eng, w):
            for st in range(NT2):
                for mt in range(2):
                    w(ev[f"eost_{st}_{mt}"])
            ccs = s.ctr("cc")
            inst = eng.collective_compute("AllGather", ALU.bypass, replica_groups=[list(range(NCORES))],
                                          ins=[s.cc_e_in.ap().opt()], outs=[s.cc_e_out.ap().opt()])
            ev["eall"] = s.step(ccs, inst)
        self.on("g", eag_g)

    def _emit_attn(self):
        s = self
        ev = s.ev
        C = s.ctr
        SV, ST, SS, SG = s.S["v"], s.S["t"], s.S["s"], s.S["g"]

        s.es_ffn.close()
        s._ffn_freed = True
        s.es_att = ExitStack()

        def sba(name, shape, dt=F32):
            return s.es_att.enter_context(s.nc.sbuf_tensor(name, list(shape), dt))
        s.WqTs = sba("WqTsS", [128, 8 * D], BF16)
        s.WkTs = sba("WkTsS", [128, 8 * D], BF16)
        s.WvTs = sba("WvTsS", [128, 8 * D], BF16)
        s.WoTs = sba("WoTsS", [128, 8 * D], BF16)
        s.xh_g = sba("xh_gS", [128, 8 * 512], BF16)
        s.S_f32 = sba("S_f32S", [128, 4 * 512])
        s.sqs = sba("sqsS", [128, 256], BF16)
        s.q_fm = sba("q_fmS", [128, 8 * 512], BF16)
        s.k_fm = sba("k_fmS", [128, 8 * 512], BF16)
        s.k_tm = sba("k_tmS", [128, 4 * 1024], BF16)
        s.v_tm = sba("v_tmS", [128, 4 * 1024], BF16)
        s.v_fm = sba("v_fmS", [128, 8 * 512], BF16)
        s.praw = [sba(f"praw{i}", [128, 516], BF16) for i in range(2)]
        s.cvt = [sba(f"cvt{i}", [128, 512], BF16) for i in range(2)]
        s.tails = sba("tailsS", [128, 3 * 8 * 4])
        s.smrow = sba("smrowS", [12, 512])
        s.smrowb = sba("smrowbS", [12, 512], BF16)
        s.smcol = sba("smcolS", [128, 4 * 12])
        s.Aoff = sba("AoffS", [128, 16 * 128], BF16)
        s.Wt = sba("WtS", [128, 16 * 128], BF16)
        s.Tt = [sba("TtS", [128, 128], BF16), sba("Tt2S", [128, 128], BF16)]
        s.Abd = sba("AbdS", [128, 128])
        s.zerob = sba("zerobS", [1, 256], BF16)
        s.gdec = sba("gdecS", [1, 2048], BF16)
        s.fs_t = sba("fs_tS", [128, 1024])
        s.S_sb = sba("S_sbS", [128, 4 * 512], BF16)
        s.G_sb = sba("G_sbS", [128, 256], BF16)
        s.JT_sb = sba("JT_sbS", [128, 128], BF16)
        s.g0_sb = sba("g0_sbS", [128, 256], BF16)
        s.h1_sb = sba("h1_sbS", [128, 256], BF16)
        s.u2_sb = sba("u2_sbS", [128, 256], BF16)
        s.aT_sb = sba("aT_sbS", [128, 128], BF16)
        s.o_tm = s.t_bf[1]
        s.onf = s.t_bf[2]
        s.ema_t = s.t_bf[3]
        s.gbc = [sba("gbcS", [128, 512], BF16), sba("gbc2S", [128, 512], BF16)]
        s.d1b = [sba("d1bS", [128, 512], BF16), sba("d1b2S", [128, 512], BF16)]
        s.ema_st = sba("ema_stS", [128, 8])
        s.bv_sb = sba("bv_sbS", [128, 256], BF16)
        s.kq2 = s.t_bf[0]
        s.l2r = sba("l2rS", [1, 512])
        s.colsc = sba("colscS", [128, 4 * 16])
        s.gE = [sba(f"gE{i}", [128, D], BF16) for i in range(2)]

        qxg = [C("q_xg0"), C("q_xg1")]
        qkt = [C("q_kt0"), C("q_kt1")]
        qfsi = [C("q_fsi0"), C("q_fsi1")]
        qfso = [C("q_fso0"), C("q_fso1")]
        qtt = [C("q_tt0"), C("q_tt1")]
        qoe = [C("q_oe0"), C("q_oe1")]
        qxr = [C("q_xr0"), C("q_xr1")]
        qos = [C("q_os0"), C("q_os1")]
        qge = [C("q_ge0"), C("q_ge1")]
        qwa = C("q_wat")

        pbev = {4: ev.get(f"pev_{NTILE-1}"), 5: None, 6: None, 7: ev.get(f"pev_{NTILE-1}")}
        s.pbS = {h: None for h in range(4)}

        def wat_g(eng, w):
            w(ev["eomm_7_1_1"])
            w(ev["eost_7_0"]); w(ev["eost_7_1"])
            w(ev[f"hgel_{8*32-1}"])
            for W_in, W_sb in [(s.i_WqT, s.WqTs), (s.i_WkT, s.WkTs), (s.i_WvT, s.WvTs),
                               (s.i_WoT, s.WoTs)]:
                for kc in range(8):
                    s.dma(eng, qwa, W_sb[:, kc * D:(kc + 1) * D], W_in[128 * kc:128 * (kc + 1), :])
            ev["wat"] = qwa.evt()
        self.on("g", wat_g)

        def wo_v(eng, w):
            w(ev["wat"]); w(ev["consts"])
            eng.memset(s.zerob[:, :], 0.0)
            eng.memset(s.t_zero[:, 120:121], EPS_L2)
            eng.memset(s.t_zero[:, 121:122], EPS_RMS)
            eng.memset(s.Tt[0][:, :], 0.0)
            inst = eng.memset(s.Tt[1][:, :], 0.0)
            ev["ttinit"] = s.step(SV, inst)
            for kc in range(8):
                inst = eng.tensor_scalar(out=s.WoTs[:, kc * D:(kc + 1) * D],
                                         in0=s.WoTs[:, kc * D:(kc + 1) * D],
                                         scalar1=s.onorm2s[:, kc % 2:kc % 2 + 1],
                                         scalar2=None, op0=ALU.mult)
            ev["wofold"] = s.step(SV, inst)
        self.on("v", wo_v)

        for sg in range(SEQ_PER_CORE * NGROUP):
            sq, g = divmod(sg, NGROUP)
            self._attn_pphase(sg, sq, g, pbev)
            self._attn_scan(sg, sq, g, pbev)
        s.es_att.close()

    def _attn_pphase(self, sg, sq, g, pbev):
        s = self
        ev = s.ev
        ALUc = ALU
        SV, ST, SS, SG = s.S["v"], s.S["t"], s.S["s"], s.S["g"]
        qxg = [s.ctr("q_xg0"), s.ctr("q_xg1")]
        qkt = [s.ctr(f"q_kt{i}") for i in range(4)]
        qfsi = [s.ctr("q_fsi0"), s.ctr("q_fsi1")]
        qfso = [s.ctr("q_fso0"), s.ctr("q_fso1")]
        qtt = [s.ctr(f"q_tt{i}") for i in range(16)]
        qgd = s.ctr("q_gd")
        p = sg % 2

        def xg_sy(eng, w, sg=sg, p=p):
            if sg >= 1:
                w(ev[f"pjmm_{sg-1}_2_7"])
                w(ev[f"smmm_{sg-1}"])
            else:
                w(ev[f"hgel_{8*32-1}"]); w(ev["eomm_7_1_1"])
                w(ev["eost_7_0"]); w(ev["eost_7_1"])
            for ct in range(GROUP):
                w(ev[f"xhst_{sg * GROUP + ct}"])
            for ct in range(GROUP):
                t = sg * GROUP + ct
                xr = s.xh_dram.ap()[t, :].rearrange("(p c) -> p c", p=128)
                for kc in range(8):
                    s.dma(eng, qxg[p], s.xh_g[:, kc * 512 + 128 * ct: kc * 512 + 128 * (ct + 1)],
                          xr[:, 128 * kc:128 * (kc + 1)])
            ev[f"xg_{sg}"] = qxg[p].evt()
        self.on("sy", xg_sy)

        if g == 0:
            def tz_v(eng, w, sg=sg):
                if sg >= 2:
                    w(ev[f"cvd_{(sg-1)*24 + 23}"])
                else:
                    w(ev[f"hgel_{8*32-1}"])
                inst = eng.memset(s.tails[:, :], 0.0)
                ev[f"tz_{sg}"] = s.step(SV, inst)
            self.on("v", tz_v)

        WS = [s.WqTs, s.WkTs, s.WvTs]
        CW = [s.convw["wq_c"], s.convw["wk_c"], s.convw["wv_c"]]
        DST = [s.q_fm, s.k_fm, s.v_fm]
        for wi_ in range(3):
            for mc in range(8):
                bidx = sg * 24 + wi_ * 8 + mc
                reg = 4 if bidx % 2 == 0 else 7
                rp = bidx % 2

                def pj_t(eng, w, sg=sg, wi_=wi_, mc=mc, reg=reg, p=p):
                    w(ev[f"xg_{sg}"]); w(ev["wat"])
                    if pbev[reg] is not None:
                        w(pbev[reg])
                    for kc in range(8):
                        inst = eng.matmul(s.pb[reg][0:128, 0:512],
                                          WS[wi_][:, kc * D + 128 * mc:kc * D + 128 * (mc + 1)],
                                          s.xh_g[:, kc * 512:(kc + 1) * 512],
                                          start=(kc == 0), stop=(kc == 7))
                    ev[f"pjmm_{sg}_{wi_}_{mc}"] = s.step(ST, inst)
                self.on("t", pj_t)

                def pj_s(eng, w, sg=sg, wi_=wi_, mc=mc, reg=reg, rp=rp, bidx=bidx):
                    w(ev[f"pjmm_{sg}_{wi_}_{mc}"])
                    if bidx >= 2:
                        w(ev[f"cvd_{bidx-2}"])
                    if sg % NGROUP == 0:
                        w(ev[f"tz_{sg}"])
                    eng.tensor_copy(out=s.praw[rp][:, 0:3],
                                    in_=s.tails[:, (wi_ * 8 + mc) * 3:(wi_ * 8 + mc) * 3 + 3])
                    inst = eng.tensor_copy(out=s.praw[rp][:, 3:515], in_=s.pb[reg][0:128, 0:512])
                    ev[f"pjev_{sg}_{wi_}_{mc}"] = s.step(SV, inst)
                self.on("v", pj_s)

                eng_n = "v"

                def pj_c(eng, w, sg=sg, wi_=wi_, mc=mc, rp=rp, bidx=bidx):
                    w(ev[f"pjev_{sg}_{wi_}_{mc}"])
                    cw = CW[wi_]
                    eng.tensor_scalar(out=s.cvt[rp][:, :], in0=s.praw[rp][:, 0:512],
                                      scalar1=cw[:, 4 * mc:4 * mc + 1], scalar2=None, op0=ALUc.mult)
                    for j in range(1, 4):
                        eng.scalar_tensor_tensor(out=s.cvt[rp][:, :], in0=s.praw[rp][:, j:512 + j],
                                                 scalar=cw[:, 4 * mc + j:4 * mc + j + 1],
                                                 in1=s.cvt[rp][:, :], op0=ALUc.mult, op1=ALUc.add)
                    ev[f"cv_{sg}_{wi_}_{mc}"] = s.step(s.S["v"],
                                                       eng.tensor_copy(out=s.cvt[rp][:, 0:1],
                                                                       in_=s.cvt[rp][:, 0:1]))
                self.on(eng_n, pj_c)

                def pj_tl(eng, w, sg=sg, wi_=wi_, mc=mc, rp=rp, bidx=bidx):
                    w(ev[f"pjev_{sg}_{wi_}_{mc}"])
                    inst = eng.tensor_copy(out=s.tails[:, (wi_ * 8 + mc) * 3:(wi_ * 8 + mc) * 3 + 3],
                                           in_=s.praw[rp][:, 512:515])
                    ev[f"tl_{sg}_{wi_}_{mc}"] = s.step(SV, inst)
                self.on("v", pj_tl)

                def pj_sil(eng, w, sg=sg, wi_=wi_, mc=mc, rp=rp, bidx=bidx):
                    w(ev[f"cv_{sg}_{wi_}_{mc}"])
                    if sg >= 1:
                        if wi_ == 2:
                            w(ev[f"emat_{sg-1}_{3}"]); w(ev[f"emat_{sg-1}_{2}"])
                        else:
                            w(ev[f"omm_{sg-1}_{15}"])
                    inst = eng.activation(out=DST[wi_][:, 512 * mc:512 * (mc + 1)],
                                          in_=s.cvt[rp][:, :], func=AF.Silu)
                    ev[f"cvd_{bidx}"] = s.step(SS, inst)
                self.on("s", pj_sil)
                pbev[reg] = ev[f"pjev_{sg}_{wi_}_{mc}"]

        def sm_t(eng, w, sg=sg, p=p):
            w(ev[f"xg_{sg}"]); w(ev["wsm"]); w(ev["consts"])
            if pbev[6] is not None:
                w(pbev[6])
            for kc in range(8):
                inst = eng.matmul(s.pb[6][0:12, 0:512], s.WsmTs[:, kc * 12:(kc + 1) * 12],
                                  s.xh_g[:, kc * 512:(kc + 1) * 512],
                                  start=(kc == 0), stop=(kc == 7))
            ev[f"smmm_{sg}"] = s.step(ST, inst)
        self.on("t", sm_t)

        def sm_s(eng, w, sg=sg):
            w(ev[f"smmm_{sg}"])
            if sg >= 1:
                w(ev[f"smcmm_{sg-1}_{3}"]); w(ev[f"smb_{sg-1}"])
            inst = eng.activation(out=s.smrow[:, :], in_=s.pb[6][0:12, 0:512], func=AF.Sigmoid,
                                  bias=s.mixbS[0:12, 0:1])
            ev[f"smsig_{sg}"] = s.step(SS, inst)
        self.on("s", sm_s)
        pbev[6] = ev[f"smsig_{sg}"]

        def sm_v(eng, w, sg=sg):
            w(ev[f"smsig_{sg}"])
            if sg >= 1:
                w(ev[f"gdec_{sg-1}"])
            inst = eng.tensor_copy(out=s.smrowb[:, :], in_=s.smrow[:, :])
            ev[f"smb_{sg}"] = s.step(SV, inst)
        self.on("v", sm_v)

        def sm_gd(eng, w, sg=sg, p=p):
            w(ev[f"smb_{sg}"])
            if sg >= 1:
                w(ev[f"emg_{sg-1}_{3}"])
            for h in range(4):
                s.dma(eng, s.ctr("q_gd"), s.gdec[0:1, 512 * h:512 * (h + 1)],
                      s.smrowb[4 + h:5 + h, 0:512])
            ev[f"gdec_{sg}"] = s.ctr("q_gd").evt()
        self.on("sy", sm_gd)

        for ct in range(GROUP):
            cp = ct % 2

            def kt_sc(eng, w, sg=sg, ct=ct):
                w(ev[f"cvd_{sg*24 + 15}"])
                w(ev[f"cvd_{sg*24 + 23}"])
                if sg >= 1:
                    w(ev[f"supd_{sg-1}_{GROUP-1}_3"])
                    w(ev[f"l2e_{sg-1}_{ct}"])
                    w(ev[f"bv_{sg-1}_{4*ct+3}"])
                    w(ev[f"smm_{sg-1}_{4*ct+3}"])
                    if ct >= 2:
                        w(ev[f"ema_{sg-1}_{1}"]); w(ev[f"ema_{sg-1}_{3}"])
                for b in range(8):
                    s.dma(eng, qkt[ct], s.k_tm[:, ct * 1024 + 128 * b: ct * 1024 + 128 * (b + 1)],
                          s.k_fm[:, 512 * b + 128 * ct: 512 * b + 128 * (ct + 1)], transpose=True)
                    s.dma(eng, qkt[ct], s.v_tm[:, ct * 1024 + 128 * b: ct * 1024 + 128 * (b + 1)],
                          s.v_fm[:, 512 * b + 128 * ct: 512 * b + 128 * (ct + 1)], transpose=True)
                ev[f"ktv_{sg}_{ct}"] = qkt[ct].evt()
            self.on("sy", kt_sc)

            def smc_t(eng, w, sg=sg, ct=ct):
                w(ev[f"smsig_{sg}"])
                if pbev[6] is not None:
                    w(pbev[6])
                inst = eng.matmul(s.pb[6][0:128, 0:12], s.smrow[:, 128 * ct:128 * (ct + 1)],
                                  s.ident[0:12, 0:12], start=True, stop=True)
                ev[f"smcmm_{sg}_{ct}"] = s.step(ST, inst)
            self.on("t", smc_t)

            def smc_v(eng, w, sg=sg, ct=ct):
                w(ev[f"smcmm_{sg}_{ct}"])
                inst = eng.tensor_copy(out=s.smcol[:, 12 * ct:12 * (ct + 1)], in_=s.pb[6][0:128, 0:12])
                ev[f"smc_{sg}_{ct}"] = s.step(SV, inst)
            self.on("v", smc_v)
            pbev[6] = ev[f"smc_{sg}_{ct}"]

            def q2_s(eng, w, sg=sg, ct=ct):
                w(ev[f"cvd_{sg*24 + 7}"])
                tg_ = sg * GROUP + ct
                if tg_ >= 1:
                    w(ev[f"l2e_{(tg_-1)//GROUP}_{(tg_-1)%GROUP}"])
                inst = eng.activation(out=s.kq2[:, :],
                                      in_=s.q_fm.rearrange("p (b c) -> p b c", b=8)
                                      [:, :, 128 * ct:128 * (ct + 1)],
                                      func=AF.Square)
                ev[f"q2_{sg}_{ct}"] = s.step(SS, inst)
            self.on("s", q2_s)

            def l2_t(eng, w, sg=sg, ct=ct):
                w(ev[f"q2_{sg}_{ct}"])
                if pbev[5] is not None:
                    w(pbev[5])
                for h in range(4):
                    for bb in range(2):
                        inst = eng.matmul(s.pb[5][0:1, 128 * h:128 * (h + 1)], s.onesc[:, 0:1],
                                          s.kq2[:, 128 * (2 * h + bb):128 * (2 * h + bb + 1)],
                                          start=(h == 0 and bb == 0), stop=(h == 3 and bb == 1))
                ev[f"l2mm_{sg}_{ct}"] = s.step(ST, inst)
            self.on("t", l2_t)

            def l2_v(eng, w, sg=sg, ct=ct):
                w(ev[f"l2mm_{sg}_{ct}"])
                inst = eng.tensor_copy(out=s.l2r[0:1, :], in_=s.pb[5][0:1, 0:512])
                ev[f"l2r_{sg}_{ct}"] = s.step(SV, inst)
            self.on("v", l2_v)
            pbev[5] = ev[f"l2r_{sg}_{ct}"]

            def l2c_t(eng, w, sg=sg, ct=ct):
                w(ev[f"l2r_{sg}_{ct}"])
                for h in range(4):
                    inst = eng.matmul(s.pb[5][0:128, h:h + 1], s.l2r[0:1, 128 * h:128 * (h + 1)],
                                      s.ident[0:1, 0:1], start=(h == 0), stop=(h == 3))
                ev[f"l2cmm_{sg}_{ct}"] = s.step(ST, inst)
            self.on("t", l2c_t)

            def kl2_s(eng, w, sg=sg, ct=ct):
                w(ev[f"ktv_{sg}_{ct}"])
                for h in range(4):
                    inst = eng.activation(out=s.sqs[:, :],
                                          in_=s.k_tm[:, ct * 1024 + 256 * h:ct * 1024 + 256 * (h + 1)],
                                          func=AF.Square, accum_out=s.t_s[1][:, h:h + 1])
                ev[f"kl2_{sg}_{ct}"] = s.step(SS, inst)
            self.on("s", kl2_s)

            def col_v(eng, w, sg=sg, ct=ct):
                w(ev[f"l2cmm_{sg}_{ct}"]); w(ev[f"smc_{sg}_{ct}"])
                eng.tensor_scalar(out=s.t_s[0][:, 0:4], in0=s.pb[5][0:128, 0:4],
                                  scalar1=EPS_L2, scalar2=None, op0=ALUc.add)
                eng.drain()
                inst = eng.tensor_copy(out=s.t_s[2][:, 0:4], in_=s.t_s[0][:, 0:4])
                ev[f"l2e_{sg}_{ct}"] = s.step(SV, inst)
            self.on("v", col_v)
            pbev[5] = ev[f"l2e_{sg}_{ct}"]

            def col_s(eng, w, sg=sg, ct=ct):
                w(ev[f"l2e_{sg}_{ct}"]); w(ev[f"kl2_{sg}_{ct}"]); w(ev["ttinit"])
                eng.activation(out=s.t_s[3][:, 0:4], in_=s.t_s[0][:, 0:4], func=AF.Sqrt)
                inst = eng.activation(out=s.t_s[3][:, 4:8], in_=s.t_s[1][:, 0:4], func=AF.Sqrt,
                                      bias=s.t_zero[:, 120:121])
                ev[f"l2s_{sg}_{ct}"] = s.step(SS, inst)
            self.on("s", col_s)

            def col_v2(eng, w, sg=sg, ct=ct):
                w(ev[f"l2s_{sg}_{ct}"])
                co = s.colsc
                o = 16 * ct
                eng.reciprocal(out=co[:, o + 4:o + 8], in_=s.t_s[3][:, 0:4])
                eng.drain()
                eng.reciprocal(out=co[:, o + 0:o + 4], in_=s.t_s[3][:, 4:8])
                eng.drain()
                eng.tensor_tensor(out=co[:, o + 12:o + 16], in0=s.smcol[:, 12 * ct:12 * ct + 4],
                                  in1=co[:, o + 0:o + 4], op=ALUc.mult)
                eng.drain()
                eng.tensor_tensor(out=co[:, o + 8:o + 12], in0=co[:, o + 12:o + 16],
                                  in1=co[:, o + 0:o + 4], op=ALUc.mult)
                eng.drain()
                eng.tensor_scalar(out=co[:, o + 8:o + 12], in0=co[:, o + 8:o + 12],
                                  scalar1=-1.0, scalar2=None, op0=ALUc.mult)
                eng.drain()
                inst = eng.tensor_copy(out=co[:, o:o + 1], in_=co[:, o:o + 1])
                ev[f"colsdone_{sg}_{ct}"] = s.step(SV, inst)
            self.on("v", col_v2)

            for h in range(4):
                chi = ct * 4 + h

                def amm_t(eng, w, sg=sg, ct=ct, h=h, chi=chi):
                    w(ev[f"cvd_{sg*24 + 15}"])
                    if pbev[4] is not None:
                        w(pbev[4])
                    for bb in range(2):
                        sl_ = slice(512 * (2 * h + bb) + 128 * ct, 512 * (2 * h + bb) + 128 * (ct + 1))
                        inst = eng.matmul(s.pb[4][0:128, 0:128], s.k_fm[:, sl_], s.k_fm[:, sl_],
                                          start=(bb == 0), stop=(bb == 1))
                    ev[f"amm_{sg}_{chi}"] = s.step(ST, inst)
                self.on("t", amm_t)

                def aev_v(eng, w, sg=sg, ct=ct, h=h, chi=chi):
                    w(ev[f"amm_{sg}_{chi}"]); w(ev[f"colsdone_{sg}_{ct}"])
                    if chi > 0:
                        w(ev[f"mi_{sg}_{chi-1}"])
                    o = 16 * ct
                    eng.scalar_tensor_tensor(out=s.Aoff[:, 128 * chi:128 * (chi + 1)],
                                             in0=s.pb[4][0:128, 0:128],
                                             scalar=s.colsc[:, o + 8 + h:o + 9 + h],
                                             in1=s.mask_off[:, :], op0=ALUc.mult, op1=ALUc.mult)
                    eng.drain()
                    inst = eng.scalar_tensor_tensor(out=s.Abd[:, :],
                                                    in0=s.pb[4][0:128, 0:128],
                                                    scalar=s.colsc[:, o + 8 + h:o + 9 + h],
                                                    in1=s.mask_bdsl[:, :], op0=ALUc.mult, op1=ALUc.mult)
                    ev[f"aev_{sg}_{chi}"] = s.step(SV, inst)
                self.on("v", aev_v)
                pbev[4] = ev[f"aev_{sg}_{chi}"]

                def mi_sy(eng, w, sg=sg, chi=chi):
                    w(ev[f"aev_{sg}_{chi}"])
                    p2_ = sg % 2
                    if sg >= 1:
                        w(ev[f"mo_{sg-1}_{15}"]); w(ev[f"mo_{sg-1}_{14}"])
                    for b in range(4):
                        s.dma(eng, qfsi[p2_],
                              s.fs_t[4 * chi + b:4 * chi + b + 1, 0:1024],
                              s.Abd[32 * b:32 * (b + 1), 32 * b:32 * b + 32])
                    ev[f"mi_{sg}_{chi}"] = qfsi[p2_].evt()
                self.on("sy", mi_sy)

        def fs_v(eng, w, sg=sg):
            w(ev[f"mi_{sg}_{15}"])
            X = s.fs_t
            raw = eng._eng
            for j in range(0, 31):
                raw.drain()
                for i in range(j + 1, 32):
                    raw.scalar_tensor_tensor(out=X[0:64, 32 * i:32 * (i + 1)],
                                             in0=X[0:64, 32 * j:32 * (j + 1)],
                                             scalar=X[0:64, 32 * i + j:32 * i + j + 1],
                                             in1=X[0:64, 32 * i:32 * (i + 1)],
                                             op0=ALUc.mult, op1=ALUc.add)
            eng.drain()
            inst = eng.tensor_copy(out=X[0:64, 0:1], in_=X[0:64, 0:1])
            ev[f"fsv_{sg}"] = s.step(SV, inst)
        self.on("v", fs_v)

        for chi in range(16):
            tp = chi % 2

            def mo_g(eng, w, sg=sg, chi=chi, tp=tp):
                w(ev[f"fsv_{sg}"])
                p2_ = sg % 2
                if chi >= 2:
                    w(ev[f"wt_{sg}_{chi-2}"])
                elif sg >= 1:
                    w(ev[f"wt_{sg-1}_{14 + chi}"])
                else:
                    w(ev["ttinit"])
                for b in range(4):
                    s.dma(eng, qfso[p2_],
                          s.Tt[tp][32 * b:32 * (b + 1), 32 * b:32 * b + 32],
                          s.fs_t[4 * chi + b:4 * chi + b + 1, 0:1024])
                ev[f"mo_{sg}_{chi}"] = qfso[p2_].evt()
            self.on("g", mo_g)

            def tta_g(eng, w, sg=sg, chi=chi, tp=tp):
                w(ev[f"mo_{sg}_{chi}"])
                inst = eng.tensor_tensor(out=s.Tt[tp][:, :], in0=s.Tt[tp][:, :],
                                         in1=s.identb[:, :], op=ALUc.add)
                ev[f"ttpre_{sg}_{chi}"] = s.step(SG, inst)
            self.on("g", tta_g)

            def tt_sc(eng, w, sg=sg, chi=chi, tp=tp):
                w(ev[f"ttpre_{sg}_{chi}"])
                if sg >= 1:
                    w(ev[f"g0_{sg-1}_{chi}"])
                s.dma(eng, qtt[chi], s.Wt[:, 128 * chi:128 * (chi + 1)], s.Tt[tp][:, :],
                      transpose=True)
                ev[f"wt_{sg}_{chi}"] = qtt[chi].evt()
            self.on("sy", tt_sc)

    def _attn_scan(self, sg, sq, g, pbev):
        s = self
        ev = s.ev
        ALUc = ALU
        SV, ST, SS, SG = s.S["v"], s.S["t"], s.S["s"], s.S["g"]
        qoe = [s.ctr("q_oe0"), s.ctr("q_oe1")]
        qon = [s.ctr("q_on0"), s.ctr("q_on1")]
        qxr = [s.ctr("q_xr0"), s.ctr("q_xr1")]
        qos = [s.ctr("q_os0"), s.ctr("q_os1")]
        qge = [s.ctr("q_ge0"), s.ctr("q_ge1")]
        qgd = s.ctr("q_gd")

        for h in range(4):
            def ema_t(eng, w, sg=sg, h=h):
                w(ev[f"gdec_{sg}"])
                if pbev[6] is not None:
                    w(pbev[6])
                inst = eng.matmul(s.pb[6][0:128, 0:512], s.onesr[0:1, 0:128],
                                  s.gdec[0:1, 512 * h:512 * (h + 1)], start=True, stop=True)
                ev[f"emg_{sg}_{h}"] = s.step(ST, inst)
            self.on("t", ema_t)

            eng_n = "v"
            SE = SV
            pe = 0

            def ema_c(eng, w, sg=sg, h=h, pe=pe):
                w(ev[f"emg_{sg}_{h}"])
                inst = eng.tensor_copy(out=s.gbc[pe][:, :], in_=s.pb[6][0:128, 0:512])
                ev[f"gcp_{sg}_{h}"] = s.step(SV, inst)
            self.on("v", ema_c)
            pbev[6] = ev[f"gcp_{sg}_{h}"]

            def ema_e(eng, w, sg=sg, h=h, SE=SE, pe=pe):
                w(ev[f"gcp_{sg}_{h}"])
                w(ev[f"cvd_{sg*24 + 23}"])
                w(ev[f"ktv_{sg}_{2}"]); w(ev[f"ktv_{sg}_{3}"])
                eng.drain()
                for bb in range(2):
                    b = 2 * h + bb
                    vsl = s.v_fm[:, 512 * b:512 * (b + 1)]
                    eng.tensor_tensor(out=s.d1b[pe][:, :], in0=s.gbc[pe][:, :], in1=vsl,
                                      op=ALUc.mult)
                    eng.drain()
                    eng.tensor_tensor(out=s.d1b[pe][:, :], in0=vsl, in1=s.d1b[pe][:, :],
                                      op=ALUc.subtract)
                    eng.drain()
                    if g == 0:
                        eng.memset(s.ema_st[:, b:b + 1], 0.0)
                        eng.drain()
                    eng.tensor_tensor_scan(out=vsl, data0=s.gbc[pe][:, :], data1=s.d1b[pe][:, :],
                                           initial=s.ema_st[:, b:b + 1],
                                           op0=ALUc.mult, op1=ALUc.add)
                    eng.drain()
                    eng.tensor_copy(out=s.ema_st[:, b:b + 1], in_=vsl[:, 511:512])
                    eng.drain()
                inst = eng.tensor_copy(out=s.gbc[pe][:, 0:1], in_=s.gbc[pe][:, 0:1])
                ev[f"ema_{sg}_{h}"] = s.step(SE, inst)
            self.on(eng_n, ema_e)

        for ct in range(GROUP):
            first_chunk = (g == 0 and ct == 0)
            o = 16 * ct
            tglob = sg * GROUP + ct
            cp = ct % 2

            for h in range(4):
                chi = ct * 4 + h

                def jt_t(eng, w, sg=sg, ct=ct, h=h, chi=chi):
                    w(ev[f"wt_{sg}_{chi}"])
                    if pbev[4] is not None:
                        w(pbev[4])
                    inst = eng.matmul(s.pb[4][0:128, 0:128], s.Aoff[:, 128 * chi:128 * (chi + 1)],
                                      s.Wt[:, 128 * chi:128 * (chi + 1)], start=True, stop=True)
                    ev[f"jtmm_{sg}_{chi}"] = s.step(ST, inst)
                    ev[f"jt_{sg}_{chi}"] = ev[f"jtmm_{sg}_{chi}"]
                self.on("t", jt_t)

                def jt_v(eng, w, sg=sg, chi=chi):
                    w(ev[f"jtmm_{sg}_{chi}"])
                    inst = eng.tensor_copy(out=s.JT_sb[:, :], in_=s.pb[4][0:128, 0:128])
                    ev[f"jtev_{sg}_{chi}"] = s.step(SV, inst)
                self.on("v", jt_v)
                pbev[4] = ev[f"jtev_{sg}_{chi}"]

                def bv_v(eng, w, sg=sg, ct=ct, h=h, chi=chi, o=o):
                    w(ev[f"ktv_{sg}_{ct}"]); w(ev[f"colsdone_{sg}_{ct}"])
                    inst = eng.tensor_scalar(out=s.bv_sb[:, :],
                                             in0=s.v_tm[:, ct * 1024 + 256 * h:ct * 1024 + 256 * (h + 1)],
                                             scalar1=s.colsc[:, o + 12 + h:o + 13 + h],
                                             scalar2=None, op0=ALUc.mult)
                    ev[f"bv_{sg}_{chi}"] = s.step(SV, inst)
                self.on("v", bv_v)

                def g_t(eng, w, sg=sg, ct=ct, h=h, chi=chi, first_chunk=first_chunk):
                    if pbev[5] is not None:
                        w(pbev[5])
                    if not first_chunk:
                        w(ev[f"sev_{sg}_{ct-1}_{h}" if ct > 0 else f"sev_{sg-1}_{GROUP-1}_{h}"])
                        for bb in range(2):
                            sl_ = slice(512 * (2 * h + bb) + 128 * ct,
                                        512 * (2 * h + bb) + 128 * (ct + 1))
                            inst = eng.matmul(s.pb[5][0:128, 0:256], s.k_fm[:, sl_],
                                              s.S_sb[:, 512 * h + 256 * bb:512 * h + 256 * (bb + 1)],
                                              start=(bb == 0), stop=(bb == 1))
                    else:
                        inst = eng.matmul(s.pb[5][0:128, 0:256], s.zerob[0:1, 0:128],
                                          s.zerob[0:1, 0:256], start=True, stop=True)
                    ev[f"gmm_{sg}_{chi}"] = s.step(ST, inst)
                self.on("t", g_t)

                def g_v(eng, w, sg=sg, ct=ct, h=h, chi=chi, o=o):
                    w(ev[f"gmm_{sg}_{chi}"]); w(ev[f"bv_{sg}_{chi}"])
                    inst = eng.scalar_tensor_tensor(out=s.G_sb[:, :], in0=s.pb[5][0:128, 0:256],
                                                    scalar=s.colsc[:, o + 8 + h:o + 9 + h],
                                                    in1=s.bv_sb[:, :], op0=ALUc.mult, op1=ALUc.add)
                    ev[f"gev_{sg}_{chi}"] = s.step(SV, inst)
                self.on("v", g_v)
                pbev[5] = ev[f"gev_{sg}_{chi}"]

                def u_t0(eng, w, sg=sg, chi=chi):
                    w(ev[f"gev_{sg}_{chi}"])
                    if pbev[6] is not None:
                        w(pbev[6])
                    inst = eng.matmul(s.pb[6][0:128, 0:256], s.Wt[:, 128 * chi:128 * (chi + 1)],
                                      s.G_sb[:, :], start=True, stop=True)
                    ev[f"g0mm_{sg}_{chi}"] = s.step(ST, inst)
                    ev[f"g0_{sg}_{chi}"] = ev[f"g0mm_{sg}_{chi}"]
                self.on("t", u_t0)

                def u_v0(eng, w, sg=sg, chi=chi):
                    w(ev[f"g0mm_{sg}_{chi}"])
                    inst = eng.tensor_copy(out=s.g0_sb[:, :], in_=s.pb[6][0:128, 0:256])
                    ev[f"g0ev_{sg}_{chi}"] = s.step(SV, inst)
                self.on("v", u_v0)

                def u_t1(eng, w, sg=sg, chi=chi):
                    w(ev[f"g0ev_{sg}_{chi}"]); w(ev[f"jtev_{sg}_{chi}"])
                    inst = eng.matmul(s.pb[6][0:128, 256:512], s.JT_sb[:, :], s.g0_sb[:, :],
                                      start=True, stop=True)
                    ev[f"h1mm_{sg}_{chi}"] = s.step(ST, inst)
                self.on("t", u_t1)

                def u_v1(eng, w, sg=sg, chi=chi):
                    w(ev[f"h1mm_{sg}_{chi}"])
                    inst = eng.tensor_tensor(out=s.h1_sb[:, :], in0=s.pb[6][0:128, 256:512],
                                             in1=s.g0_sb[:, :], op=ALUc.add)
                    ev[f"h1ev_{sg}_{chi}"] = s.step(SV, inst)
                self.on("v", u_v1)

                def u_t2(eng, w, sg=sg, chi=chi):
                    w(ev[f"h1ev_{sg}_{chi}"])
                    inst = eng.matmul(s.pb[6][0:128, 0:256], s.JT_sb[:, :], s.h1_sb[:, :],
                                      start=True, stop=True)
                    ev[f"h2mm_{sg}_{chi}"] = s.step(ST, inst)
                self.on("t", u_t2)

                def u_v2(eng, w, sg=sg, chi=chi):
                    w(ev[f"h2mm_{sg}_{chi}"])
                    inst = eng.tensor_tensor(out=s.h1_sb[:, :], in0=s.pb[6][0:128, 0:256],
                                             in1=s.g0_sb[:, :], op=ALUc.add)
                    ev[f"h2ev_{sg}_{chi}"] = s.step(SV, inst)
                self.on("v", u_v2)

                def u_t3(eng, w, sg=sg, chi=chi):
                    w(ev[f"h2ev_{sg}_{chi}"])
                    inst = eng.matmul(s.pb[6][0:128, 256:512], s.JT_sb[:, :], s.h1_sb[:, :],
                                      start=True, stop=True)
                    ev[f"u3mm_{sg}_{chi}"] = s.step(ST, inst)
                self.on("t", u_t3)

                def u_v3(eng, w, sg=sg, chi=chi):
                    w(ev[f"u3mm_{sg}_{chi}"])
                    inst = eng.tensor_tensor(out=s.u2_sb[:, :], in0=s.pb[6][0:128, 256:512],
                                             in1=s.g0_sb[:, :], op=ALUc.add)
                    ev[f"uev_{sg}_{chi}"] = s.step(SV, inst)
                self.on("v", u_v3)
                pbev[6] = ev[f"uev_{sg}_{chi}"]

                def at_t(eng, w, sg=sg, ct=ct, h=h, chi=chi):
                    w(ev[f"cvd_{sg*24 + 7}"]); w(ev[f"cvd_{sg*24 + 15}"])
                    if pbev[4] is not None:
                        w(pbev[4])
                    for bb in range(2):
                        sl_ = slice(512 * (2 * h + bb) + 128 * ct, 512 * (2 * h + bb) + 128 * (ct + 1))
                        inst = eng.matmul(s.pb[4][0:128, 0:128], s.k_fm[:, sl_], s.q_fm[:, sl_],
                                          start=(bb == 0), stop=(bb == 1))
                    ev[f"atmm_{sg}_{chi}"] = s.step(ST, inst)
                self.on("t", at_t)

                def at_v(eng, w, sg=sg, ct=ct, h=h, chi=chi, o=o):
                    w(ev[f"atmm_{sg}_{chi}"])
                    inst = eng.tensor_tensor(out=s.aT_sb[:, :], in0=s.pb[4][0:128, 0:128],
                                             in1=s.mask_ui[:, :], op=ALUc.mult)
                    ev[f"atev_{sg}_{chi}"] = s.step(SV, inst)
                self.on("v", at_v)
                pbev[4] = ev[f"atev_{sg}_{chi}"]

                def o_t(eng, w, sg=sg, ct=ct, h=h, chi=chi, first_chunk=first_chunk):
                    w(ev[f"uev_{sg}_{chi}"]); w(ev[f"atev_{sg}_{chi}"])
                    if pbev[7] is not None:
                        w(pbev[7])
                    if not first_chunk:
                        for bb in range(2):
                            sl_ = slice(512 * (2 * h + bb) + 128 * ct,
                                        512 * (2 * h + bb) + 128 * (ct + 1))
                            eng.matmul(s.pb[7][0:128, 0:256], s.q_fm[:, sl_],
                                       s.S_sb[:, 512 * h + 256 * bb:512 * h + 256 * (bb + 1)],
                                       start=(bb == 0), stop=False)
                        inst = eng.matmul(s.pb[7][0:128, 0:256], s.aT_sb[:, :], s.u2_sb[:, :],
                                          start=False, stop=True)
                    else:
                        inst = eng.matmul(s.pb[7][0:128, 0:256], s.aT_sb[:, :], s.u2_sb[:, :],
                                          start=True, stop=True)
                    ev[f"omm_{sg}_{chi}"] = s.step(ST, inst)
                self.on("t", o_t)

                def o_v(eng, w, sg=sg, ct=ct, h=h, chi=chi, o=o):
                    w(ev[f"omm_{sg}_{chi}"])
                    tg_ = sg * GROUP + ct
                    if tg_ >= 1:
                        w(ev[f"onft_{(tg_-1)//GROUP}_{(tg_-1)%GROUP}"])
                    inst = eng.tensor_scalar(out=s.o_tm[:, 256 * h:256 * (h + 1)],
                                             in0=s.pb[7][0:128, 0:256],
                                             scalar1=s.colsc[:, o + 4 + h:o + 5 + h],
                                             scalar2=None, op0=ALUc.mult)
                    ev[f"oev_{sg}_{chi}"] = s.step(SV, inst)
                self.on("v", o_v)
                pbev[7] = ev[f"oev_{sg}_{chi}"]

                def s_t(eng, w, sg=sg, ct=ct, h=h, chi=chi, first_chunk=first_chunk):
                    w(ev[f"uev_{sg}_{chi}"]); w(ev[f"ktv_{sg}_{ct}"])
                    if sq == 0 and g == 0 and ct == 0:
                        w(ev[f"hgel_{8*32-1}"])
                        w(ev["bcev_7"])
                    if s.pbS[h] is not None:
                        w(s.pbS[h])
                    for mm_ in range(2):
                        inst = eng.matmul(
                            s.pb[h][0:128, 256 * mm_:256 * (mm_ + 1)],
                            s.k_tm[:, ct * 1024 + 256 * h + 128 * mm_:
                                   ct * 1024 + 256 * h + 128 * (mm_ + 1)],
                            s.u2_sb[:, :],
                            start=(mm_ == 0), stop=(mm_ == 1))
                    ev[f"smm_{sg}_{chi}"] = s.step(ST, inst)
                self.on("t", s_t)

                def s_va(eng, w, sg=sg, ct=ct, h=h, chi=chi, first_chunk=first_chunk):
                    w(ev[f"smm_{sg}_{chi}"])
                    if first_chunk:
                        inst = eng.tensor_copy(out=s.S_f32[:, 512 * h:512 * (h + 1)],
                                               in_=s.pb[h][0:128, 0:512])
                    else:
                        inst = eng.tensor_tensor(out=s.S_f32[:, 512 * h:512 * (h + 1)],
                                                 in0=s.S_f32[:, 512 * h:512 * (h + 1)],
                                                 in1=s.pb[h][0:128, 0:512], op=ALUc.add)
                    ev[f"sadd_{sg}_{chi}"] = s.step(SV, inst)
                self.on("v", s_va)
                s.pbS[h] = ev[f"sadd_{sg}_{chi}"]

                def s_sv(eng, w, sg=sg, ct=ct, h=h, chi=chi):
                    w(ev[f"sadd_{sg}_{chi}"])
                    inst = eng.copy(out=s.S_sb[:, 512 * h:512 * (h + 1)],
                                    in_=s.S_f32[:, 512 * h:512 * (h + 1)])
                    ev[f"sev_{sg}_{ct}_{h}"] = s.step(SS, inst)
                self.on("s", s_sv)
                if ct == GROUP - 1 and h == 3:
                    ev[f"supd_{sg}_{GROUP-1}_3"] = ev[f"sev_{sg}_{ct}_{h}"]

            def ep_sc1(eng, w, sg=sg, ct=ct, cp=cp):
                for h in range(4):
                    w(ev[f"ema_{sg}_{h}"])
                w(ev[f"oev_{sg}_{ct*4+3}"])
                tg_ = sg * GROUP + ct
                if tg_ >= 1:
                    w(ev[f"cmb_{(tg_-1)//GROUP}_{(tg_-1)%GROUP}"])
                for b in range(8):
                    s.dma(eng, qoe[cp], s.ema_t[:, 128 * b:128 * (b + 1)],
                          s.v_fm[:, 512 * b + 128 * ct:512 * b + 128 * (ct + 1)], transpose=True)
                ev[f"emat_{sg}_{ct}"] = qoe[cp].evt()
            self.on("sy", ep_sc1)

            def ep_v1(eng, w, sg=sg, ct=ct, o=o, tglob=tglob):
                w(ev[f"emat_{sg}_{ct}"])
                w(ev[f"oev_{sg}_{ct*4+3}"])
                eng.tensor_tensor(out=s.ema_t[:, :], in0=s.ema_t[:, :], in1=s.o_tm[:, :],
                                  op=ALUc.subtract)
                eng.drain()
                for h in range(4):
                    inst = eng.scalar_tensor_tensor(out=s.o_tm[:, 256 * h:256 * (h + 1)],
                                                    in0=s.ema_t[:, 256 * h:256 * (h + 1)],
                                                    scalar=s.smcol[:, 12 * ct + 8 + h:12 * ct + 9 + h],
                                                    in1=s.o_tm[:, 256 * h:256 * (h + 1)],
                                                    op0=ALUc.mult, op1=ALUc.add)
                    eng.drain()
                ev[f"cmb_{sg}_{ct}"] = s.step(SV, inst)
            self.on("v", ep_v1)

            def ep_s1(eng, w, sg=sg, ct=ct):
                w(ev[f"cmb_{sg}_{ct}"])
                for h in range(4):
                    eng.activation(out=s.sqs[:, :], in_=s.o_tm[:, 256 * h:256 * (h + 1)],
                                   func=AF.Square, scale=1.0 / 16.0,
                                   accum_out=s.t_s[4][:, h:h + 1])
                inst = eng.activation(out=s.t_s[5][:, 4:8], in_=s.t_s[4][:, 0:4], func=AF.Sqrt,
                                      bias=s.t_zero[:, 121:122])
                ev[f"rmssq_{sg}_{ct}"] = s.step(SS, inst)
            self.on("s", ep_s1)

            def ep_v2(eng, w, sg=sg, ct=ct):
                w(ev[f"rmssq_{sg}_{ct}"])
                eng.reciprocal(out=s.t_s[5][:, 0:4], in_=s.t_s[5][:, 4:8])
                eng.drain()
                for h in range(4):
                    eng.tensor_scalar(out=s.o_tm[:, 256 * h:256 * (h + 1)],
                                      in0=s.o_tm[:, 256 * h:256 * (h + 1)],
                                      scalar1=s.t_s[5][:, h:h + 1], scalar2=None, op0=ALUc.mult)
                    eng.drain()
                inst = eng.tensor_copy(out=s.o_tm[:, 0:1], in_=s.o_tm[:, 0:1])
                ev[f"onrm_{sg}_{ct}"] = s.step(SV, inst)
            self.on("v", ep_v2)

            def ep_sc2(eng, w, sg=sg, ct=ct, cp=cp):
                w(ev[f"onrm_{sg}_{ct}"])
                tg_ = sg * GROUP + ct
                if tg_ >= 1:
                    w(ev[f"womm_{(tg_-1)//GROUP}_{(tg_-1)%GROUP}"])
                for b in range(8):
                    s.dma(eng, qon[cp], s.onf[:, 128 * b:128 * (b + 1)],
                          s.o_tm[:, 128 * b:128 * (b + 1)], transpose=True)
                ev[f"onft_{sg}_{ct}"] = qon[cp].evt()
            self.on("s", ep_sc2)

            def ep_xr(eng, w, sg=sg, ct=ct, cp=cp, tglob=tglob):
                if sg * GROUP + ct >= 2:
                    w(ev[f"ost_{sg*GROUP+ct-2}"])
                else:
                    w(ev["eall"])
                ev[f"xr_{tglob}"] = s.dma(eng, qxr[cp], s.t_x[cp][:, :],
                                          s.x_own[128 * tglob:128 * (tglob + 1), :])
            self.on("sy", ep_xr)

            def ep_ge(eng, w, sg=sg, ct=ct, cp=cp, tglob=tglob):
                w(ev["eall"]); w(ev["gxi"])
                if sg * GROUP + ct >= 2:
                    w(ev[f"epev_{sg*GROUP+ct-2}"])
                ev[f"ge_{tglob}"] = s.idma(eng, qge[cp],
                                           out=s.gE[cp][:, :], out_offset=None,
                                           in_=s.cc_e_out.ap(),
                                           in_offset=IOoA(ap=s.r_gidx[:, tglob:tglob + 1], axis=0))
            self.on("g", ep_ge)

            def ep_v3(eng, w, sg=sg, ct=ct, cp=cp, tglob=tglob):
                w(ev[f"xr_{tglob}"]); w(ev[f"ge_{tglob}"]); w(ev["stats"])
                if tglob >= 1:
                    w(ev[f"womm_{(tglob-1)//GROUP}_{(tglob-1)%GROUP}"])
                eng.tensor_scalar(out=s.t_s[2][:, 4:5], in0=s.r_rs[:, tglob:tglob + 1],
                                  scalar1=1.0, scalar2=None, op0=ALUc.add)
                eng.tensor_tensor(out=s.t_s[2][:, 5:6], in0=s.r_mu[:, tglob:tglob + 1],
                                  in1=s.r_rs[:, tglob:tglob + 1], op=ALUc.mult)
                eng.drain()
                eng.tensor_scalar(out=s.t_s[2][:, 5:6], in0=s.t_s[2][:, 5:6],
                                  scalar1=-1.0, scalar2=None, op0=ALUc.mult)
                eng.drain()
                eng.tensor_scalar(out=s.t_a[:, :], in0=s.t_x[cp][:, :],
                                  scalar1=s.t_s[2][:, 4:5], scalar2=s.t_s[2][:, 5:6],
                                  op0=ALUc.mult, op1=ALUc.add)
                eng.drain()
                inst = eng.tensor_scalar(out=s.gE[cp][:, :], in0=s.gE[cp][:, :],
                                         scalar1=s.r_wgt[:, tglob:tglob + 1], scalar2=None,
                                         op0=ALUc.mult)
                ev[f"res_{sg}_{ct}"] = s.step(SV, inst)
            self.on("v", ep_v3)

            def ep_wo(eng, w, sg=sg, ct=ct, cp=cp, tglob=tglob):
                w(ev[f"onft_{sg}_{ct}"]); w(ev["wofold"]); w(ev[f"res_{sg}_{ct}"])
                if pbev[5] is not None:
                    w(pbev[5])
                if pbev[7] is not None:
                    w(pbev[7])
                for nn_ in range(2):
                    psl = s.pb[5 if nn_ == 0 else 7]
                    for kc in range(8):
                        eng.matmul(psl[0:128, 0:512], s.onf[:, 128 * kc:128 * (kc + 1)],
                                   s.WoTs[:, kc * D + 512 * nn_:kc * D + 512 * (nn_ + 1)],
                                   start=(kc == 0), stop=False)
                    eng.matmul(psl[0:128, 0:512], s.ident[:, :],
                               s.t_a[:, 512 * nn_:512 * (nn_ + 1)], start=False, stop=False)
                    inst = eng.matmul(psl[0:128, 0:512], s.identb[:, :],
                                      s.gE[cp][:, 512 * nn_:512 * (nn_ + 1)], start=False, stop=True)
                ev[f"womm_{sg}_{ct}"] = s.step(ST, inst)
            self.on("t", ep_wo)

            def ep_ev(eng, w, sg=sg, ct=ct, cp=cp, tglob=tglob):
                w(ev[f"womm_{sg}_{ct}"])
                eng.copy(out=s.t_x[cp][:, 0:512], in_=s.pb[5][0:128, 0:512])
                inst = eng.copy(out=s.t_x[cp][:, 512:1024], in_=s.pb[7][0:128, 0:512])
                ev[f"epev_{tglob}"] = s.step(SS, inst)
            self.on("s", ep_ev)
            pbev[5] = ev[f"epev_{tglob}"]
            pbev[7] = ev[f"epev_{tglob}"]

            def ep_st(eng, w, sg=sg, ct=ct, cp=cp, tglob=tglob):
                w(ev[f"epev_{tglob}"])
                ev[f"ost_{tglob}"] = s.dma(eng, qos[cp], s.o_out[128 * tglob:128 * (tglob + 1), :],
                                           s.t_x[cp][:, :])
            self.on("sy", ep_st)

    def _emit_combine(self):
        pass

    def _emit_taps_and_finish(self):
        s = self
        ev = s.ev
        taps = []
        if "stats" in self.taps:
            taps += [("r_mu", s.r_mu, [128, NTILE], F32, "stats"),
                     ("r_var", s.r_var, [128, NTILE], F32, "stats"),
                     ("r_rs", s.r_rs, [128, NTILE], F32, "stats"),
                     ("r_rs2", s.r_rs2, [128, NTILE], F32, "stats")]
        if "snap" in self.taps and self.stage >= 1:
            taps += [("snap_ps", s.snap_ps, [128, 8], F32, f"selr_{self.tsnap}"),
                     ("snap_oh", s.snap_oh, [128, 8], BF16, f"selr_{self.tsnap}"),
                     ("snap_l8", s.snap_l8, [128, 8], F32, f"selr_{self.tsnap}")]
        if "route" in self.taps and self.stage >= 1:
            taps += [("r_top1", s.r_top1, [128, NTILE], F32, f"selr_{NTILE-1}"),
                     ("r_selr", s.r_selr, [128, NTILE], F32, f"selr_{NTILE-1}"),
                     ("r_wgt", s.r_wgt, [128, NTILE], F32,
                      f"p2v_{NTILE-1}" if self.stage >= 2 else f"wgt_{NTILE-1}"),
                     ("cntrow", s.cntrow, [1, 8 * (NTILE + 1)], F32, "cntincl")]
        if "pass2" in self.taps and self.stage >= 2:
            if self.sub & 2:
                taps += [("r_rank", s.r_rank, [128, NTILE], F32, f"p2v_{NTILE-1}"),
                         ("r_keep", s.r_keep, [128, NTILE], F32, f"p2v_{NTILE-1}")]
            taps += [("cnt8", s.cnt8, [8, 8], F32, "cnt8")]
            if self.sub & 8:
                taps += [("slotid", s.slotid, [128, 16], F32, "slotid")]

        if "attn" in self.taps and self.stage >= 4:
            s_ = self
            ataps = [
                ("q_fm", s_.q_fm, [128, 4096], BF16, "cvd_191"),
                ("k_fm", s_.k_fm, [128, 4096], BF16, "cvd_191"),
                ("v_fm", s_.v_fm, [128, 4096], BF16, "ema_7_3"),
                ("k_tm", s_.k_tm, [128, 4096], BF16, "ktv_7_3"),
                ("v_tm", s_.v_tm, [128, 4096], BF16, "ktv_7_3"),
                ("smrow", s_.smrow, [12, 512], F32, "smsig_7"),
                ("smcol", s_.smcol, [128, 48], F32, "smc_7_3"),
                ("colsc", s_.colsc, [128, 64], F32, "colsdone_7_3"),
                ("Aoff", s_.Aoff, [128, 2048], BF16, "aev_7_15"),
                ("Wt", s_.Wt, [128, 2048], BF16, "wt_7_15"),
                ("fs_t", s_.fs_t, [128, 1024], F32, "fsv_7"),
                ("S_f32", s_.S_f32, [128, 2048], F32, "sadd_7_15"),
                ("G_sb", s_.G_sb, [128, 256], BF16, "gev_7_15"),
                ("g0_sb", s_.g0_sb, [128, 256], BF16, "g0ev_7_15"),
                ("u2_sb", s_.u2_sb, [128, 256], BF16, "uev_7_15"),
                ("aT_sb", s_.aT_sb, [128, 128], BF16, "atev_7_15"),
                ("o_tm", s_.o_tm, [128, 1024], BF16, "onrm_7_3"),
                ("onf", s_.onf, [128, 1024], BF16, "onft_7_3"),
                ("ema_st", s_.ema_st, [128, 8], F32, "ema_7_3"),
                ("gdec", s_.gdec, [1, 2048], BF16, "gdec_7"),
                ("t_a", s_.t_a, [128, 1024], F32, "res_7_3"),
            ]
            taps += ataps
        if "ffn" in self.taps and self.stage >= 3:
            def tap_eout(eng, w):
                w(ev["eall"])
                out = self.dout("tap_eall", [NEXP * CAP, D], BF16)
                ct = s.ctr("q_tapff")
                for i in range(16):
                    s.dma(eng, ct, out[1024 * i:1024 * (i + 1), :],
                          s.cc_e_out.ap()[1024 * i:1024 * (i + 1), :])
            self.on("sy", tap_eout)

        def tap_sy(eng, w):
            ct = s.ctr("q_tap")
            for name, src, shape, dt, evname in taps:
                w(ev[evname])
                out = self.dout("tap_" + name, shape, dt)
                s.dma(eng, ct, out[:, :], src[:, :])
            if self.stage < 4:
                w(ev.get("gwsplit") or (s.S["v"], 0))
                for t in range(NTILE):
                    s.dma(eng, ct, s.o_out[128 * t:128 * (t + 1), 0:129], s.t_zero[:, :])
            eng.wait_ge(ct.sem, ct.n)
            for qn, ctr in list(s._ctrs.items()):
                if qn.startswith("q_") and ctr.n > 0:
                    eng.wait_ge(ctr.sem, ctr.n)
        self.on("sy", tap_sy)


def build_mk(taps=(), stage=99, tsnap=2, sub=15):
    mk = MK(taps=taps, stage=stage, tsnap=tsnap, sub=sub)
    nc = mk.build()
    return mk, nc


from concourse.bass_utils import run_bass_kernel_spmd as _run_spmd

_CACHE = {}


def kernel(**inputs):
    inputs = {k: np.asarray(v) for k, v in inputs.items()}
    if "nc" not in _CACHE:
        mk, nc = build_mk(taps=(), stage=4)
        _CACHE["nc"] = nc
    nc = _CACHE["nc"]
    in_maps = [host_inputs_for_core(c, inputs) for c in range(NCORES)]
    res = _run_spmd(nc, in_maps, core_ids=list(range(NCORES)))
    out = np.concatenate([np.asarray(res.results[c]["out"]) for c in range(NCORES)], 0)
    return out.reshape(B, L, D).astype(np.float32)


# version 04
# speedup vs baseline: 1.1443x; 1.0410x over previous
import numpy as np
import ml_dtypes
from contextlib import ExitStack

import concourse.bass as bass
import concourse.mybir as mybir

F32 = mybir.dt.float32
BF16 = mybir.dt.bfloat16
U32 = mybir.dt.uint32
ALU = mybir.AluOpType
AF = mybir.ActivationFunctionType
AX = mybir.AxisListType
IOoA = bass.IndirectOffsetOnAxis

B, L, D = 16, 2048, 1024
NCORES = 8
SEQ_PER_CORE = 2
TOK_PER_CORE = SEQ_PER_CORE * L
NTILE = TOK_PER_CORE // 128
NEXP = 8
CAP = 2048
NH = 4
DK = 256
CHUNK = 128
NCHUNK = L // CHUNK
GROUP = 4
NGROUP = NCHUNK // GROUP
EPS_LN = 1e-5
EPS_L2 = 1e-6
EPS_RMS = 1e-5
IDBUF_N = NEXP * CAP + 128


def host_constants(core):
    c = {}
    r = np.arange(128)
    c["c_ident"] = np.eye(128, dtype=np.float32)
    c["c_identb"] = np.eye(128, dtype=ml_dtypes.bfloat16)
    c["c_iota8"] = np.tile(np.arange(8, dtype=np.float32), (128, 1)).copy()
    c["c_u1strict"] = np.triu(np.ones((128, 128)), 1).astype(ml_dtypes.bfloat16)
    c["c_onesc"] = np.ones((128, 1), dtype=ml_dtypes.bfloat16)
    c["c_onesr"] = np.ones((1, 512), dtype=ml_dtypes.bfloat16)
    c["c_mask_sl"] = (r[:, None] > r[None, :]).astype(np.float32)
    c["c_mask_ui"] = (r[:, None] <= r[None, :]).astype(np.float32)
    blk = r // 32
    c["c_mask_bdsl"] = ((blk[:, None] == blk[None, :]) & (r[:, None] > r[None, :])).astype(np.float32)
    c["c_mask_off"] = (blk[:, None] > blk[None, :]).astype(np.float32)
    c["c_corelt"] = (np.arange(8) < core).astype(np.float32).reshape(8, 1)
    c["c_tokid1"] = (core * TOK_PER_CORE + np.arange(TOK_PER_CORE, dtype=np.float32) + 1.0
                     ).reshape(NTILE, 128).T.copy()
    c["c_slotidx"] = (core * CAP + np.arange(CAP, dtype=np.uint32)).reshape(16, 128).T.copy()
    return c


def host_inputs_for_core(core, inp):
    x = np.ascontiguousarray(np.asarray(inp["L0_attn_hidden_states"], dtype=np.float32).reshape(B * L, D))
    d = {}
    d["x_own"] = np.ascontiguousarray(x[core * TOK_PER_CORE:(core + 1) * TOK_PER_CORE])
    d["x_full"] = x
    d["WqT"] = np.ascontiguousarray(np.asarray(inp["L0_attn_Wq"]).T)
    d["WkT"] = np.ascontiguousarray(np.asarray(inp["L0_attn_Wk"]).T)
    d["WvT"] = np.ascontiguousarray(np.asarray(inp["L0_attn_Wv"]).T)
    d["WoT"] = np.ascontiguousarray(np.asarray(inp["L0_attn_Wo"]).T)
    d["WsmT"] = np.ascontiguousarray(np.concatenate(
        [np.asarray(inp["L0_attn_Wb"]).T, np.asarray(inp["L0_attn_Wdec"]).T,
         np.asarray(inp["L0_attn_Wmix"]).T], axis=1))
    d["mixb"] = np.ascontiguousarray(np.concatenate(
        [np.zeros(8, np.float32), np.asarray(inp["L0_attn_mix_bias"], np.float32)]).reshape(12, 1))
    d["mixb_row"] = np.ascontiguousarray(
        np.concatenate([np.zeros(8, np.float32), np.asarray(inp["L0_attn_mix_bias"], np.float32)]).reshape(1, 12))
    for nm, key in (("wq_c", "L0_attn_wq_conv"), ("wk_c", "L0_attn_wk_conv"), ("wv_c", "L0_attn_wv_conv")):
        w = np.asarray(inp[key])
        d[nm] = np.ascontiguousarray(w.reshape(8, 128, 4).transpose(1, 0, 2).reshape(128, 32))
    d["onorm2"] = np.ascontiguousarray(np.asarray(inp["L0_attn_onorm_w"]).reshape(2, 128).T)
    d["gwT"] = np.ascontiguousarray(np.asarray(inp["L0_ffn_gate_weight"]).T)
    d["ln_g"] = np.ascontiguousarray(np.asarray(inp["L0_ffn_ln_gamma"]).reshape(8, 128).T)
    d["ln_b"] = np.ascontiguousarray(np.asarray(inp["L0_ffn_ln_beta"]).reshape(8, 128).T)
    d["wi_e"] = np.ascontiguousarray(np.asarray(inp["L0_ffn_wi"])[core])
    d["wo_e"] = np.ascontiguousarray(np.asarray(inp["L0_ffn_wo"])[core])
    d.update(host_constants(core))
    return d


class Ctr:
    def __init__(self, sem, name):
        self.sem = sem
        self.n = 0
        self.name = name

    def evt(self):
        return (self, self.n)


def _wait(eng, wstate, event):
    ctr, val = event
    if val <= 0:
        return
    if wstate.get(ctr.name, -1) >= val:
        return
    wstate[ctr.name] = val
    eng.wait_ge(ctr.sem, val)



def _ap_name(a):
    try:
        return a.tensor.name
    except Exception:
        return None


class DrainGuard:

    _READS = {
        "tensor_copy": ("in_",), "tensor_tensor": ("in0", "in1"),
        "tensor_scalar": ("in0", "scalar1", "scalar2"),
        "scalar_tensor_tensor": ("in0", "scalar", "in1"),
        "tensor_reduce": ("in_",), "tensor_tensor_scan": ("data0", "data1", "initial"),
        "reciprocal": ("in_",), "bn_stats": ("in_",), "bn_aggr": ("in_",),
        "max": ("in_",), "max_index": ("in_max", "in_values"),
        "activation": ("in_", "bias", "scale"), "copy": ("in_",),
        "add": ("in_",), "mul": ("in_",), "iota": (), "memset": (),
        "affine_select": ("in_",), "partition_broadcast": ("in_ap",),
        "partition_all_reduce": ("in_ap",),
    }
    _WRITES = {
        "tensor_copy": ("out",), "tensor_tensor": ("out",), "tensor_scalar": ("out", "accum_out"),
        "scalar_tensor_tensor": ("out", "accum_out"), "tensor_reduce": ("out",),
        "tensor_tensor_scan": ("out",), "reciprocal": ("out",), "bn_stats": ("out",),
        "bn_aggr": ("out",), "max": ("out",), "max_index": ("out",),
        "activation": ("out", "accum_out"), "copy": ("out",), "add": ("out",),
        "mul": ("out",), "iota": ("out",), "memset": ("ap",), "affine_select": ("out",),
        "partition_broadcast": ("out_ap",), "partition_all_reduce": ("out_ap",),
    }

    _SPECIAL = {"bn_stats", "bn_aggr", "max", "max_index", "reciprocal",
                "tensor_tensor_scan", "iota", "affine_select"}

    def __init__(self, eng):
        self._eng = eng
        self._w_ord = {}
        self._r_ord = {}
        self._ord = 0

    def drain(self):
        self._eng.drain()
        self._w_ord.clear()
        self._r_ord.clear()

    def __getattr__(self, name):
        eng = object.__getattribute__(self, "_eng")
        attr = getattr(eng, name)
        if name not in self._READS:
            return attr
        rkeys = self._READS[name]
        wkeys = self._WRITES[name]

        def wrapper(*args, **kw):
            import inspect as _i
            if args:
                try:
                    ba = _i.signature(attr).bind(*args, **kw)
                    allkw = dict(ba.arguments)
                except (TypeError, ValueError):
                    allkw = dict(kw)
            else:
                allkw = dict(kw)
            special = name in self._SPECIAL or allkw.get("accum_out") is not None \
                or "is_equal" in str(allkw.get("op0", "")) or "is_equal" in str(allkw.get("op", ""))
            rn = {n for k in rkeys for n in [_ap_name(allkw.get(k))] if n}
            wn = {n for k in wkeys for n in [_ap_name(allkw.get(k))] if n}
            cur = self._ord
            need = False

            def close(rec):
                if rec is None:
                    return False
                o_, sp_ = rec
                return special or sp_ or (cur - o_) <= 2

            for n in rn:
                if close(self._w_ord.get(n)):
                    need = True
            for n in wn:
                if close(self._w_ord.get(n)) or close(self._r_ord.get(n)):
                    need = True
            if need:
                self.drain()
            inst = attr(*args, **kw)
            self._ord += 1
            cur = self._ord
            for n in wn:
                self._w_ord[n] = (cur, special)
            for n in rn:
                self._r_ord[n] = (cur, special)
            return inst
        return wrapper


class MK:
    ENG = {"t": "tensor", "v": "vector", "s": "scalar", "g": "gpsimd", "sy": "sync"}

    def __init__(self, taps=(), stage=99, tsnap=2, sub=15):
        self.taps = set(taps)
        self.sub = sub
        self.stage = stage
        self.tsnap = tsnap
        self.nc = bass.Bass("TRN2", target_bir_lowering=False)
        self._guards = {}
        self.es = ExitStack()
        self._ctrs = {}

    def din(self, name, shape, dt=F32):
        return self.nc.dram_tensor(name, list(shape), dt, kind="ExternalInput").ap()

    def dout(self, name, shape, dt=F32):
        return self.nc.dram_tensor(name, list(shape), dt, kind="ExternalOutput").ap()

    def sb(self, name, shape, dt=F32):
        return self.es.enter_context(self.nc.sbuf_tensor(name, list(shape), dt))

    def psum(self, name, shape, dt=F32):
        return self.es.enter_context(self.nc.psum_tensor(name, list(shape), dt))

    def ctr(self, name):
        if name not in self._ctrs:
            sem = self.es.enter_context(self.nc.semaphore("sm_" + name))
            self._ctrs[name] = Ctr(sem, name)
        return self._ctrs[name]

    def on(self, which, fn):
        wstate = self.wd[which]

        def body(eng):
            if which in ("v", "s", "g"):
                g = self._guards.get(which)
                if g is None or g._eng is not eng:
                    g = DrainGuard(eng)
                    self._guards[which] = g
                    g.drain()
                eng = g
            fn(eng, lambda ev: _wait(eng, wstate, ev))
        getattr(self.blk, self.ENG[which])(body)

    def dma(self, eng, ctr, out, in_, **kw):
        inst = eng.dma_start(out=out, in_=in_, **kw)
        inst.then_inc(ctr.sem, 16)
        ctr.n += 16
        return ctr.evt()

    def idma(self, eng, ctr, **kw):
        inst = eng.indirect_dma_start(**kw)
        inst.then_inc(ctr.sem, 16)
        ctr.n += 16
        return ctr.evt()

    def step(self, ctr, inst, k=1):
        inst.then_inc(ctr.sem, k)
        ctr.n += k
        return ctr.evt()

    def build(self):
        s = self
        s.x_own = s.din("x_own", [TOK_PER_CORE, D])
        s.x_full = s.din("x_full", [B * L, D])
        s.i_WqT = s.din("WqT", [D, D]); s.i_WkT = s.din("WkT", [D, D])
        s.i_WvT = s.din("WvT", [D, D]); s.i_WoT = s.din("WoT", [D, D])
        s.i_WsmT = s.din("WsmT", [D, 12])
        s.i_gwT = s.din("gwT", [D, 8])
        s.i_wi = s.din("wi_e", [D, 4096]); s.i_wo = s.din("wo_e", [4096, D])
        s.i_mixb = s.din("mixb", [12, 1])
        s.i_mixbr = s.din("mixb_row", [1, 12])
        s.i_convw = {k: s.din(k, [128, 32]) for k in ("wq_c", "wk_c", "wv_c")}
        s.i_onorm2 = s.din("onorm2", [128, 2])
        s.i_lng = s.din("ln_g", [128, 8]); s.i_lnb = s.din("ln_b", [128, 8])
        s.i_ident = s.din("c_ident", [128, 128])
        s.i_iota8 = s.din("c_iota8", [128, 8])
        s.i_msl = s.din("c_mask_sl", [128, 128]); s.i_mui = s.din("c_mask_ui", [128, 128])
        s.i_mbd = s.din("c_mask_bdsl", [128, 128]); s.i_moff = s.din("c_mask_off", [128, 128])
        s.i_corelt = s.din("c_corelt", [8, 1])
        s.i_tokid1 = s.din("c_tokid1", [128, NTILE])
        s.i_identb = s.din("c_identb", [128, 128], BF16)
        s.i_u1 = s.din("c_u1strict", [128, 128], BF16)
        s.i_onesc = s.din("c_onesc", [128, 1], BF16)
        s.i_onesr = s.din("c_onesr", [1, 512], BF16)
        s.i_slotidx = s.din("c_slotidx", [128, 16], U32)
        s.o_out = s.dout("out", [TOK_PER_CORE, D])

        s.cc_cnt_in = s.nc.dram_tensor("cc_cnt_in", [1, 8], F32)
        s.cc_cnt_out = s.nc.dram_tensor("cc_cnt_out", [8, 8], F32)
        s.cc_id_in = s.nc.dram_tensor("cc_id_in", [IDBUF_N], F32)
        s.cc_id_out = s.nc.dram_tensor("cc_id_out", [IDBUF_N], F32)
        s.cc_e_in = s.nc.dram_tensor("cc_e_in", [CAP, D], BF16)
        s.cc_e_out = s.nc.dram_tensor("cc_e_out", [NEXP * CAP, D], BF16)
        s.xh_dram = s.nc.dram_tensor("xh_dram", [NTILE, 8 * 128 * 128], BF16)

        with s.es:
            s._alloc()
            s.blk = s.es.enter_context(s.nc.Block())
            s.wd = {k: {} for k in s.ENG}
            s._emit()
        return s.nc

    def _alloc(self):
        sb = self.sb
        s = self
        s.ident = sb("identS", [128, 128])
        s.identb = sb("identbS", [128, 128], BF16)
        s.iota8 = sb("iota8S", [128, 8])
        s.u1strict = sb("u1S", [128, 128], BF16)
        s.onesc = sb("onescS", [128, 1], BF16)
        s.onesr = sb("onesrS", [1, 512], BF16)
        s.mask_sl = sb("mslS", [128, 128])
        s.mask_ui = sb("muiS", [128, 128])
        s.mask_bdsl = sb("mbdS", [128, 128])
        s.mask_off = sb("moffS", [128, 128])
        s.corelt = sb("coreltS", [8, 1])
        s.tokid1 = sb("tokid1S", [128, NTILE])
        s.slotidx = sb("slotidxS", [128, 16], U32)
        s.onorm2s = sb("onorm2S", [128, 2])
        s.lng = sb("lngS", [128, 8])
        s.lnb = sb("lnbS", [128, 8])
        s.lnbb = sb("lnbbS", [128, 8], BF16)
        s.mixbS = sb("mixbSS", [12, 1])
        s.mixbR = sb("mixbRS", [1, 12])
        s.convw = {k: sb(k + "S", [128, 32]) for k in ("wq_c", "wk_c", "wv_c")}

        s.WsmTs = sb("WsmTsS", [128, 8 * 12], BF16)
        s.gwf = sb("gwfS", [128, 8 * 8])
        s.gwraw = sb("gwrawS", [128, 8 * 8])
        s.gwh = sb("gwhS", [128, 8 * 8], BF16)
        s.gwl = sb("gwlS", [128, 8 * 8], BF16)
        s.bconst = sb("bconstS", [1, 4096], BF16)

        s.r_top1 = sb("r_top1", [128, NTILE])
        s.r_selr = sb("r_selr", [128, NTILE])
        s.r_wgt = sb("r_wgt", [128, NTILE])
        s.r_mu = sb("r_mu", [128, NTILE])
        s.r_var = sb("r_var", [128, NTILE])
        s.r_rs = sb("r_rs", [128, NTILE])
        s.r_rs2 = sb("r_rs2", [128, NTILE])
        s.r_keep = sb("r_keep", [128, NTILE])
        s.r_rank = sb("r_rank", [128, NTILE])
        s.r_gidx = sb("r_gidx", [128, NTILE], U32)
        s.r_sidx = sb("r_sidx", [128, NTILE], U32)

        s.t_w1 = sb("t_w1", [128, NTILE])
        s.t_w2 = sb("t_w2", [128, NTILE])
        s.cntrow = sb("cntrow", [1, 8 * (NTILE + 1)])
        s.cnt8 = sb("cnt8", [8, 8])
        s.prefrow = sb("prefrow", [1, 8])
        s.cbc = sb("cbc", [128, 8])
        s.crow = sb("crow", [1, 8])


        s.t_x = [sb(f"t_x{i}", [128, D]) for i in range(2)]
        s.t_a = sb("t_a", [128, D])
        s.t_bf = [sb(f"t_bf{i}", [128, D], BF16) for i in range(4)]
        s.t_bn = sb("t_bn", [128, 12])
        s.t_ag = sb("t_ag", [128, 2])
        s.t_p8 = [sb(f"t_p8{i}", [8, 128]) for i in range(2)]
        s.t_l8 = [sb(f"t_l8{i}", [128, 8]) for i in range(2)]
        s.t_mx = [sb(f"t_mx{i}", [128, 8]) for i in range(2)]
        s.t_ai = [sb(f"t_ai{i}", [128, 8], U32) for i in range(2)]
        s.t_oh = [sb(f"t_oh{i}", [128, 8], BF16) for i in range(2)]
        s.t_ohf = [sb(f"t_ohf{i}", [128, 8]) for i in range(2)]
        s.t_s = [sb(f"t_s{i}", [128, 8]) for i in range(6)]
        s.t_zero = sb("t_zero", [128, 129])
        s.slotid = sb("slotid", [128, 16])
        s.gxi = sb("gxi", [128, 16], U32)
        s.f_mu = sb("f_mu", [128, 16])
        s.f_var = sb("f_var", [128, 16])
        s.f_sc = sb("f_sc", [128, 16])
        if "snap" in self.taps:
            s.snap_ps = sb("snap_ps", [128, 8])
            s.snap_oh = sb("snap_oh", [128, 8], BF16)
            s.snap_l8 = sb("snap_l8", [128, 8])

        s.pb = [self.psum(f"pb{i}", [128, 512]) for i in range(8)]

        s.es_s1x = ExitStack()
        s.xhb = [s.es_s1x.enter_context(s.nc.sbuf_tensor(f"xhb{i}", [128, 8 * 128], BF16))
                 for i in range(2)]
        s.xlb = [s.es_s1x.enter_context(s.nc.sbuf_tensor(f"xlb{i}", [128, 8 * 128], BF16))
                 for i in range(2)]
        s.es_s1x.close()

        s.es_ffn = ExitStack()

        def sbf(name, shape, dt=F32):
            return s.es_ffn.enter_context(s.nc.sbuf_tensor(name, list(shape), dt))
        s.d_t = [sbf(f"d_t{i}", [128, 8 * 256], BF16) for i in range(2)]
        s.h_t = [sbf("h_t0", [128, 32 * 256], BF16)]
        s.e_outb = [sbf(f"e_outb{i}", [128, D], BF16) for i in range(2)]
        s.f_x = [sbf(f"f_x{i}", [128, D]) for i in range(2)]
        s.wis = sbf("wisS", [128, 8 * 4096], BF16)
        s.wos = sbf("wosS", [128, 32 * D], BF16)
        self._ffn_freed = False

    def _emit(self):
        s = self
        ev = {}
        s.ev = ev
        C = s.ctr
        s.S = {k: C(k) for k in ("t", "v", "s", "g")}

        def ld_sy(eng, w):
            cc = C("q_const")
            for src, dst in [(s.i_ident, s.ident), (s.i_iota8, s.iota8), (s.i_msl, s.mask_sl),
                             (s.i_mui, s.mask_ui), (s.i_mbd, s.mask_bdsl), (s.i_moff, s.mask_off),
                             (s.i_corelt, s.corelt), (s.i_tokid1, s.tokid1), (s.i_identb, s.identb),
                             (s.i_u1, s.u1strict), (s.i_onesc, s.onesc), (s.i_onesr, s.onesr),
                             (s.i_slotidx, s.slotidx),
                             (s.i_onorm2, s.onorm2s), (s.i_lng, s.lng), (s.i_lnb, s.lnb),
                             (s.i_mixb, s.mixbS), (s.i_mixbr, s.mixbR)]:
                s.dma(eng, cc, dst[:, :], src[:, :])
            for k in ("wq_c", "wk_c", "wv_c"):
                s.dma(eng, cc, s.convw[k][:, :], s.i_convw[k][:, :])
            cg = C("q_gwraw")
            for kc in range(8):
                s.dma(eng, cg, s.gwraw[:, 8 * kc:8 * kc + 8], s.i_gwT[128 * kc:128 * (kc + 1), :])
            ev["consts"] = cc.evt()
            ev["gwraw"] = cg.evt()
        self.on("sy", ld_sy)

        def ld_gp(eng, w):
            cw = C("q_wis")
            for kc in range(8):
                s.dma(eng, cw, s.wis[:, kc * 4096:(kc + 1) * 4096], s.i_wi[128 * kc:128 * (kc + 1), :])
            ev["wis"] = cw.evt()
            co = C("q_wos")
            for kc in range(32):
                s.dma(eng, co, s.wos[:, kc * D:(kc + 1) * D], s.i_wo[128 * kc:128 * (kc + 1), :])
            ev["wos"] = co.evt()
            cf = C("q_gwf")
            for kc in range(8):
                s.dma(eng, cf, s.gwf[:, 8 * kc:8 * kc + 8], s.i_gwT[128 * kc:128 * (kc + 1), :])
            ev["gwf_raw"] = cf.evt()
            cm = C("q_wsm")
            for kc in range(8):
                s.dma(eng, cm, s.WsmTs[:, kc * 12:(kc + 1) * 12], s.i_WsmT[128 * kc:128 * (kc + 1), :])
            ev["wsm"] = cm.evt()
        self.on("g", ld_gp)

        def prep_v(eng, w):
            w(ev["consts"]); w(ev["gwf_raw"])
            for kc in range(8):
                eng.tensor_scalar(out=s.gwf[:, 8 * kc:8 * kc + 8], in0=s.gwf[:, 8 * kc:8 * kc + 8],
                                  scalar1=s.lng[:, kc:kc + 1], scalar2=None, op0=ALU.mult)
            eng.tensor_copy(out=s.lnbb[:, :], in_=s.lnb[:, :])
            eng.drain()
            eng.tensor_copy(out=s.gwh[:, :], in_=s.gwf[:, :])
            eng.drain()
            eng.tensor_tensor(out=s.gwl[:, :], in0=s.gwf[:, :], in1=s.gwh[:, :], op=ALU.subtract)
            ev["gwsplit"] = s.step(s.S["v"], eng.memset(s.t_zero[:, :], 0.0))
        self.on("v", prep_v)

        def crow_t(eng, w):
            w(ev["gwraw"]); w(ev["consts"])
            ps = s.pb[7]
            for kc in range(8):
                inst = eng.matmul(ps[0:1, 0:8], s.lnb[:, kc:kc + 1], s.gwraw[:, 8 * kc:8 * kc + 8],
                                  start=(kc == 0), stop=(kc == 7))
            ev["crow_mm"] = s.step(s.S["t"], inst)
        self.on("t", crow_t)

        def crow_v(eng, w):
            w(ev["crow_mm"])
            ev["crow"] = s.step(s.S["v"], eng.tensor_copy(out=s.crow[:, :], in_=s.pb[7][0:1, 0:8]))
        self.on("v", crow_v)

        def cbc_t(eng, w):
            w(ev["crow"])
            inst = eng.matmul(s.pb[7][0:128, 8:16], s.mask_ui[0:1, 0:128], s.crow[0:1, 0:8],
                              start=True, stop=True)
            ev["cbc_mm"] = s.step(s.S["t"], inst)
        self.on("t", cbc_t)

        def cbc_v(eng, w):
            w(ev["cbc_mm"])
            ev["cbc"] = s.step(s.S["v"], eng.tensor_copy(out=s.cbc[:, :], in_=s.pb[7][0:128, 8:16]))
        self.on("v", cbc_v)

        if self.stage >= 1:
            self._emit_s1()
        if self.stage >= 2:
            self._emit_cc1()
        if self.stage >= 3:
            self._emit_ffn()
        if self.stage >= 4:
            self._emit_attn()
        if self.stage >= 5:
            self._emit_combine()
        if not self._ffn_freed:
            self.es_ffn.close()
            self._ffn_freed = True
        self._emit_taps_and_finish()

    def _emit_s1(self):
        s = self
        ev = s.ev
        C = s.ctr
        SV, ST, SS = s.S["v"], s.S["t"], s.S["s"]
        qx = [C("q_x0"), C("q_x1")]
        qtp = [C("q_tp0"), C("q_tp1")]
        qxs = [C("q_xs0"), C("q_xs1")]

        for t in range(NTILE):
            r = t % 2

            def pa_sy(eng, w, t=t, r=r):
                if t >= 2:
                    w(ev[f"statdone_{t-2}"])
                ev[f"xa_{t}"] = s.dma(eng, qx[r], s.t_x[r][:, :], s.x_own[128 * t:128 * (t + 1), :])
            self.on("sy", pa_sy)

            def pa_v(eng, w, t=t, r=r):
                w(ev[f"xa_{t}"])
                eng.bn_stats(out=s.t_bn[:, 0:6], in_=s.t_x[r][:, 0:512])
                eng.bn_stats(out=s.t_bn[:, 6:12], in_=s.t_x[r][:, 512:1024])
                eng.drain()
                eng.bn_aggr(out=s.t_ag[:, :], in_=s.t_bn[:, :])
                eng.drain()
                eng.tensor_copy(out=s.r_mu[:, t:t + 1], in_=s.t_ag[:, 0:1])
                ev[f"statdone_{t}"] = s.step(SV, eng.tensor_copy(out=s.r_var[:, t:t + 1],
                                                                 in_=s.t_ag[:, 1:2]))
            self.on("v", pa_v)

        def pb_v1(eng, w):
            ev["vareps"] = s.step(SV, eng.tensor_scalar(out=s.t_w1[:, :], in0=s.r_var[:, :],
                                                        scalar1=EPS_LN, scalar2=None, op0=ALU.add))
        self.on("v", pb_v1)

        def pb_s1(eng, w):
            w(ev["vareps"])
            ev["sqrt1"] = s.step(SS, eng.activation(out=s.t_w2[:, :], in_=s.t_w1[:, :], func=AF.Sqrt))
        self.on("s", pb_s1)

        def pb_v2(eng, w):
            w(ev["sqrt1"])
            eng.reciprocal(out=s.r_rs[:, :], in_=s.t_w2[:, :])
            eng.drain()
            eng.tensor_tensor(out=s.t_w1[:, :], in0=s.r_rs[:, :], in1=s.r_rs[:, :], op=ALU.mult)
            ev["v2e"] = s.step(SV, eng.tensor_scalar(out=s.t_w1[:, :], in0=s.t_w1[:, :],
                                                     scalar1=-EPS_LN, scalar2=(1.0 + EPS_LN),
                                                     op0=ALU.mult, op1=ALU.add))
        self.on("v", pb_v2)

        def pb_s2(eng, w):
            w(ev["v2e"])
            ev["sqrt2"] = s.step(SS, eng.activation(out=s.t_w2[:, :], in_=s.t_w1[:, :], func=AF.Sqrt))
        self.on("s", pb_s2)

        def pb_v3(eng, w):
            w(ev["sqrt2"])
            eng.reciprocal(out=s.r_rs2[:, :], in_=s.t_w2[:, :])
            eng.drain()
            ev["stats"] = s.step(SV, eng.tensor_copy(out=s.t_s[0][:, 0:1], in_=s.r_rs2[:, 0:1]))
        self.on("v", pb_v3)

        for t in range(NTILE):
            r = t % 2

            def pbt_sy(eng, w, t=t, r=r):
                if t >= 2:
                    w(ev[f"xbf_{t-2}"])
                else:
                    w(ev[f"statdone_{NTILE - 2 + t}"])
                ev[f"xb_{t}"] = s.dma(eng, qx[r], s.t_x[r][:, :], s.x_own[128 * t:128 * (t + 1), :])
            self.on("sy", pbt_sy)

            def pbt_v(eng, w, t=t, r=r):
                w(ev[f"xb_{t}"]); w(ev["stats"])
                if t >= 2:
                    w(ev[f"xpose_{t-2}"])
                eng.tensor_scalar(out=s.t_bf[r][:, :], in0=s.t_x[r][:, :], scalar1=s.r_mu[:, t:t + 1],
                                  scalar2=s.r_rs[:, t:t + 1], op0=ALU.subtract, op1=ALU.mult)
                eng.tensor_scalar(out=s.t_a[:, :], in0=s.t_x[r][:, :], scalar1=s.r_mu[:, t:t + 1],
                                  scalar2=s.r_rs[:, t:t + 1], op0=ALU.subtract, op1=ALU.mult)
                ev[f"xbf_{t}"] = s.step(SV, eng.tensor_tensor(out=s.t_bf[2 + r][:, :], in0=s.t_a[:, :],
                                                              in1=s.t_bf[r][:, :], op=ALU.subtract))
            self.on("v", pbt_v)

            def pbt_sc(eng, w, t=t, r=r):
                w(ev[f"xbf_{t}"])
                if t >= 2:
                    w(ev[f"lgmm_{t-2}"])
                    w(ev[f"xhst_{t-2}"])
                for kc in range(8):
                    s.dma(eng, qtp[r], s.xhb[r][:, 128 * kc:128 * (kc + 1)],
                          s.t_bf[r][:, 128 * kc:128 * (kc + 1)], transpose=True)
                    s.dma(eng, qtp[r], s.xlb[r][:, 128 * kc:128 * (kc + 1)],
                          s.t_bf[2 + r][:, 128 * kc:128 * (kc + 1)], transpose=True)
                ev[f"xpose_{t}"] = qtp[r].evt()
            self.on("s", pbt_sc)

            def pbt_st(eng, w, t=t, r=r):
                w(ev[f"xpose_{t}"])
                ev[f"xhst_{t}"] = s.dma(eng, qxs[r], s.xh_dram.ap()[t, :], s.xhb[r][:, :])
            self.on("sy", pbt_st)

            def pbt_t1(eng, w, t=t, r=r):
                w(ev[f"xpose_{t}"]); w(ev["gwsplit"]); w(ev["cbc"])
                if t >= 2:
                    w(ev[f"pev_{t-2}"])
                psl = s.pb[4 if r == 0 else 7]
                first = True
                for kc in range(8):
                    xs = s.xhb[r][:, 128 * kc:128 * (kc + 1)]
                    xl = s.xlb[r][:, 128 * kc:128 * (kc + 1)]
                    eng.matmul(psl[0:8, 0:128], s.gwh[:, 8 * kc:8 * kc + 8], xs, start=first, stop=False)
                    first = False
                    eng.matmul(psl[0:8, 0:128], s.gwl[:, 8 * kc:8 * kc + 8], xs, start=False, stop=False)
                    inst = eng.matmul(psl[0:8, 0:128], s.gwh[:, 8 * kc:8 * kc + 8], xl, start=False,
                                      stop=(kc == 7))
                ev[f"lgmm_{t}"] = s.step(ST, inst)
            self.on("t", pbt_t1)

            def pbt_v2(eng, w, t=t, r=r):
                w(ev[f"lgmm_{t}"])
                psl = s.pb[4 if r == 0 else 7]
                ev[f"pev_{t}"] = s.step(SV, eng.tensor_copy(out=s.t_p8[r][:, :], in_=psl[0:8, 0:128]))
            self.on("v", pbt_v2)

            def pbt_t2(eng, w, t=t, r=r):
                w(ev[f"pev_{t}"])
                if t >= 2:
                    w(ev[f"rout_{t-2}"])
                inst = eng.matmul(s.pb[5][0:128, 8 * r:8 * r + 8], s.t_p8[r][:, :], s.ident[0:8, 0:8],
                                  start=True, stop=True)
                ev[f"ptm_{t}"] = s.step(ST, inst)
            self.on("t", pbt_t2)

            def pbt_v3(eng, w, t=t, r=r):
                w(ev[f"ptm_{t}"])
                eng.scalar_tensor_tensor(out=s.t_l8[r][:, :], in0=s.pb[5][0:128, 8 * r:8 * r + 8],
                                         scalar=s.r_rs2[:, t:t + 1], in1=s.cbc[:, :],
                                         op0=ALU.mult, op1=ALU.add)
                eng.drain()
                eng.max(out=s.t_mx[r][:, :], in_=s.t_l8[r][:, :])
                eng.drain()
                eng.max_index(out=s.t_ai[r][:, :], in_max=s.t_mx[r][:, :], in_values=s.t_l8[r][:, :])
                eng.drain()
                eng.tensor_copy(out=s.r_top1[:, t:t + 1], in_=s.t_ai[r][:, 0:1])
                eng.tensor_scalar(out=s.t_s[0][:, r:r + 1], in0=s.t_mx[r][:, 0:1], scalar1=-1.0,
                                  scalar2=None, op0=ALU.mult)
                eng.tensor_scalar(out=s.t_oh[r][:, :], in0=s.iota8[:, :],
                                  scalar1=s.r_top1[:, t:t + 1], scalar2=None, op0=ALU.is_equal)
                eng.drain()
                ev[f"rout_{t}"] = s.step(SV, eng.tensor_copy(out=s.t_ohf[r][:, :], in_=s.t_oh[r][:, :]))
            self.on("v", pbt_v3)

            def pbt_s2(eng, w, t=t, r=r):
                w(ev[f"rout_{t}"])
                ev[f"esum_{t}"] = s.step(SS, eng.activation(
                    out=s.t_s[1][:, 0:8], in_=s.t_l8[r][:, :], func=AF.Exp,
                    bias=s.t_s[0][:, r:r + 1], accum_out=s.t_s[2][:, r:r + 1]))
            self.on("s", pbt_s2)

            def pbt_v4(eng, w, t=t, r=r):
                w(ev[f"esum_{t}"])
                ev[f"wgt_{t}"] = s.step(SV, eng.reciprocal(out=s.r_wgt[:, t:t + 1],
                                                           in_=s.t_s[2][:, r:r + 1]))
            self.on("v", pbt_v4)

            def pbt_t3(eng, w, t=t, r=r):
                w(ev[f"rout_{t}"])
                if t >= 2:
                    w(ev[f"selr_{t-2}"])
                eng.matmul(s.pb[5][0:128, 16 + 8 * r:24 + 8 * r], s.u1strict[:, :], s.t_oh[r][:, :],
                           start=True, stop=True)
                inst = eng.matmul(s.pb[6][0:1, 8 * t:8 * t + 8], s.onesc[:, 0:1], s.t_oh[r][:, :],
                                  start=(t == 0), stop=(t == NTILE - 1))
                ev[f"cnt_{t}"] = s.step(ST, inst)
            self.on("t", pbt_t3)

            def pbt_v5(eng, w, t=t, r=r):
                w(ev[f"cnt_{t}"])
                eng.tensor_copy(out=s.t_s[3][:, 0:8], in_=s.pb[5][0:128, 16 + 8 * r:24 + 8 * r])
                eng.drain()
                eng.tensor_tensor(out=s.t_s[3][:, 0:8], in0=s.t_s[3][:, 0:8],
                                  in1=s.t_ohf[r][:, :], op=ALU.mult)
                eng.drain()
                eng.tensor_reduce(out=s.t_s[4][:, 0:1], in_=s.t_s[3][:, 0:8],
                                  axis=AX.X, op=ALU.add)
                eng.drain()
                eng.tensor_copy(out=s.r_selr[:, t:t + 1], in_=s.t_s[4][:, 0:1])
                if t == self.tsnap and "snap" in self.taps:
                    eng.tensor_copy(out=s.snap_ps[:, 0:8], in_=s.pb[5][0:128, 16 + 8 * r:24 + 8 * r])
                    eng.tensor_copy(out=s.snap_oh[:, :], in_=s.t_oh[r][:, :])
                    eng.tensor_copy(out=s.snap_l8[:, :], in_=s.t_l8[r][:, :])
                ev[f"selr_{t}"] = s.step(SV, eng.tensor_copy(out=s.t_s[3][:, 0:1], in_=s.t_s[3][:, 0:1]))
            self.on("v", pbt_v5)

        def cnts_v(eng, w):
            w(ev[f"cnt_{NTILE-1}"])
            eng.memset(s.cntrow[:, 0:8], 0.0)
            eng.tensor_copy(out=s.cntrow[:, 8:8 + 8 * NTILE], in_=s.pb[6][0:1, 0:8 * NTILE])
            for t in range(NTILE):
                eng.drain()
                inst = eng.tensor_tensor(out=s.cntrow[:, 8 * (t + 1):8 * (t + 2)],
                                         in0=s.cntrow[:, 8 * (t + 1):8 * (t + 2)],
                                         in1=s.cntrow[:, 8 * t:8 * (t + 1)], op=ALU.add)
            eng.drain()
            ev["cntincl"] = s.step(SV, inst)
        self.on("v", cnts_v)

    def _emit_cc1(self):
        s = self
        ev = s.ev
        C = s.ctr
        SV, ST = s.S["v"], s.S["t"]
        ccs = C("cc")

        def cc_g(eng, w):
            w(ev["cntincl"])
            c1 = C("q_ccin")
            s.dma(eng, c1, s.cc_cnt_in.ap(), s.cntrow[:, 8 * NTILE:8 * (NTILE + 1)])
            eng.wait_ge(c1.sem, c1.n)
            if self.sub & 1:
                inst = eng.collective_compute("AllGather", ALU.bypass,
                                              replica_groups=[list(range(NCORES))],
                                              ins=[s.cc_cnt_in.ap().opt()],
                                              outs=[s.cc_cnt_out.ap().opt()])
                s.step(ccs, inst)
                eng.wait_ge(ccs.sem, ccs.n)
                c2 = C("q_ccout")
                ev["cnt8"] = s.dma(eng, c2, s.cnt8[:, :], s.cc_cnt_out.ap())
            else:
                c2 = C("q_ccout")
                for cc_ in range(8):
                    ev["cnt8"] = s.dma(eng, c2, s.cnt8[cc_:cc_ + 1, :], s.cc_cnt_in.ap())
        self.on("g", cc_g)

        def pref_t(eng, w):
            w(ev["cnt8"])
            inst = eng.matmul(s.pb[5][0:1, 32:40], s.corelt[:, :], s.cnt8[:, :], start=True, stop=True)
            ev["prefmm"] = s.step(ST, inst)
        self.on("t", pref_t)

        def pref_v(eng, w):
            w(ev["prefmm"])
            ev["prefrow"] = s.step(SV, eng.tensor_copy(out=s.prefrow[:, :], in_=s.pb[5][0:1, 32:40]))
        self.on("v", pref_v)

        for t in range(NTILE if (self.sub & 2) else 0):
            r = t % 2

            def p2_t(eng, w, t=t, r=r):
                w(ev["prefrow"])
                if t >= 1 and not (self.sub & 16):
                    w(ev[f"p2v_{t-1}"])
                ps = s.pb[5]
                sl = slice(40 + 8 * r, 48 + 8 * r)
                eng.matmul(ps[0:128, sl], s.mask_ui[0:1, 0:128], s.cntrow[0:1, 8 * t:8 * (t + 1)],
                           start=True, stop=False)
                inst = eng.matmul(ps[0:128, sl], s.mask_ui[0:1, 0:128], s.prefrow[0:1, 0:8],
                                  start=False, stop=True)
                ev[f"p2mm_{t}"] = s.step(ST, inst)
            self.on("t", p2_t)

            if self.sub & 16:
                continue

            def p2_v(eng, w, t=t, r=r):
                w(ev[f"p2mm_{t}"])
                sl = slice(40 + 8 * r, 48 + 8 * r)
                if self.sub & 64:
                    ev[f"p2v_{t}"] = s.step(SV, eng.tensor_copy(out=s.t_s[5][:, 0:1],
                                                                in_=s.iota8[:, 0:1]))
                    return
                eng.tensor_scalar(out=s.t_oh[r][:, :], in0=s.iota8[:, :], scalar1=s.r_top1[:, t:t + 1],
                                  scalar2=None, op0=ALU.is_equal)
                eng.tensor_copy(out=s.t_ohf[r][:, :], in_=s.t_oh[r][:, :])
                if self.sub & 128:
                    ev[f"p2v_{t}"] = s.step(SV, eng.tensor_copy(out=s.t_s[5][:, 0:1],
                                                                in_=s.iota8[:, 0:1]))
                    return
                eng.tensor_copy(out=s.t_s[3][:, 0:8], in_=s.pb[5][0:128, sl])
                eng.tensor_tensor(out=s.t_s[3][:, 0:8], in0=s.t_s[3][:, 0:8], in1=s.t_ohf[r][:, :],
                                  op=ALU.mult)
                eng.drain()
                eng.tensor_reduce(out=s.t_s[4][:, 0:1], in_=s.t_s[3][:, 0:8], axis=AX.X, op=ALU.add)
                eng.drain()
                if self.sub & 256:
                    ev[f"p2v_{t}"] = s.step(SV, eng.tensor_copy(out=s.t_s[5][:, 0:1],
                                                                in_=s.iota8[:, 0:1]))
                    return
                eng.tensor_tensor(out=s.r_rank[:, t:t + 1], in0=s.r_selr[:, t:t + 1],
                                  in1=s.t_s[4][:, 0:1], op=ALU.add)
                if self.sub & 32:
                    ev[f"p2v_{t}"] = s.step(SV, eng.tensor_copy(out=s.r_sidx[:, t:t + 1],
                                                                in_=s.t_s[5][:, 0:1]))
                    return
                eng.tensor_scalar(out=s.r_keep[:, t:t + 1], in0=s.r_rank[:, t:t + 1],
                                  scalar1=float(CAP) - 0.5, scalar2=None, op0=ALU.is_le)
                eng.tensor_tensor(out=s.r_wgt[:, t:t + 1], in0=s.r_wgt[:, t:t + 1],
                                  in1=s.r_keep[:, t:t + 1], op=ALU.mult)
                eng.tensor_scalar(out=s.t_s[4][:, 0:1], in0=s.r_rank[:, t:t + 1],
                                  scalar1=float(CAP - 1), scalar2=None, op0=ALU.min)
                eng.scalar_tensor_tensor(out=s.t_s[5][:, 0:1], in0=s.r_top1[:, t:t + 1],
                                         scalar=float(CAP), in1=s.t_s[4][:, 0:1],
                                         op0=ALU.mult, op1=ALU.add)
                eng.tensor_copy(out=s.r_gidx[:, t:t + 1], in_=s.t_s[5][:, 0:1])
                eng.scalar_tensor_tensor(out=s.t_s[5][:, 0:1], in0=s.r_top1[:, t:t + 1],
                                         scalar=float(CAP), in1=s.r_rank[:, t:t + 1],
                                         op0=ALU.mult, op1=ALU.add)
                eng.scalar_tensor_tensor(out=s.t_s[5][:, 0:1], in0=s.r_keep[:, t:t + 1],
                                         scalar=-100000.0, in1=s.t_s[5][:, 0:1],
                                         op0=ALU.mult, op1=ALU.add)
                eng.tensor_scalar(out=s.t_s[5][:, 0:1], in0=s.t_s[5][:, 0:1], scalar1=100000.0,
                                  scalar2=None, op0=ALU.add)
                ev[f"p2v_{t}"] = s.step(SV, eng.tensor_copy(out=s.r_sidx[:, t:t + 1],
                                                            in_=s.t_s[5][:, 0:1]))
            self.on("v", p2_v)

        if not (self.sub & 4):
            return

        def idz_sy(eng, w):
            w(ev["gwsplit"])
            cz = C("q_idz")
            idv = s.cc_id_in.ap().rearrange("(p f) -> p f", p=128)
            ev["idz"] = s.dma(eng, cz, idv, s.t_zero[:, 0:IDBUF_N // 128])
        self.on("sy", idz_sy)

        def scat_g(eng, w):
            w(ev["idz"])
            w(ev[f"p2v_{NTILE-1}"])
            csc = C("q_scat")
            for t in range(NTILE):
                s.idma(eng, csc,
                       out=s.cc_id_in.ap().rearrange("(n o) -> n o", o=1),
                       out_offset=IOoA(ap=s.r_sidx[:, t:t + 1], axis=0),
                       in_=s.tokid1[:, t:t + 1], in_offset=None,
                       bounds_check=NEXP * CAP - 1, oob_is_err=False)
            eng.wait_ge(csc.sem, csc.n)
            ccs = s.ctr("cc")
            inst = eng.collective_compute("AllReduce", ALU.add, replica_groups=[list(range(NCORES))],
                                          ins=[s.cc_id_in.ap().opt()], outs=[s.cc_id_out.ap().opt()])
            s.step(ccs, inst)
            eng.wait_ge(ccs.sem, ccs.n)
            if self.sub & 8:
                cgt = C("q_slotg")
                for c in range(16):
                    s.idma(eng, cgt,
                           out=s.slotid[:, c:c + 1], out_offset=None,
                           in_=s.cc_id_out.ap().rearrange("(n o) -> n o", o=1),
                           in_offset=IOoA(ap=s.slotidx[:, c:c + 1], axis=0))
                ev["slotid"] = cgt.evt()
        self.on("g", scat_g)

        if not (self.sub & 8):
            return

        def gxi_v(eng, w):
            w(ev["slotid"])
            eng.tensor_scalar(out=s.f_sc[:, :], in0=s.slotid[:, :], scalar1=-1.0, scalar2=0.0,
                              op0=ALU.add, op1=ALU.max)
            ev["gxi"] = s.step(s.S["v"], eng.tensor_copy(out=s.gxi[:, :], in_=s.f_sc[:, :]))
        self.on("v", gxi_v)

    def _emit_ffn(self):
        s = self
        ev = s.ev
        C = s.ctr
        SV, ST, SS, SG = s.S["v"], s.S["t"], s.S["s"], s.S["g"]

        for ch in range(8):
            def bc_t1(eng, w, ch=ch):
                w(ev["wis"]); w(ev["consts"])
                if ch > 0:
                    w(ev[f"bcev_{ch-1}"])
                for kc in range(8):
                    inst = eng.matmul(s.pb[0][0:1, 0:512], s.lnbb[:, kc:kc + 1],
                                      s.wis[:, kc * 4096 + 512 * ch: kc * 4096 + 512 * (ch + 1)],
                                      start=(kc == 0), stop=(kc == 7))
                ev[f"bcmm_{ch}"] = s.step(ST, inst)
            self.on("t", bc_t1)

            def bc_v1(eng, w, ch=ch):
                w(ev[f"bcmm_{ch}"])
                ev[f"bcev_{ch}"] = s.step(SV, eng.tensor_copy(out=s.bconst[:, 512 * ch:512 * (ch + 1)],
                                                              in_=s.pb[0][0:1, 0:512]))
            self.on("v", bc_v1)

        def wis_v(eng, w):
            w(ev["bcev_7"]); w(ev["consts"])
            for kc in range(8):
                inst = eng.tensor_scalar(out=s.wis[:, kc * 4096:(kc + 1) * 4096],
                                         in0=s.wis[:, kc * 4096:(kc + 1) * 4096],
                                         scalar1=s.lng[:, kc:kc + 1], scalar2=None, op0=ALU.mult)
            ev["wis_scaled"] = s.step(SV, inst)
        self.on("v", wis_v)

        qgx = [C("q_gx0"), C("q_gx1")]
        qdt = [C("q_dt0"), C("q_dt1")]
        qes = [C("q_es0"), C("q_es1")]
        NT2 = 8

        def emit_tile(c):
            rc = c % 2

            def fg_g(eng, w, c=c, rc=rc):
                w(ev["gxi"])
                if c >= 2:
                    w(ev[f"fd_{c-2}"])
                ev[f"fgx_{c}"] = s.idma(eng, qgx[rc],
                                        out=s.f_x[rc][:, :], out_offset=None,
                                        in_=s.x_full[:, :],
                                        in_offset=IOoA(ap=s.gxi[:, c:c + 1], axis=0))
            self.on("g", fg_g)

            def fs_v(eng, w, c=c, rc=rc):
                w(ev[f"fgx_{c}"])
                eng.bn_stats(out=s.t_bn[:, 0:6], in_=s.f_x[rc][:, 0:512])
                eng.bn_stats(out=s.t_bn[:, 6:12], in_=s.f_x[rc][:, 512:1024])
                eng.drain()
                eng.bn_aggr(out=s.t_ag[:, :], in_=s.t_bn[:, :])
                eng.drain()
                eng.tensor_copy(out=s.f_mu[:, c:c + 1], in_=s.t_ag[:, 0:1])
                ev[f"fstat_{c}"] = s.step(SV, eng.tensor_scalar(
                    out=s.t_s[0][:, rc:rc + 1], in0=s.t_ag[:, 1:2],
                    scalar1=EPS_LN, scalar2=None, op0=ALU.add))
            self.on("v", fs_v)

            def fs_s(eng, w, c=c, rc=rc):
                w(ev[f"fstat_{c}"])
                ev[f"fsq_{c}"] = s.step(SS, eng.activation(out=s.t_s[1][:, rc:rc + 1],
                                                           in_=s.t_s[0][:, rc:rc + 1], func=AF.Sqrt))
            self.on("s", fs_s)

            def fs_v2(eng, w, c=c, rc=rc):
                w(ev[f"fsq_{c}"])
                eng.reciprocal(out=s.t_s[2][:, rc:rc + 1], in_=s.t_s[1][:, rc:rc + 1])
                eng.drain()
                eng.tensor_tensor(out=s.t_s[3][:, rc:rc + 1], in0=s.t_s[2][:, rc:rc + 1],
                                  in1=s.t_s[2][:, rc:rc + 1], op=ALU.mult)
                eng.tensor_scalar(out=s.t_s[3][:, rc:rc + 1], in0=s.t_s[3][:, rc:rc + 1],
                                  scalar1=-EPS_LN, scalar2=(1.0 + EPS_LN), op0=ALU.mult, op1=ALU.add)
                ev[f"fv2_{c}"] = s.step(SV, eng.tensor_copy(out=s.t_s[4][:, rc:rc + 1],
                                                            in_=s.t_s[3][:, rc:rc + 1]))
            self.on("v", fs_v2)

            def fs_s2(eng, w, c=c, rc=rc):
                w(ev[f"fv2_{c}"])
                ev[f"fsq2_{c}"] = s.step(SS, eng.activation(out=s.t_s[5][:, rc:rc + 1],
                                                            in_=s.t_s[4][:, rc:rc + 1], func=AF.Sqrt))
            self.on("s", fs_s2)

            def fd_v(eng, w, c=c, rc=rc):
                w(ev[f"fsq2_{c}"])
                if c >= 4:
                    w(ev[f"hmm_{32 * (c // 2 - 2) + 31}"])
                eng.reciprocal(out=s.t_s[4][:, rc:rc + 1], in_=s.t_s[5][:, rc:rc + 1])
                eng.drain()
                eng.tensor_tensor(out=s.f_sc[:, c:c + 1], in0=s.t_s[2][:, rc:rc + 1],
                                  in1=s.t_s[4][:, rc:rc + 1], op=ALU.mult)
                eng.drain()
                ev[f"fd_{c}"] = s.step(SV, eng.tensor_scalar(
                    out=s.t_bf[rc][:, :], in0=s.f_x[rc][:, :],
                    scalar1=s.f_mu[:, c:c + 1], scalar2=s.f_sc[:, c:c + 1],
                    op0=ALU.subtract, op1=ALU.mult))
            self.on("v", fd_v)

            def fd_sc(eng, w, c=c, rc=rc):
                w(ev[f"fd_{c}"])
                if c < 4 and self.stage >= 1:
                    w(ev[f"xhst_{NTILE-2}"]); w(ev[f"xhst_{NTILE-1}"])
                    w(ev[f"lgmm_{NTILE-1}"])
                if c >= 4:
                    w(ev[f"hmm_{32 * (c // 2 - 2) + 31}"])
                st2 = (c // 2) % 2
                for kc in range(8):
                    s.dma(eng, qdt[rc], s.d_t[st2][:, kc * 256 + rc * 128: kc * 256 + rc * 128 + 128],
                          s.t_bf[rc][:, 128 * kc:128 * (kc + 1)], transpose=True)
                ev[f"fdt_{c}"] = qdt[rc].evt()
            self.on("s", fd_sc)

        def emit_step(st):
            p = st % 2

            for ic4 in range(8):
                def fh_t(eng, w, st=st, p=p, ic4=ic4):
                    if ic4 == 0:
                        w(ev[f"fdt_{2*st}"]); w(ev[f"fdt_{2*st+1}"]); w(ev["wis_scaled"])
                        if st == 0:
                            w(ev[f"pev_{NTILE-1}"])
                    for ic in range(4 * ic4, 4 * ic4 + 4):
                        g = 32 * st + ic
                        if g >= 4:
                            w(ev[f"hgel_{g-4}"])
                        psl = s.pb[(4, 7, 1, 2)[ic % 4]]
                        for kc in range(8):
                            eng.matmul(psl[0:128, 0:256],
                                       s.wis[:, kc * 4096 + 128 * ic:kc * 4096 + 128 * (ic + 1)],
                                       s.d_t[p][:, kc * 256:(kc + 1) * 256], start=(kc == 0), stop=False)
                        inst = eng.matmul(psl[0:128, 0:256], s.bconst[0:1, 128 * ic:128 * (ic + 1)],
                                          s.onesr[0:1, 0:256], start=False, stop=True)
                        ev[f"hmm_{g}"] = s.step(ST, inst)
                self.on("t", fh_t)

                def fh_s(eng, w, st=st, p=p, ic4=ic4):
                    for ic in range(4 * ic4, 4 * ic4 + 4):
                        g = 32 * st + ic
                        w(ev[f"hmm_{g}"])
                        psl = s.pb[(4, 7, 1, 2)[ic % 4]]
                        inst = eng.activation(out=s.h_t[0][:, 256 * ic:256 * (ic + 1)],
                                              in_=psl[0:128, 0:256], func=AF.Gelu)
                        ev[f"hgel_{g}"] = s.step(SS, inst)
                self.on("s", fh_s)

            for mt in range(2):
                def fe_t(eng, w, st=st, p=p, mt=mt):
                    if mt == 0:
                        w(ev[f"hgel_{32*st+31}"]); w(ev["wos"])
                    if st == 0 and mt == 0:
                        w(ev[f"p2v_{NTILE-1}"]); w(ev["cntincl"])
                    prev = (st, mt - 1) if mt == 1 else (st - 1, 1)
                    if prev[0] >= 0:
                        w(ev[f"eoev_{prev[0]}_{prev[1]}"])
                    for nn_ in range(2):
                        psl = s.pb[5 if nn_ == 0 else 6]
                        for kc in range(32):
                            inst = eng.matmul(psl[0:128, 0:512],
                                              s.h_t[0][:, 256 * kc + 128 * mt:256 * kc + 128 * mt + 128],
                                              s.wos[:, kc * D + 512 * nn_: kc * D + 512 * (nn_ + 1)],
                                              start=(kc == 0), stop=(kc == 31))
                        ev[f"eomm_{st}_{mt}_{nn_}"] = s.step(ST, inst)
                self.on("t", fe_t)

                def fe_v(eng, w, st=st, p=p, mt=mt):
                    w(ev[f"eomm_{st}_{mt}_0"]); w(ev[f"eomm_{st}_{mt}_1"])
                    if st > 0:
                        w(ev[f"eost_{st-1}_{mt}"])
                    eng.tensor_copy(out=s.e_outb[mt][:, 0:512], in_=s.pb[5][0:128, 0:512])
                    inst = eng.tensor_copy(out=s.e_outb[mt][:, 512:1024], in_=s.pb[6][0:128, 0:512])
                    ev[f"eoev_{st}_{mt}"] = s.step(SV, inst)
                self.on("v", fe_v)

                def fe_g(eng, w, st=st, p=p, mt=mt):
                    w(ev[f"eoev_{st}_{mt}"])
                    ev[f"eost_{st}_{mt}"] = s.dma(
                        eng, qes[mt],
                        s.cc_e_in.ap()[256 * st + 128 * mt:256 * st + 128 * (mt + 1), :],
                        s.e_outb[mt][:, :])
                self.on("g", fe_g)

        for st in range(NT2):
            emit_tile(2 * st)
            emit_tile(2 * st + 1)
            emit_step(st)

        def eag_g(eng, w):
            for st in range(NT2):
                for mt in range(2):
                    w(ev[f"eost_{st}_{mt}"])
            ccs = s.ctr("cc")
            inst = eng.collective_compute("AllGather", ALU.bypass, replica_groups=[list(range(NCORES))],
                                          ins=[s.cc_e_in.ap().opt()], outs=[s.cc_e_out.ap().opt()])
            ev["eall"] = s.step(ccs, inst)
        self.on("g", eag_g)

    def _emit_attn(self):
        s = self
        ev = s.ev
        C = s.ctr
        SV, ST, SS, SG = s.S["v"], s.S["t"], s.S["s"], s.S["g"]

        s.es_ffn.close()
        s._ffn_freed = True
        s.es_att = ExitStack()

        def sba(name, shape, dt=F32):
            return s.es_att.enter_context(s.nc.sbuf_tensor(name, list(shape), dt))
        s.WqTs = sba("WqTsS", [128, 8 * D], BF16)
        s.WkTs = sba("WkTsS", [128, 8 * D], BF16)
        s.WvTs = sba("WvTsS", [128, 8 * D], BF16)
        s.WoTs = sba("WoTsS", [128, 8 * D], BF16)
        s.xh_g = sba("xh_gS", [128, 8 * 512], BF16)
        s.S_f32 = sba("S_f32S", [128, 4 * 512])
        s.sqs = sba("sqsS", [128, 256], BF16)
        s.q_fm = sba("q_fmS", [128, 8 * 512], BF16)
        s.k_fm = sba("k_fmS", [128, 8 * 512], BF16)
        s.k_tm = sba("k_tmS", [128, 4 * 1024], BF16)
        s.v_tm = sba("v_tmS", [128, 4 * 1024], BF16)
        s.v_fm = sba("v_fmS", [128, 8 * 512], BF16)
        s.praw = [sba(f"praw{i}", [128, 516], BF16) for i in range(2)]
        s.cvt = [sba(f"cvt{i}", [128, 512], BF16) for i in range(2)]
        s.tails = sba("tailsS", [128, 3 * 8 * 4])
        s.smrowb = sba("smrowbS", [12, 512], BF16)
        s.smcol = sba("smcolS", [128, 4 * 12])
        s.Aoff = sba("AoffS", [128, 16 * 128], BF16)
        s.Wt = sba("WtS", [128, 16 * 128], BF16)
        s.Tt = [sba("TtS", [128, 128], BF16), sba("Tt2S", [128, 128], BF16)]
        s.Abd = sba("AbdS", [128, 128])
        s.zerob = sba("zerobS", [1, 256], BF16)
        s.gdec = sba("gdecS", [1, 2048], BF16)
        s.fs_t = sba("fs_tS", [128, 1024])
        s.S_sb = sba("S_sbS", [128, 4 * 512], BF16)
        s.G_sb = [sba(f"G_sbS{h}", [128, 256], BF16) for h in range(4)]
        s.JT_sb = [sba(f"JT_sbS{h}", [128, 128], BF16) for h in range(4)]
        s.g0_sb = [sba(f"g0_sbS{h}", [128, 256], BF16) for h in range(4)]
        s.h1_sb = [sba(f"h1_sbS{h}", [128, 256], BF16) for h in range(4)]
        s.u2_sb = [sba(f"u2_sbS{h}", [128, 256], BF16) for h in range(4)]
        s.aT_sb = [sba(f"aT_sbS{h}", [128, 128], BF16) for h in range(4)]
        s.bv_sb = [sba(f"bv_sbS{h}", [128, 256], BF16) for h in range(4)]
        s.o_tm = s.t_bf[1]
        s.onf = s.t_bf[2]
        s.ema_t = s.t_bf[3]
        s.gbc = [sba("gbcS", [128, 512], BF16)]
        s.d1b = [sba("d1bS", [128, 512], BF16)]
        s.ema_st = sba("ema_stS", [128, 8])
        s.kq2 = s.t_bf[0]
        s.l2r = sba("l2rS", [1, 512])
        s.colsc = sba("colscS", [128, 4 * 16])
        s.gE = [sba(f"gE{i}", [128, D], BF16) for i in range(2)]

        qxg = [C("q_xg0"), C("q_xg1")]
        qkt = [C("q_kt0"), C("q_kt1")]
        qfsi = [C("q_fsi0"), C("q_fsi1")]
        qfso = [C("q_fso0"), C("q_fso1")]
        qtt = [C("q_tt0"), C("q_tt1")]
        qoe = [C("q_oe0"), C("q_oe1")]
        qxr = [C("q_xr0"), C("q_xr1")]
        qos = [C("q_os0"), C("q_os1")]
        qge = [C("q_ge0"), C("q_ge1")]
        qwa = C("q_wat")

        pbev = {4: ev.get(f"pev_{NTILE-1}"), 5: None, 6: None, 7: ev.get(f"pev_{NTILE-1}")}
        s.pbS = {h: None for h in range(4)}

        def wat_g(eng, w):
            w(ev["eomm_7_1_1"])
            w(ev["eost_7_0"]); w(ev["eost_7_1"])
            w(ev[f"hgel_{8*32-1}"])
            for W_in, W_sb in [(s.i_WqT, s.WqTs), (s.i_WkT, s.WkTs), (s.i_WvT, s.WvTs),
                               (s.i_WoT, s.WoTs)]:
                for kc in range(8):
                    s.dma(eng, qwa, W_sb[:, kc * D:(kc + 1) * D], W_in[128 * kc:128 * (kc + 1), :])
            ev["wat"] = qwa.evt()
        self.on("g", wat_g)

        def wo_v(eng, w):
            w(ev["wat"]); w(ev["consts"])
            eng.memset(s.zerob[:, :], 0.0)
            eng.memset(s.t_zero[:, 120:121], EPS_L2)
            eng.memset(s.t_zero[:, 121:122], EPS_RMS)
            eng.memset(s.Tt[0][:, :], 0.0)
            inst = eng.memset(s.Tt[1][:, :], 0.0)
            ev["ttinit"] = s.step(SV, inst)
            for kc in range(8):
                inst = eng.tensor_scalar(out=s.WoTs[:, kc * D:(kc + 1) * D],
                                         in0=s.WoTs[:, kc * D:(kc + 1) * D],
                                         scalar1=s.onorm2s[:, kc % 2:kc % 2 + 1],
                                         scalar2=None, op0=ALU.mult)
            ev["wofold"] = s.step(SV, inst)
        self.on("v", wo_v)

        for sg in range(SEQ_PER_CORE * NGROUP):
            sq, g = divmod(sg, NGROUP)
            self._attn_pphase(sg, sq, g, pbev)
            self._attn_scan(sg, sq, g, pbev)
        s.es_att.close()

    def _attn_pphase(self, sg, sq, g, pbev):
        s = self
        ev = s.ev
        ALUc = ALU
        SV, ST, SS, SG = s.S["v"], s.S["t"], s.S["s"], s.S["g"]
        qxg = [s.ctr("q_xg0"), s.ctr("q_xg1")]
        qkt = [s.ctr(f"q_kt{i}") for i in range(4)]
        qfsi = [s.ctr("q_fsi0"), s.ctr("q_fsi1")]
        qfso = [s.ctr("q_fso0"), s.ctr("q_fso1")]
        qtt = [s.ctr(f"q_tt{i}") for i in range(16)]
        qgd = s.ctr("q_gd")
        p = sg % 2

        def xg_sy(eng, w, sg=sg, p=p):
            if sg >= 1:
                w(ev[f"pjmm_{sg-1}_2_7"])
                w(ev[f"smmm_{sg-1}"])
            else:
                w(ev[f"hgel_{8*32-1}"]); w(ev["eomm_7_1_1"])
                w(ev["eost_7_0"]); w(ev["eost_7_1"])
            for ct in range(GROUP):
                w(ev[f"xhst_{sg * GROUP + ct}"])
            for ct in range(GROUP):
                t = sg * GROUP + ct
                xr = s.xh_dram.ap()[t, :].rearrange("(p c) -> p c", p=128)
                for kc in range(8):
                    s.dma(eng, qxg[p], s.xh_g[:, kc * 512 + 128 * ct: kc * 512 + 128 * (ct + 1)],
                          xr[:, 128 * kc:128 * (kc + 1)])
            ev[f"xg_{sg}"] = qxg[p].evt()
        self.on("sy", xg_sy)

        if g == 0:
            def tz_v(eng, w, sg=sg):
                if sg >= 2:
                    w(ev[f"cvd_{(sg-1)*24 + 23}"])
                else:
                    w(ev[f"hgel_{8*32-1}"])
                inst = eng.memset(s.tails[:, :], 0.0)
                ev[f"tz_{sg}"] = s.step(SV, inst)
            self.on("v", tz_v)

        WS = [s.WqTs, s.WkTs, s.WvTs]
        CW = [s.convw["wq_c"], s.convw["wk_c"], s.convw["wv_c"]]
        DST = [s.q_fm, s.k_fm, s.v_fm]
        for wi_ in range(3):
            for mc in range(8):
                bidx = sg * 24 + wi_ * 8 + mc
                reg = 4 if bidx % 2 == 0 else 7
                rp = bidx % 2

                def pj_t(eng, w, sg=sg, wi_=wi_, mc=mc, reg=reg, p=p):
                    w(ev[f"xg_{sg}"]); w(ev["wat"])
                    if pbev[reg] is not None:
                        w(pbev[reg])
                    for kc in range(8):
                        inst = eng.matmul(s.pb[reg][0:128, 0:512],
                                          WS[wi_][:, kc * D + 128 * mc:kc * D + 128 * (mc + 1)],
                                          s.xh_g[:, kc * 512:(kc + 1) * 512],
                                          start=(kc == 0), stop=(kc == 7))
                    ev[f"pjmm_{sg}_{wi_}_{mc}"] = s.step(ST, inst)
                self.on("t", pj_t)

                def pj_s(eng, w, sg=sg, wi_=wi_, mc=mc, reg=reg, rp=rp, bidx=bidx):
                    w(ev[f"pjmm_{sg}_{wi_}_{mc}"])
                    if bidx >= 2:
                        w(ev[f"cvd_{bidx-2}"])
                    if sg % NGROUP == 0:
                        w(ev[f"tz_{sg}"])
                    eng.tensor_copy(out=s.praw[rp][:, 0:3],
                                    in_=s.tails[:, (wi_ * 8 + mc) * 3:(wi_ * 8 + mc) * 3 + 3])
                    inst = eng.tensor_copy(out=s.praw[rp][:, 3:515], in_=s.pb[reg][0:128, 0:512])
                    ev[f"pjev_{sg}_{wi_}_{mc}"] = s.step(SV, inst)
                self.on("v", pj_s)

                eng_n = "v"

                def pj_c(eng, w, sg=sg, wi_=wi_, mc=mc, rp=rp, bidx=bidx):
                    w(ev[f"pjev_{sg}_{wi_}_{mc}"])
                    cw = CW[wi_]
                    eng.tensor_scalar(out=s.cvt[rp][:, :], in0=s.praw[rp][:, 0:512],
                                      scalar1=cw[:, 4 * mc:4 * mc + 1], scalar2=None, op0=ALUc.mult)
                    for j in range(1, 4):
                        eng.scalar_tensor_tensor(out=s.cvt[rp][:, :], in0=s.praw[rp][:, j:512 + j],
                                                 scalar=cw[:, 4 * mc + j:4 * mc + j + 1],
                                                 in1=s.cvt[rp][:, :], op0=ALUc.mult, op1=ALUc.add)
                    ev[f"cv_{sg}_{wi_}_{mc}"] = s.step(s.S["v"],
                                                       eng.tensor_copy(out=s.cvt[rp][:, 0:1],
                                                                       in_=s.cvt[rp][:, 0:1]))
                self.on(eng_n, pj_c)

                def pj_tl(eng, w, sg=sg, wi_=wi_, mc=mc, rp=rp, bidx=bidx):
                    w(ev[f"pjev_{sg}_{wi_}_{mc}"])
                    inst = eng.tensor_copy(out=s.tails[:, (wi_ * 8 + mc) * 3:(wi_ * 8 + mc) * 3 + 3],
                                           in_=s.praw[rp][:, 512:515])
                    ev[f"tl_{sg}_{wi_}_{mc}"] = s.step(SV, inst)
                self.on("v", pj_tl)

                def pj_sil(eng, w, sg=sg, wi_=wi_, mc=mc, rp=rp, bidx=bidx):
                    w(ev[f"cv_{sg}_{wi_}_{mc}"])
                    if sg >= 1:
                        if wi_ == 2:
                            w(ev[f"emat_{sg-1}_{3}"]); w(ev[f"emat_{sg-1}_{2}"])
                        else:
                            w(ev[f"omm_{sg-1}_{15}"])
                    inst = eng.activation(out=DST[wi_][:, 512 * mc:512 * (mc + 1)],
                                          in_=s.cvt[rp][:, :], func=AF.Silu)
                    ev[f"cvd_{bidx}"] = s.step(SS, inst)
                self.on("s", pj_sil)
                pbev[reg] = ev[f"pjev_{sg}_{wi_}_{mc}"]

        def sm_t(eng, w, sg=sg, p=p):
            w(ev[f"xg_{sg}"]); w(ev["wsm"]); w(ev["consts"])
            if pbev[6] is not None:
                w(pbev[6])
            for kc in range(8):
                inst = eng.matmul(s.pb[6][0:12, 0:512], s.WsmTs[:, kc * 12:(kc + 1) * 12],
                                  s.xh_g[:, kc * 512:(kc + 1) * 512],
                                  start=(kc == 0), stop=(kc == 7))
            ev[f"smmm_{sg}"] = s.step(ST, inst)
        self.on("t", sm_t)

        def sm_s(eng, w, sg=sg):
            w(ev[f"smmm_{sg}"])
            if sg >= 1:
                w(ev[f"smcmm_{sg-1}_{3}"]); w(ev[f"gdec_{sg-1}"])
            inst = eng.activation(out=s.smrowb[:, :], in_=s.pb[6][0:12, 0:512], func=AF.Sigmoid,
                                  bias=s.mixbS[0:12, 0:1])
            ev[f"smsig_{sg}"] = s.step(SS, inst)
        self.on("s", sm_s)
        pbev[6] = ev[f"smsig_{sg}"]

        ev[f"smb_{sg}"] = ev[f"smsig_{sg}"]

        def sm_gd(eng, w, sg=sg, p=p):
            w(ev[f"smb_{sg}"])
            if sg >= 1:
                w(ev[f"emg_{sg-1}_{3}"])
            for h in range(4):
                s.dma(eng, s.ctr("q_gd"), s.gdec[0:1, 512 * h:512 * (h + 1)],
                      s.smrowb[4 + h:5 + h, 0:512])
            ev[f"gdec_{sg}"] = s.ctr("q_gd").evt()
        self.on("sy", sm_gd)

        for ct in range(GROUP):
            cp = ct % 2

            def kt_sc(eng, w, sg=sg, ct=ct):
                w(ev[f"cvd_{sg*24 + 15}"])
                w(ev[f"cvd_{sg*24 + 23}"])
                if sg >= 1:
                    w(ev[f"supd_{sg-1}_{GROUP-1}_3"])
                    w(ev[f"l2e_{sg-1}_{ct}"])
                    w(ev[f"bv_{sg-1}_{4*ct+3}"])
                    w(ev[f"smm_{sg-1}_{4*ct+3}"])
                    if ct >= 2:
                        w(ev[f"ema_{sg-1}_{1}"]); w(ev[f"ema_{sg-1}_{3}"])
                for b in range(8):
                    s.dma(eng, qkt[ct], s.k_tm[:, ct * 1024 + 128 * b: ct * 1024 + 128 * (b + 1)],
                          s.k_fm[:, 512 * b + 128 * ct: 512 * b + 128 * (ct + 1)], transpose=True)
                    s.dma(eng, qkt[ct], s.v_tm[:, ct * 1024 + 128 * b: ct * 1024 + 128 * (b + 1)],
                          s.v_fm[:, 512 * b + 128 * ct: 512 * b + 128 * (ct + 1)], transpose=True)
                ev[f"ktv_{sg}_{ct}"] = qkt[ct].evt()
            self.on("sy", kt_sc)

            def smc_t(eng, w, sg=sg, ct=ct):
                w(ev[f"smsig_{sg}"])
                if pbev[6] is not None:
                    w(pbev[6])
                inst = eng.matmul(s.pb[6][0:128, 0:12], s.smrowb[:, 128 * ct:128 * (ct + 1)],
                                  s.identb[0:12, 0:12], start=True, stop=True)
                ev[f"smcmm_{sg}_{ct}"] = s.step(ST, inst)
            self.on("t", smc_t)

            def smc_v(eng, w, sg=sg, ct=ct):
                w(ev[f"smcmm_{sg}_{ct}"])
                inst = eng.tensor_copy(out=s.smcol[:, 12 * ct:12 * (ct + 1)], in_=s.pb[6][0:128, 0:12])
                ev[f"smc_{sg}_{ct}"] = s.step(SV, inst)
            self.on("v", smc_v)
            pbev[6] = ev[f"smc_{sg}_{ct}"]

            def q2_s(eng, w, sg=sg, ct=ct):
                w(ev[f"cvd_{sg*24 + 7}"])
                tg_ = sg * GROUP + ct
                if tg_ >= 1:
                    w(ev[f"l2e_{(tg_-1)//GROUP}_{(tg_-1)%GROUP}"])
                inst = eng.activation(out=s.kq2[:, :],
                                      in_=s.q_fm.rearrange("p (b c) -> p b c", b=8)
                                      [:, :, 128 * ct:128 * (ct + 1)],
                                      func=AF.Square)
                ev[f"q2_{sg}_{ct}"] = s.step(SS, inst)
            self.on("s", q2_s)

            def l2_t(eng, w, sg=sg, ct=ct):
                w(ev[f"q2_{sg}_{ct}"])
                if pbev[5] is not None:
                    w(pbev[5])
                for h in range(4):
                    for bb in range(2):
                        inst = eng.matmul(s.pb[5][0:1, 128 * h:128 * (h + 1)], s.onesc[:, 0:1],
                                          s.kq2[:, 128 * (2 * h + bb):128 * (2 * h + bb + 1)],
                                          start=(h == 0 and bb == 0), stop=(h == 3 and bb == 1))
                ev[f"l2mm_{sg}_{ct}"] = s.step(ST, inst)
            self.on("t", l2_t)

            def l2_v(eng, w, sg=sg, ct=ct):
                w(ev[f"l2mm_{sg}_{ct}"])
                inst = eng.tensor_copy(out=s.l2r[0:1, :], in_=s.pb[5][0:1, 0:512])
                ev[f"l2r_{sg}_{ct}"] = s.step(SV, inst)
            self.on("v", l2_v)
            pbev[5] = ev[f"l2r_{sg}_{ct}"]

            def l2c_t(eng, w, sg=sg, ct=ct):
                w(ev[f"l2r_{sg}_{ct}"])
                for h in range(4):
                    inst = eng.matmul(s.pb[5][0:128, h:h + 1], s.l2r[0:1, 128 * h:128 * (h + 1)],
                                      s.ident[0:1, 0:1], start=(h == 0), stop=(h == 3))
                ev[f"l2cmm_{sg}_{ct}"] = s.step(ST, inst)
            self.on("t", l2c_t)

            def kl2_s(eng, w, sg=sg, ct=ct):
                w(ev[f"ktv_{sg}_{ct}"])
                for h in range(4):
                    inst = eng.activation(out=s.sqs[:, :],
                                          in_=s.k_tm[:, ct * 1024 + 256 * h:ct * 1024 + 256 * (h + 1)],
                                          func=AF.Square, accum_out=s.t_s[1][:, h:h + 1])
                ev[f"kl2_{sg}_{ct}"] = s.step(SS, inst)
            self.on("s", kl2_s)

            def col_v(eng, w, sg=sg, ct=ct):
                w(ev[f"l2cmm_{sg}_{ct}"]); w(ev[f"smc_{sg}_{ct}"])
                eng.tensor_scalar(out=s.t_s[0][:, 0:4], in0=s.pb[5][0:128, 0:4],
                                  scalar1=EPS_L2, scalar2=None, op0=ALUc.add)
                eng.drain()
                inst = eng.tensor_copy(out=s.t_s[2][:, 0:4], in_=s.t_s[0][:, 0:4])
                ev[f"l2e_{sg}_{ct}"] = s.step(SV, inst)
            self.on("v", col_v)
            pbev[5] = ev[f"l2e_{sg}_{ct}"]

            def col_s(eng, w, sg=sg, ct=ct):
                w(ev[f"l2e_{sg}_{ct}"]); w(ev[f"kl2_{sg}_{ct}"]); w(ev["ttinit"])
                eng.activation(out=s.t_s[3][:, 0:4], in_=s.t_s[0][:, 0:4], func=AF.Sqrt)
                inst = eng.activation(out=s.t_s[3][:, 4:8], in_=s.t_s[1][:, 0:4], func=AF.Sqrt,
                                      bias=s.t_zero[:, 120:121])
                ev[f"l2s_{sg}_{ct}"] = s.step(SS, inst)
            self.on("s", col_s)

            def col_v2(eng, w, sg=sg, ct=ct):
                w(ev[f"l2s_{sg}_{ct}"])
                co = s.colsc
                o = 16 * ct
                eng.reciprocal(out=co[:, o + 4:o + 8], in_=s.t_s[3][:, 0:4])
                eng.drain()
                eng.reciprocal(out=co[:, o + 0:o + 4], in_=s.t_s[3][:, 4:8])
                eng.drain()
                eng.tensor_tensor(out=co[:, o + 12:o + 16], in0=s.smcol[:, 12 * ct:12 * ct + 4],
                                  in1=co[:, o + 0:o + 4], op=ALUc.mult)
                eng.drain()
                eng.tensor_tensor(out=co[:, o + 8:o + 12], in0=co[:, o + 12:o + 16],
                                  in1=co[:, o + 0:o + 4], op=ALUc.mult)
                eng.drain()
                eng.tensor_scalar(out=co[:, o + 8:o + 12], in0=co[:, o + 8:o + 12],
                                  scalar1=-1.0, scalar2=None, op0=ALUc.mult)
                eng.drain()
                inst = eng.tensor_copy(out=co[:, o:o + 1], in_=co[:, o:o + 1])
                ev[f"colsdone_{sg}_{ct}"] = s.step(SV, inst)
            self.on("v", col_v2)

            for h in range(4):
                chi = ct * 4 + h

                def amm_t(eng, w, sg=sg, ct=ct, h=h, chi=chi):
                    w(ev[f"cvd_{sg*24 + 15}"])
                    if pbev[4] is not None:
                        w(pbev[4])
                    for bb in range(2):
                        sl_ = slice(512 * (2 * h + bb) + 128 * ct, 512 * (2 * h + bb) + 128 * (ct + 1))
                        inst = eng.matmul(s.pb[4][0:128, 0:128], s.k_fm[:, sl_], s.k_fm[:, sl_],
                                          start=(bb == 0), stop=(bb == 1))
                    ev[f"amm_{sg}_{chi}"] = s.step(ST, inst)
                self.on("t", amm_t)

                def aev_v(eng, w, sg=sg, ct=ct, h=h, chi=chi):
                    w(ev[f"amm_{sg}_{chi}"]); w(ev[f"colsdone_{sg}_{ct}"])
                    if chi > 0:
                        w(ev[f"mi_{sg}_{chi-1}"])
                    o = 16 * ct
                    eng.scalar_tensor_tensor(out=s.Aoff[:, 128 * chi:128 * (chi + 1)],
                                             in0=s.pb[4][0:128, 0:128],
                                             scalar=s.colsc[:, o + 8 + h:o + 9 + h],
                                             in1=s.mask_off[:, :], op0=ALUc.mult, op1=ALUc.mult)
                    eng.drain()
                    inst = eng.scalar_tensor_tensor(out=s.Abd[:, :],
                                                    in0=s.pb[4][0:128, 0:128],
                                                    scalar=s.colsc[:, o + 8 + h:o + 9 + h],
                                                    in1=s.mask_bdsl[:, :], op0=ALUc.mult, op1=ALUc.mult)
                    ev[f"aev_{sg}_{chi}"] = s.step(SV, inst)
                self.on("v", aev_v)
                pbev[4] = ev[f"aev_{sg}_{chi}"]

                def mi_sy(eng, w, sg=sg, chi=chi):
                    w(ev[f"aev_{sg}_{chi}"])
                    p2_ = sg % 2
                    if sg >= 1:
                        w(ev[f"mo_{sg-1}_{15}"]); w(ev[f"mo_{sg-1}_{14}"])
                    for b in range(4):
                        s.dma(eng, qfsi[p2_],
                              s.fs_t[4 * chi + b:4 * chi + b + 1, 0:1024],
                              s.Abd[32 * b:32 * (b + 1), 32 * b:32 * b + 32])
                    ev[f"mi_{sg}_{chi}"] = qfsi[p2_].evt()
                self.on("sy", mi_sy)

        def fs_v(eng, w, sg=sg):
            w(ev[f"mi_{sg}_{15}"])
            X = s.fs_t
            raw = eng._eng
            for j in range(0, 31):
                raw.drain()
                for i in range(j + 1, 32):
                    raw.scalar_tensor_tensor(out=X[0:64, 32 * i:32 * (i + 1)],
                                             in0=X[0:64, 32 * j:32 * (j + 1)],
                                             scalar=X[0:64, 32 * i + j:32 * i + j + 1],
                                             in1=X[0:64, 32 * i:32 * (i + 1)],
                                             op0=ALUc.mult, op1=ALUc.add)
            eng.drain()
            inst = eng.tensor_copy(out=X[0:64, 0:1], in_=X[0:64, 0:1])
            ev[f"fsv_{sg}"] = s.step(SV, inst)
        self.on("v", fs_v)

        for chi in range(16):
            tp = chi % 2

            def mo_g(eng, w, sg=sg, chi=chi, tp=tp):
                w(ev[f"fsv_{sg}"])
                p2_ = sg % 2
                if chi >= 2:
                    w(ev[f"wt_{sg}_{chi-2}"])
                elif sg >= 1:
                    w(ev[f"wt_{sg-1}_{14 + chi}"])
                else:
                    w(ev["ttinit"])
                for b in range(4):
                    s.dma(eng, qfso[p2_],
                          s.Tt[tp][32 * b:32 * (b + 1), 32 * b:32 * b + 32],
                          s.fs_t[4 * chi + b:4 * chi + b + 1, 0:1024])
                ev[f"mo_{sg}_{chi}"] = qfso[p2_].evt()
            self.on("g", mo_g)

            def tta_g(eng, w, sg=sg, chi=chi, tp=tp):
                w(ev[f"mo_{sg}_{chi}"])
                inst = eng.tensor_tensor(out=s.Tt[tp][:, :], in0=s.Tt[tp][:, :],
                                         in1=s.identb[:, :], op=ALUc.add)
                ev[f"ttpre_{sg}_{chi}"] = s.step(SG, inst)
            self.on("g", tta_g)

            def tt_sc(eng, w, sg=sg, chi=chi, tp=tp):
                w(ev[f"ttpre_{sg}_{chi}"])
                if sg >= 1:
                    w(ev[f"g0_{sg-1}_{chi}"])
                s.dma(eng, qtt[chi], s.Wt[:, 128 * chi:128 * (chi + 1)], s.Tt[tp][:, :],
                      transpose=True)
                ev[f"wt_{sg}_{chi}"] = qtt[chi].evt()
            self.on("sy", tt_sc)

    def _attn_scan(self, sg, sq, g, pbev):
        s = self
        ev = s.ev
        ALUc = ALU
        SV, ST, SS, SG = s.S["v"], s.S["t"], s.S["s"], s.S["g"]
        qoe = [s.ctr("q_oe0"), s.ctr("q_oe1")]
        qon = [s.ctr("q_on0"), s.ctr("q_on1")]
        qxr = [s.ctr("q_xr0"), s.ctr("q_xr1")]
        qos = [s.ctr("q_os0"), s.ctr("q_os1")]
        qge = [s.ctr("q_ge0"), s.ctr("q_ge1")]
        qgd = s.ctr("q_gd")

        for h in range(4):
            def ema_t(eng, w, sg=sg, h=h):
                w(ev[f"gdec_{sg}"])
                if pbev[6] is not None:
                    w(pbev[6])
                inst = eng.matmul(s.pb[6][0:128, 0:512], s.onesr[0:1, 0:128],
                                  s.gdec[0:1, 512 * h:512 * (h + 1)], start=True, stop=True)
                ev[f"emg_{sg}_{h}"] = s.step(ST, inst)
            self.on("t", ema_t)

            eng_n = "v"
            SE = SV
            pe = 0

            def ema_c(eng, w, sg=sg, h=h, pe=pe):
                w(ev[f"emg_{sg}_{h}"])
                inst = eng.tensor_copy(out=s.gbc[pe][:, :], in_=s.pb[6][0:128, 0:512])
                ev[f"gcp_{sg}_{h}"] = s.step(SV, inst)
            self.on("v", ema_c)
            pbev[6] = ev[f"gcp_{sg}_{h}"]

            def ema_e(eng, w, sg=sg, h=h, SE=SE, pe=pe):
                w(ev[f"gcp_{sg}_{h}"])
                w(ev[f"cvd_{sg*24 + 23}"])
                w(ev[f"ktv_{sg}_{2}"]); w(ev[f"ktv_{sg}_{3}"])
                eng.drain()
                for bb in range(2):
                    b = 2 * h + bb
                    vsl = s.v_fm[:, 512 * b:512 * (b + 1)]
                    eng.tensor_tensor(out=s.d1b[pe][:, :], in0=s.gbc[pe][:, :], in1=vsl,
                                      op=ALUc.mult)
                    eng.drain()
                    eng.tensor_tensor(out=s.d1b[pe][:, :], in0=vsl, in1=s.d1b[pe][:, :],
                                      op=ALUc.subtract)
                    eng.drain()
                    if g == 0:
                        eng.memset(s.ema_st[:, b:b + 1], 0.0)
                        eng.drain()
                    eng.tensor_tensor_scan(out=vsl, data0=s.gbc[pe][:, :], data1=s.d1b[pe][:, :],
                                           initial=s.ema_st[:, b:b + 1],
                                           op0=ALUc.mult, op1=ALUc.add)
                    eng.drain()
                    eng.tensor_copy(out=s.ema_st[:, b:b + 1], in_=vsl[:, 511:512])
                    eng.drain()
                inst = eng.tensor_copy(out=s.gbc[pe][:, 0:1], in_=s.gbc[pe][:, 0:1])
                ev[f"ema_{sg}_{h}"] = s.step(SE, inst)
            self.on(eng_n, ema_e)

        for ct in range(GROUP):
            first_chunk = (g == 0 and ct == 0)
            o = 16 * ct
            tglob = sg * GROUP + ct
            cp = ct % 2

            for h in range(4):
                chi = ct * 4 + h
                oreg = 5 if h < 2 else 6
                ooff = 256 * (h % 2)
                def jt_t(eng, w, sg=sg, ct=ct, h=h, chi=chi):
                    w(ev[f"wt_{sg}_{chi}"])
                    if pbev[4] is not None:
                        w(pbev[4])
                    inst = eng.matmul(s.pb[4][0:128, 0:128], s.Aoff[:, 128 * chi:128 * (chi + 1)],
                                      s.Wt[:, 128 * chi:128 * (chi + 1)], start=True, stop=True)
                    ev[f"jtmm_{sg}_{chi}"] = s.step(ST, inst)
                    ev[f"jt_{sg}_{chi}"] = ev[f"jtmm_{sg}_{chi}"]
                self.on("t", jt_t)
                def jt_v(eng, w, sg=sg, h=h, chi=chi):
                    w(ev[f"jtmm_{sg}_{chi}"])
                    inst = eng.tensor_copy(out=s.JT_sb[h][:, :], in_=s.pb[4][0:128, 0:128])
                    ev[f"jtev_{sg}_{chi}"] = s.step(SV, inst)
                self.on("v", jt_v)
                pbev[4] = ev[f"jtev_{sg}_{chi}"]

            for h in range(4):
                chi = ct * 4 + h
                oreg = 5 if h < 2 else 6
                ooff = 256 * (h % 2)
                def bv_v(eng, w, sg=sg, ct=ct, h=h, chi=chi, o=o):
                    w(ev[f"ktv_{sg}_{ct}"]); w(ev[f"colsdone_{sg}_{ct}"])
                    inst = eng.tensor_scalar(out=s.bv_sb[h][:, :],
                                             in0=s.v_tm[:, ct * 1024 + 256 * h:ct * 1024 + 256 * (h + 1)],
                                             scalar1=s.colsc[:, o + 12 + h:o + 13 + h],
                                             scalar2=None, op0=ALUc.mult)
                    ev[f"bv_{sg}_{chi}"] = s.step(SV, inst)
                self.on("v", bv_v)

            for h in range(4):
                chi = ct * 4 + h
                oreg = 5 if h < 2 else 6
                ooff = 256 * (h % 2)
                def g_t(eng, w, sg=sg, ct=ct, h=h, chi=chi, first_chunk=first_chunk):
                    if s.pbS[h] is not None:
                        w(s.pbS[h])
                    if not first_chunk:
                        w(ev[f"sev_{sg}_{ct-1}_{h}" if ct > 0 else f"sev_{sg-1}_{GROUP-1}_{h}"])
                        for bb in range(2):
                            sl_ = slice(512 * (2 * h + bb) + 128 * ct,
                                        512 * (2 * h + bb) + 128 * (ct + 1))
                            inst = eng.matmul(s.pb[h][0:128, 0:256], s.k_fm[:, sl_],
                                              s.S_sb[:, 512 * h + 256 * bb:512 * h + 256 * (bb + 1)],
                                              start=(bb == 0), stop=(bb == 1))
                    else:
                        inst = eng.matmul(s.pb[h][0:128, 0:256], s.zerob[0:1, 0:128],
                                          s.zerob[0:1, 0:256], start=True, stop=True)
                    ev[f"gmm_{sg}_{chi}"] = s.step(ST, inst)
                self.on("t", g_t)
                def g_v(eng, w, sg=sg, ct=ct, h=h, chi=chi, o=o):
                    w(ev[f"gmm_{sg}_{chi}"]); w(ev[f"bv_{sg}_{chi}"])
                    inst = eng.scalar_tensor_tensor(out=s.G_sb[h][:, :], in0=s.pb[h][0:128, 0:256],
                                                    scalar=s.colsc[:, o + 8 + h:o + 9 + h],
                                                    in1=s.bv_sb[h][:, :], op0=ALUc.mult, op1=ALUc.add)
                    ev[f"gev_{sg}_{chi}"] = s.step(SV, inst)
                self.on("v", g_v)
                s.pbS[h] = ev[f"gev_{sg}_{chi}"]

            for h in range(4):
                chi = ct * 4 + h
                oreg = 5 if h < 2 else 6
                ooff = 256 * (h % 2)
                def u_t0(eng, w, sg=sg, h=h, chi=chi):
                    w(ev[f"gev_{sg}_{chi}"])
                    inst = eng.matmul(s.pb[h][0:128, 0:256], s.Wt[:, 128 * chi:128 * (chi + 1)],
                                      s.G_sb[h][:, :], start=True, stop=True)
                    ev[f"g0mm_{sg}_{chi}"] = s.step(ST, inst)
                    ev[f"g0_{sg}_{chi}"] = ev[f"g0mm_{sg}_{chi}"]
                self.on("t", u_t0)
                def u_v0(eng, w, sg=sg, h=h, chi=chi):
                    w(ev[f"g0mm_{sg}_{chi}"])
                    inst = eng.tensor_copy(out=s.g0_sb[h][:, :], in_=s.pb[h][0:128, 0:256])
                    ev[f"g0ev_{sg}_{chi}"] = s.step(SV, inst)
                self.on("v", u_v0)
            for h in range(4):
                chi = ct * 4 + h
                oreg = 5 if h < 2 else 6
                ooff = 256 * (h % 2)
                def u_t1(eng, w, sg=sg, h=h, chi=chi):
                    w(ev[f"g0ev_{sg}_{chi}"]); w(ev[f"jtev_{sg}_{chi}"])
                    inst = eng.matmul(s.pb[h][0:128, 256:512], s.JT_sb[h][:, :], s.g0_sb[h][:, :],
                                      start=True, stop=True)
                    ev[f"h1mm_{sg}_{chi}"] = s.step(ST, inst)
                self.on("t", u_t1)
                def u_v1(eng, w, sg=sg, h=h, chi=chi):
                    w(ev[f"h1mm_{sg}_{chi}"])
                    inst = eng.tensor_tensor(out=s.h1_sb[h][:, :], in0=s.pb[h][0:128, 256:512],
                                             in1=s.g0_sb[h][:, :], op=ALUc.add)
                    ev[f"h1ev_{sg}_{chi}"] = s.step(SV, inst)
                self.on("v", u_v1)
            for h in range(4):
                chi = ct * 4 + h
                oreg = 5 if h < 2 else 6
                ooff = 256 * (h % 2)
                def u_t2(eng, w, sg=sg, h=h, chi=chi):
                    w(ev[f"h1ev_{sg}_{chi}"])
                    inst = eng.matmul(s.pb[h][0:128, 0:256], s.JT_sb[h][:, :], s.h1_sb[h][:, :],
                                      start=True, stop=True)
                    ev[f"h2mm_{sg}_{chi}"] = s.step(ST, inst)
                self.on("t", u_t2)
                def u_v2(eng, w, sg=sg, h=h, chi=chi):
                    w(ev[f"h2mm_{sg}_{chi}"])
                    inst = eng.tensor_tensor(out=s.h1_sb[h][:, :], in0=s.pb[h][0:128, 0:256],
                                             in1=s.g0_sb[h][:, :], op=ALUc.add)
                    ev[f"h2ev_{sg}_{chi}"] = s.step(SV, inst)
                self.on("v", u_v2)
            for h in range(4):
                chi = ct * 4 + h
                oreg = 5 if h < 2 else 6
                ooff = 256 * (h % 2)
                def u_t3(eng, w, sg=sg, h=h, chi=chi):
                    w(ev[f"h2ev_{sg}_{chi}"])
                    inst = eng.matmul(s.pb[h][0:128, 256:512], s.JT_sb[h][:, :], s.h1_sb[h][:, :],
                                      start=True, stop=True)
                    ev[f"u3mm_{sg}_{chi}"] = s.step(ST, inst)
                self.on("t", u_t3)
                def u_v3(eng, w, sg=sg, h=h, chi=chi):
                    w(ev[f"u3mm_{sg}_{chi}"])
                    inst = eng.tensor_tensor(out=s.u2_sb[h][:, :], in0=s.pb[h][0:128, 256:512],
                                             in1=s.g0_sb[h][:, :], op=ALUc.add)
                    ev[f"uev_{sg}_{chi}"] = s.step(SV, inst)
                self.on("v", u_v3)
                s.pbS[h] = ev[f"uev_{sg}_{chi}"]

            for h in range(4):
                chi = ct * 4 + h
                oreg = 5 if h < 2 else 6
                ooff = 256 * (h % 2)
                def at_t(eng, w, sg=sg, ct=ct, h=h, chi=chi):
                    w(ev[f"cvd_{sg*24 + 7}"]); w(ev[f"cvd_{sg*24 + 15}"])
                    if pbev[4] is not None:
                        w(pbev[4])
                    for bb in range(2):
                        sl_ = slice(512 * (2 * h + bb) + 128 * ct, 512 * (2 * h + bb) + 128 * (ct + 1))
                        inst = eng.matmul(s.pb[4][0:128, 0:128], s.k_fm[:, sl_], s.q_fm[:, sl_],
                                          start=(bb == 0), stop=(bb == 1))
                    ev[f"atmm_{sg}_{chi}"] = s.step(ST, inst)
                self.on("t", at_t)
                def at_v(eng, w, sg=sg, ct=ct, h=h, chi=chi, o=o):
                    w(ev[f"atmm_{sg}_{chi}"])
                    inst = eng.tensor_tensor(out=s.aT_sb[h][:, :], in0=s.pb[4][0:128, 0:128],
                                             in1=s.mask_ui[:, :], op=ALUc.mult)
                    ev[f"atev_{sg}_{chi}"] = s.step(SV, inst)
                self.on("v", at_v)
                pbev[4] = ev[f"atev_{sg}_{chi}"]

                oreg = 5 if h < 2 else 6
                ooff = 256 * (h % 2)
            for h in range(4):
                chi = ct * 4 + h
                oreg = 5 if h < 2 else 6
                ooff = 256 * (h % 2)
                def o_t(eng, w, sg=sg, ct=ct, h=h, chi=chi, first_chunk=first_chunk,
                        oreg=oreg, ooff=ooff):
                    w(ev[f"uev_{sg}_{chi}"]); w(ev[f"atev_{sg}_{chi}"])
                    if pbev[oreg] is not None:
                        w(pbev[oreg])
                    if not first_chunk:
                        for bb in range(2):
                            sl_ = slice(512 * (2 * h + bb) + 128 * ct,
                                        512 * (2 * h + bb) + 128 * (ct + 1))
                            eng.matmul(s.pb[oreg][0:128, ooff:ooff + 256], s.q_fm[:, sl_],
                                       s.S_sb[:, 512 * h + 256 * bb:512 * h + 256 * (bb + 1)],
                                       start=(bb == 0), stop=False)
                        inst = eng.matmul(s.pb[oreg][0:128, ooff:ooff + 256], s.aT_sb[h][:, :],
                                          s.u2_sb[h][:, :], start=False, stop=True)
                    else:
                        inst = eng.matmul(s.pb[oreg][0:128, ooff:ooff + 256], s.aT_sb[h][:, :],
                                          s.u2_sb[h][:, :], start=True, stop=True)
                    ev[f"omm_{sg}_{chi}"] = s.step(ST, inst)
                self.on("t", o_t)
                def o_v(eng, w, sg=sg, ct=ct, h=h, chi=chi, o=o, oreg=oreg, ooff=ooff):
                    w(ev[f"omm_{sg}_{chi}"])
                    tg_ = sg * GROUP + ct
                    if tg_ >= 1:
                        w(ev[f"onft_{(tg_-1)//GROUP}_{(tg_-1)%GROUP}"])
                    inst = eng.tensor_scalar(out=s.o_tm[:, 256 * h:256 * (h + 1)],
                                             in0=s.pb[oreg][0:128, ooff:ooff + 256],
                                             scalar1=s.colsc[:, o + 4 + h:o + 5 + h],
                                             scalar2=None, op0=ALUc.mult)
                    ev[f"oev_{sg}_{chi}"] = s.step(SV, inst)
                self.on("v", o_v)
                pbev[oreg] = ev[f"oev_{sg}_{chi}"]

            for h in range(4):
                chi = ct * 4 + h
                oreg = 5 if h < 2 else 6
                ooff = 256 * (h % 2)
                def s_t(eng, w, sg=sg, ct=ct, h=h, chi=chi, first_chunk=first_chunk):
                    w(ev[f"uev_{sg}_{chi}"]); w(ev[f"ktv_{sg}_{ct}"])
                    if sq == 0 and g == 0 and ct == 0:
                        w(ev[f"hgel_{8*32-1}"])
                        w(ev["bcev_7"])
                    if s.pbS[h] is not None:
                        w(s.pbS[h])
                    for mm_ in range(2):
                        inst = eng.matmul(
                            s.pb[h][0:128, 256 * mm_:256 * (mm_ + 1)],
                            s.k_tm[:, ct * 1024 + 256 * h + 128 * mm_:
                                   ct * 1024 + 256 * h + 128 * (mm_ + 1)],
                            s.u2_sb[h][:, :],
                            start=(mm_ == 0), stop=(mm_ == 1))
                    ev[f"smm_{sg}_{chi}"] = s.step(ST, inst)
                self.on("t", s_t)
                def s_va(eng, w, sg=sg, ct=ct, h=h, chi=chi, first_chunk=first_chunk):
                    w(ev[f"smm_{sg}_{chi}"])
                    if first_chunk:
                        inst = eng.tensor_copy(out=s.S_f32[:, 512 * h:512 * (h + 1)],
                                               in_=s.pb[h][0:128, 0:512])
                    else:
                        inst = eng.tensor_tensor(out=s.S_f32[:, 512 * h:512 * (h + 1)],
                                                 in0=s.S_f32[:, 512 * h:512 * (h + 1)],
                                                 in1=s.pb[h][0:128, 0:512], op=ALUc.add)
                    ev[f"sadd_{sg}_{chi}"] = s.step(SV, inst)
                self.on("v", s_va)
                s.pbS[h] = ev[f"sadd_{sg}_{chi}"]
                def s_sv(eng, w, sg=sg, ct=ct, h=h, chi=chi):
                    w(ev[f"sadd_{sg}_{chi}"])
                    inst = eng.copy(out=s.S_sb[:, 512 * h:512 * (h + 1)],
                                    in_=s.S_f32[:, 512 * h:512 * (h + 1)])
                    ev[f"sev_{sg}_{ct}_{h}"] = s.step(SS, inst)
                self.on("s", s_sv)

            if ct == GROUP - 1:
                ev[f"supd_{sg}_{GROUP-1}_3"] = ev[f"sev_{sg}_{ct}_{3}"]

            def ep_sc1(eng, w, sg=sg, ct=ct, cp=cp):
                for h in range(4):
                    w(ev[f"ema_{sg}_{h}"])
                w(ev[f"oev_{sg}_{ct*4+3}"])
                tg_ = sg * GROUP + ct
                if tg_ >= 1:
                    w(ev[f"cmb_{(tg_-1)//GROUP}_{(tg_-1)%GROUP}"])
                for b in range(8):
                    s.dma(eng, qoe[cp], s.ema_t[:, 128 * b:128 * (b + 1)],
                          s.v_fm[:, 512 * b + 128 * ct:512 * b + 128 * (ct + 1)], transpose=True)
                ev[f"emat_{sg}_{ct}"] = qoe[cp].evt()
            self.on("sy", ep_sc1)

            def ep_v1(eng, w, sg=sg, ct=ct, o=o, tglob=tglob):
                w(ev[f"emat_{sg}_{ct}"])
                w(ev[f"oev_{sg}_{ct*4+3}"])
                eng.tensor_tensor(out=s.ema_t[:, :], in0=s.ema_t[:, :], in1=s.o_tm[:, :],
                                  op=ALUc.subtract)
                eng.drain()
                for h in range(4):
                    inst = eng.scalar_tensor_tensor(out=s.o_tm[:, 256 * h:256 * (h + 1)],
                                                    in0=s.ema_t[:, 256 * h:256 * (h + 1)],
                                                    scalar=s.smcol[:, 12 * ct + 8 + h:12 * ct + 9 + h],
                                                    in1=s.o_tm[:, 256 * h:256 * (h + 1)],
                                                    op0=ALUc.mult, op1=ALUc.add)
                    eng.drain()
                ev[f"cmb_{sg}_{ct}"] = s.step(SV, inst)
            self.on("v", ep_v1)

            def ep_s1(eng, w, sg=sg, ct=ct):
                w(ev[f"cmb_{sg}_{ct}"])
                for h in range(4):
                    eng.activation(out=s.sqs[:, :], in_=s.o_tm[:, 256 * h:256 * (h + 1)],
                                   func=AF.Square, scale=1.0 / 16.0,
                                   accum_out=s.t_s[4][:, h:h + 1])
                inst = eng.activation(out=s.t_s[5][:, 4:8], in_=s.t_s[4][:, 0:4], func=AF.Sqrt,
                                      bias=s.t_zero[:, 121:122])
                ev[f"rmssq_{sg}_{ct}"] = s.step(SS, inst)
            self.on("s", ep_s1)

            def ep_v2(eng, w, sg=sg, ct=ct):
                w(ev[f"rmssq_{sg}_{ct}"])
                eng.reciprocal(out=s.t_s[5][:, 0:4], in_=s.t_s[5][:, 4:8])
                eng.drain()
                for h in range(4):
                    eng.tensor_scalar(out=s.o_tm[:, 256 * h:256 * (h + 1)],
                                      in0=s.o_tm[:, 256 * h:256 * (h + 1)],
                                      scalar1=s.t_s[5][:, h:h + 1], scalar2=None, op0=ALUc.mult)
                    eng.drain()
                inst = eng.tensor_copy(out=s.o_tm[:, 0:1], in_=s.o_tm[:, 0:1])
                ev[f"onrm_{sg}_{ct}"] = s.step(SV, inst)
            self.on("v", ep_v2)

            def ep_sc2(eng, w, sg=sg, ct=ct, cp=cp):
                w(ev[f"onrm_{sg}_{ct}"])
                tg_ = sg * GROUP + ct
                if tg_ >= 1:
                    w(ev[f"womm_{(tg_-1)//GROUP}_{(tg_-1)%GROUP}"])
                for b in range(8):
                    s.dma(eng, qon[cp], s.onf[:, 128 * b:128 * (b + 1)],
                          s.o_tm[:, 128 * b:128 * (b + 1)], transpose=True)
                ev[f"onft_{sg}_{ct}"] = qon[cp].evt()
            self.on("s", ep_sc2)

            def ep_xr(eng, w, sg=sg, ct=ct, cp=cp, tglob=tglob):
                if sg * GROUP + ct >= 2:
                    w(ev[f"ost_{sg*GROUP+ct-2}"])
                else:
                    w(ev["eall"])
                ev[f"xr_{tglob}"] = s.dma(eng, qxr[cp], s.t_x[cp][:, :],
                                          s.x_own[128 * tglob:128 * (tglob + 1), :])
            self.on("sy", ep_xr)

            def ep_ge(eng, w, sg=sg, ct=ct, cp=cp, tglob=tglob):
                w(ev["eall"]); w(ev["gxi"])
                if sg * GROUP + ct >= 2:
                    w(ev[f"epev_{sg*GROUP+ct-2}"])
                ev[f"ge_{tglob}"] = s.idma(eng, qge[cp],
                                           out=s.gE[cp][:, :], out_offset=None,
                                           in_=s.cc_e_out.ap(),
                                           in_offset=IOoA(ap=s.r_gidx[:, tglob:tglob + 1], axis=0))
            self.on("g", ep_ge)

            def ep_v3(eng, w, sg=sg, ct=ct, cp=cp, tglob=tglob):
                w(ev[f"xr_{tglob}"]); w(ev[f"ge_{tglob}"]); w(ev["stats"])
                if tglob >= 1:
                    w(ev[f"womm_{(tglob-1)//GROUP}_{(tglob-1)%GROUP}"])
                eng.tensor_scalar(out=s.t_s[2][:, 4:5], in0=s.r_rs[:, tglob:tglob + 1],
                                  scalar1=1.0, scalar2=None, op0=ALUc.add)
                eng.tensor_tensor(out=s.t_s[2][:, 5:6], in0=s.r_mu[:, tglob:tglob + 1],
                                  in1=s.r_rs[:, tglob:tglob + 1], op=ALUc.mult)
                eng.drain()
                eng.tensor_scalar(out=s.t_s[2][:, 5:6], in0=s.t_s[2][:, 5:6],
                                  scalar1=-1.0, scalar2=None, op0=ALUc.mult)
                eng.drain()
                eng.tensor_scalar(out=s.t_a[:, :], in0=s.t_x[cp][:, :],
                                  scalar1=s.t_s[2][:, 4:5], scalar2=s.t_s[2][:, 5:6],
                                  op0=ALUc.mult, op1=ALUc.add)
                eng.drain()
                inst = eng.tensor_scalar(out=s.gE[cp][:, :], in0=s.gE[cp][:, :],
                                         scalar1=s.r_wgt[:, tglob:tglob + 1], scalar2=None,
                                         op0=ALUc.mult)
                ev[f"res_{sg}_{ct}"] = s.step(SV, inst)
            self.on("v", ep_v3)

            def ep_wo(eng, w, sg=sg, ct=ct, cp=cp, tglob=tglob):
                w(ev[f"onft_{sg}_{ct}"]); w(ev["wofold"]); w(ev[f"res_{sg}_{ct}"])
                if pbev[7] is not None:
                    w(pbev[7])
                if pbev[4] is not None:
                    w(pbev[4])
                for nn_ in range(2):
                    psl = s.pb[7 if nn_ == 0 else 4]
                    for kc in range(8):
                        eng.matmul(psl[0:128, 0:512], s.onf[:, 128 * kc:128 * (kc + 1)],
                                   s.WoTs[:, kc * D + 512 * nn_:kc * D + 512 * (nn_ + 1)],
                                   start=(kc == 0), stop=False)
                    eng.matmul(psl[0:128, 0:512], s.ident[:, :],
                               s.t_a[:, 512 * nn_:512 * (nn_ + 1)], start=False, stop=False)
                    inst = eng.matmul(psl[0:128, 0:512], s.identb[:, :],
                                      s.gE[cp][:, 512 * nn_:512 * (nn_ + 1)], start=False, stop=True)
                ev[f"womm_{sg}_{ct}"] = s.step(ST, inst)
            self.on("t", ep_wo)

            def ep_ev(eng, w, sg=sg, ct=ct, cp=cp, tglob=tglob):
                w(ev[f"womm_{sg}_{ct}"])
                eng.copy(out=s.t_x[cp][:, 0:512], in_=s.pb[7][0:128, 0:512])
                inst = eng.copy(out=s.t_x[cp][:, 512:1024], in_=s.pb[4][0:128, 0:512])
                ev[f"epev_{tglob}"] = s.step(SS, inst)
            self.on("s", ep_ev)
            pbev[7] = ev[f"epev_{tglob}"]
            pbev[4] = ev[f"epev_{tglob}"]

            def ep_st(eng, w, sg=sg, ct=ct, cp=cp, tglob=tglob):
                w(ev[f"epev_{tglob}"])
                ev[f"ost_{tglob}"] = s.dma(eng, qos[cp], s.o_out[128 * tglob:128 * (tglob + 1), :],
                                           s.t_x[cp][:, :])
            self.on("sy", ep_st)

    def _emit_combine(self):
        pass

    def _emit_taps_and_finish(self):
        s = self
        ev = s.ev
        taps = []
        if "stats" in self.taps:
            taps += [("r_mu", s.r_mu, [128, NTILE], F32, "stats"),
                     ("r_var", s.r_var, [128, NTILE], F32, "stats"),
                     ("r_rs", s.r_rs, [128, NTILE], F32, "stats"),
                     ("r_rs2", s.r_rs2, [128, NTILE], F32, "stats")]
        if "snap" in self.taps and self.stage >= 1:
            taps += [("snap_ps", s.snap_ps, [128, 8], F32, f"selr_{self.tsnap}"),
                     ("snap_oh", s.snap_oh, [128, 8], BF16, f"selr_{self.tsnap}"),
                     ("snap_l8", s.snap_l8, [128, 8], F32, f"selr_{self.tsnap}")]
        if "route" in self.taps and self.stage >= 1:
            taps += [("r_top1", s.r_top1, [128, NTILE], F32, f"selr_{NTILE-1}"),
                     ("r_selr", s.r_selr, [128, NTILE], F32, f"selr_{NTILE-1}"),
                     ("r_wgt", s.r_wgt, [128, NTILE], F32,
                      f"p2v_{NTILE-1}" if self.stage >= 2 else f"wgt_{NTILE-1}"),
                     ("cntrow", s.cntrow, [1, 8 * (NTILE + 1)], F32, "cntincl")]
        if "pass2" in self.taps and self.stage >= 2:
            if self.sub & 2:
                taps += [("r_rank", s.r_rank, [128, NTILE], F32, f"p2v_{NTILE-1}"),
                         ("r_keep", s.r_keep, [128, NTILE], F32, f"p2v_{NTILE-1}")]
            taps += [("cnt8", s.cnt8, [8, 8], F32, "cnt8")]
            if self.sub & 8:
                taps += [("slotid", s.slotid, [128, 16], F32, "slotid")]

        if "attn" in self.taps and self.stage >= 4:
            s_ = self
            ataps = [
                ("q_fm", s_.q_fm, [128, 4096], BF16, "cvd_191"),
                ("k_fm", s_.k_fm, [128, 4096], BF16, "cvd_191"),
                ("v_fm", s_.v_fm, [128, 4096], BF16, "ema_7_3"),
                ("k_tm", s_.k_tm, [128, 4096], BF16, "ktv_7_3"),
                ("v_tm", s_.v_tm, [128, 4096], BF16, "ktv_7_3"),
                ("smrow", s_.smrowb, [12, 512], BF16, "smsig_7"),
                ("smcol", s_.smcol, [128, 48], F32, "smc_7_3"),
                ("colsc", s_.colsc, [128, 64], F32, "colsdone_7_3"),
                ("Aoff", s_.Aoff, [128, 2048], BF16, "aev_7_15"),
                ("Wt", s_.Wt, [128, 2048], BF16, "wt_7_15"),
                ("fs_t", s_.fs_t, [128, 1024], F32, "fsv_7"),
                ("S_f32", s_.S_f32, [128, 2048], F32, "sadd_7_15"),
                ("G_sb", s_.G_sb, [128, 256], BF16, "gev_7_15"),
                ("g0_sb", s_.g0_sb, [128, 256], BF16, "g0ev_7_15"),
                ("u2_sb", s_.u2_sb, [128, 256], BF16, "uev_7_15"),
                ("aT_sb", s_.aT_sb, [128, 128], BF16, "atev_7_15"),
                ("o_tm", s_.o_tm, [128, 1024], BF16, "onrm_7_3"),
                ("onf", s_.onf, [128, 1024], BF16, "onft_7_3"),
                ("ema_st", s_.ema_st, [128, 8], F32, "ema_7_3"),
                ("gdec", s_.gdec, [1, 2048], BF16, "gdec_7"),
                ("t_a", s_.t_a, [128, 1024], F32, "res_7_3"),
            ]
            taps += ataps
        if "ffn" in self.taps and self.stage >= 3:
            def tap_eout(eng, w):
                w(ev["eall"])
                out = self.dout("tap_eall", [NEXP * CAP, D], BF16)
                ct = s.ctr("q_tapff")
                for i in range(16):
                    s.dma(eng, ct, out[1024 * i:1024 * (i + 1), :],
                          s.cc_e_out.ap()[1024 * i:1024 * (i + 1), :])
            self.on("sy", tap_eout)

        def tap_sy(eng, w):
            ct = s.ctr("q_tap")
            for name, src, shape, dt, evname in taps:
                w(ev[evname])
                out = self.dout("tap_" + name, shape, dt)
                s.dma(eng, ct, out[:, :], src[:, :])
            if self.stage < 4:
                w(ev.get("gwsplit") or (s.S["v"], 0))
                for t in range(NTILE):
                    s.dma(eng, ct, s.o_out[128 * t:128 * (t + 1), 0:129], s.t_zero[:, :])
            eng.wait_ge(ct.sem, ct.n)
            for qn, ctr in list(s._ctrs.items()):
                if qn.startswith("q_") and ctr.n > 0:
                    eng.wait_ge(ctr.sem, ctr.n)
        self.on("sy", tap_sy)


def build_mk(taps=(), stage=99, tsnap=2, sub=15):
    mk = MK(taps=taps, stage=stage, tsnap=tsnap, sub=sub)
    nc = mk.build()
    return mk, nc


from concourse.bass_utils import run_bass_kernel_spmd as _run_spmd

_CACHE = {}


def kernel(**inputs):
    inputs = {k: np.asarray(v) for k, v in inputs.items()}
    if "nc" not in _CACHE:
        mk, nc = build_mk(taps=(), stage=4)
        _CACHE["nc"] = nc
    nc = _CACHE["nc"]
    in_maps = [host_inputs_for_core(c, inputs) for c in range(NCORES)]
    res = _run_spmd(nc, in_maps, core_ids=list(range(NCORES)))
    out = np.concatenate([np.asarray(res.results[c]["out"]) for c in range(NCORES)], 0)
    return out.reshape(B, L, D).astype(np.float32)
